# version 12
# baseline (speedup 1.0000x reference)
import numpy as np
import ml_dtypes
from contextlib import ExitStack

import concourse.bass as bass
import concourse.mybir as mybir
from concourse.bass_utils import run_bass_kernel_spmd

F32 = mybir.dt.float32
BF16 = mybir.dt.bfloat16
AF = mybir.ActivationFunctionType
ALU = mybir.AluOpType
AXX = mybir.AxisListType.X

TB = 8192
TQ = 2048
D = 1024
DC = 8
NP = 8
NTK = 64
NC4 = 4
NM = 16
EPS = 1e-5


class Sem:
    def __init__(self, ctx, nc, name):
        self.h = ctx.enter_context(nc.semaphore(name))
        self.n = 0

    def inc(self, inst, k=1):
        inst.then_inc(self.h, k)
        self.n += k
        return inst


class DStream:

    def __init__(self, ctx, nc, name, nslots):
        self.sems = [Sem(ctx, nc, f"{name}{j}") for j in range(nslots)]
        self.ns = nslots

    def issue(self, slot, inst):
        self.sems[slot % self.ns].inc(inst, 16)

    def mark(self, slot):
        s = self.sems[slot % self.ns]
        return (slot % self.ns, s.n)

    def wait(self, eng, mark):
        j, v = mark
        eng.wait_ge(self.sems[j].h, v)

    def drain(self, eng):
        for s in self.sems:
            eng.wait_ge(s.h, s.n)


def build_nc(phases=(0, 1, 2, 3), p1_np=NP, p1_attn=True, p1_parts="kvq",
             dbg_scratch=False):
    nc = bass.Bass("TRN2", target_bir_lowering=False,
                   detect_race_conditions=False)

    xfull_d = nc.dram_tensor("xfull", [TB, D], F32, kind="ExternalInput").ap()
    xmy_d = nc.dram_tensor("xmy", [TQ, D], F32, kind="ExternalInput").ap()
    wq_d = nc.dram_tensor("wq", [D, D], F32, kind="ExternalInput").ap()
    wk_d = nc.dram_tensor("wk", [D, D], F32, kind="ExternalInput").ap()
    wo_d = nc.dram_tensor("wo", [D, D], F32, kind="ExternalInput").ap()
    bo_d = nc.dram_tensor("bo", [D], F32, kind="ExternalInput").ap()
    wfc_d = nc.dram_tensor("wfc", [4 * D, D], F32, kind="ExternalInput").ap()
    wpr_d = nc.dram_tensor("wproj", [D, 4 * D], F32, kind="ExternalInput").ap()
    idm_d = nc.dram_tensor("ident", [128, 128], BF16, kind="ExternalInput").ap()
    out_d = nc.dram_tensor("out", [TQ, D], F32, kind="ExternalOutput").ap()

    skind = "ExternalOutput" if dbg_scratch else "Internal"
    lnx_hbm = nc.dram_tensor("lnx_scr", [128, DC * TB], BF16, kind=skind).ap()
    lnmy_hbm = nc.dram_tensor("lnmy_scr", [128, DC * TQ], BF16, kind=skind).ap()
    attn_hbm = nc.dram_tensor("attn_scr", [128, NP * TQ], BF16, kind=skind).ap()
    lsum_hbm = nc.dram_tensor("lsum_scr", [64, 512], F32, kind=skind).ap()
    rsum_hbm = nc.dram_tensor("rsum_scr", [64, 512], F32, kind=skind).ap()
    resid_hbm = nc.dram_tensor("resid_scr", [TQ, D], F32, kind=skind).ap()
    ln2_hbm = nc.dram_tensor("ln2_scr", [128, DC * TQ], BF16, kind=skind).ap()

    ctx = ExitStack()
    with ctx:
        idm = ctx.enter_context(nc.sbuf_tensor("sb_idm", [128, 128], BF16))

        S = {}
        for name in ("ph sq var std rstd norm tp tpe wcast "
                     "kproj kevac vt vte qproj qevac sc exp avd ave div "
                     "womm resid rs fcmm relu h prmm snn vch").split():
            S[name] = Sem(ctx, nc, "sem_" + name)
        DS = {}
        for name, nsl in (("dx", 16), ("dlns", 2), ("dlnw", 2), ("dwl", 1),
                          ("dao", 1), ("dwo", 2), ("dbo", 1), ("dlsum", 1),
                          ("dxmy", 2), ("dattn", 2), ("dbc", 1), ("drs", 1),
                          ("dspill", 4), ("dw3", 2), ("dres", 2), ("dout", 2),
                          ("dwp", 2)):
            DS[name] = DStream(ctx, nc, name, nsl)

        phase_k = [0]

        def vi(inst):
            S["vch"].inc(inst)
            return inst

        def vw(eng):
            eng.wait_ge(S["vch"].h, S["vch"].n)

        def phase(block_name, bodies):
            k = phase_k[0]
            with nc.Block(block_name) as blk:
                for engname in ("sync", "tensor", "scalar", "vector", "gpsimd"):
                    body, drain = bodies.get(engname, (None, None))

                    def run(eng, body=body, drain=drain, k=k):
                        if k > 0:
                            eng.wait_ge(S["ph"].h, 5 * k)
                        if body is not None:
                            body(eng)
                        for d in (drain or []):
                            if isinstance(d, Sem):
                                eng.wait_ge(d.h, d.n)
                            else:
                                d.drain(eng)
                        S["ph"].inc(eng.nop())

                    getattr(blk, engname)(run)
            phase_k[0] += 1

        p0 = ExitStack()
        NXS = 16
        xstage = [p0.enter_context(nc.sbuf_tensor(f"sb_xs{i}", [128, D], F32))
                  for i in range(NXS)]
        lntm = [p0.enter_context(nc.sbuf_tensor(f"sb_lntm{i}", [128, D], BF16))
                for i in range(2)]
        xsq = p0.enter_context(nc.sbuf_tensor("sb_xsq", [128, D], F32))
        sxb = p0.enter_context(nc.sbuf_tensor("sb_sx", [128, 8], F32))
        sxqb = p0.enter_context(nc.sbuf_tensor("sb_sxq", [128, 8], F32))
        meanb = p0.enter_context(nc.sbuf_tensor("sb_mean", [128, 8], F32))
        ex2b = p0.enter_context(nc.sbuf_tensor("sb_ex2", [128, 8], F32))
        m2b = p0.enter_context(nc.sbuf_tensor("sb_m2", [128, 8], F32))
        varb = p0.enter_context(nc.sbuf_tensor("sb_var", [128, 8], F32))
        stdb = p0.enter_context(nc.sbuf_tensor("sb_std", [128, 8], F32))
        rstdb = p0.enter_context(nc.sbuf_tensor("sb_rstd", [128, 8], F32))
        negmb = p0.enter_context(nc.sbuf_tensor("sb_negm", [128, 8], F32))
        lnstg = [p0.enter_context(nc.sbuf_tensor(f"sb_lnst{i}", [128, 1024], BF16))
                 for i in range(2)]
        psT0 = [p0.enter_context(nc.psum_tensor(f"ps_t0{i}", [128, 512], BF16))
                for i in range(2)]

        NT = 80
        NG = 20

        def src_rows(i):
            return xfull_d[i * 128:(i + 1) * 128, :] if i < 64 else \
                xmy_d[(i - 64) * 128:(i - 63) * 128, :]

        def ln_dst(i):
            if i < 64:
                return lnx_hbm[:].rearrange("p (k s) -> p k s", k=DC, s=TB)[
                    :, :, i * 128:(i + 1) * 128]
            im = i - 64
            return lnmy_hbm[:].rearrange("p (k s) -> p k s", k=DC, s=TQ)[
                :, :, im * 128:(im + 1) * 128]

        def lncol(g, j):
            return (g % 2) * 4 + j

        def xmark(i):
            return (i % NXS, 16 * (i // NXS + 1))

        def lnsmark(i):
            return (i % 2, 16 * (i // 2 + 1))

        def p0_sync(eng):
            DS["dwl"].issue(0, eng.dma_start(out=idm[:], in_=idm_d))
            for i in range(NT):
                if i % 4 == 3:
                    continue
                if i >= NXS:
                    eng.wait_ge(S["norm"].h, i - NXS + 1)
                DS["dx"].issue(i, eng.dma_start(out=xstage[i % NXS][:],
                                                in_=src_rows(i)))

        def p0_tensor(eng):
            DS["dwl"].wait(eng, (0, 16))
            for i in range(NT):
                eng.wait_ge(S["norm"].h, i + 1)
                for k2 in range(DC):
                    t = 8 * i + k2
                    grp, sub = t // 4, t % 4
                    if grp >= 2 and sub == 0:
                        eng.wait_ge(S["tpe"].h, grp - 1)
                    S["tp"].inc(eng.transpose(
                        psT0[grp % 2][:, sub * 128:(sub + 1) * 128],
                        lntm[i % 2][:, k2 * 128:(k2 + 1) * 128],
                        idm[:]))

        def p0_scalar(eng):
            DS["dx"].issue(3, eng.dma_start(out=xstage[3][:], in_=src_rows(3)))
            for g in range(NG + 3):
                i4 = 4 * g + 7
                if i4 < NT:
                    DS["dx"].issue(i4, eng.dma_start(
                        out=xstage[i4 % NXS][:], in_=src_rows(i4)))
                if 0 <= g - 3 < NG:
                    for j in range(4):
                        i = 4 * (g - 3) + j
                        eng.wait_ge(S["tpe"].h, 2 * (i + 1))
                        DS["dlns"].issue(i, eng.dma_start(
                            out=ln_dst(i),
                            in_=lnstg[i % 2][:].rearrange("p (k s) -> p k s",
                                                          k=DC, s=128)))
                if g < NG:
                    if g >= 2:
                        eng.wait_ge(S["var"].h, g - 1)
                    for j in range(4):
                        i = 4 * g + j
                        DS["dx"].wait(eng, xmark(i))
                        S["sq"].inc(eng.activation(
                            xsq[:], xstage[i % NXS][:], AF.Square,
                            accum_out=sxqb[:, lncol(g, j):lncol(g, j) + 1]))
                if 0 <= g - 1 < NG:
                    gg = g - 1
                    eng.wait_ge(S["var"].h, gg + 1)
                    c0 = (gg % 2) * 4
                    S["std"].inc(eng.activation(
                        stdb[:, c0:c0 + 4], varb[:, c0:c0 + 4], AF.Sqrt))
                if 0 <= g - 2 < NG:
                    gg = g - 2
                    eng.wait_ge(S["rstd"].h, gg + 1)
                    for j in range(4):
                        i = 4 * gg + j
                        if i >= 2:
                            eng.wait_ge(S["tp"].h, 8 * (i - 1))
                        c = lncol(gg, j)
                        S["norm"].inc(eng.activation(
                            lntm[i % 2][:], xstage[i % NXS][:], AF.Identity,
                            scale=rstdb[:, c:c + 1], bias=negmb[:, c:c + 1]))

        def p0_vector(eng):
            for g in range(NG + 3):
                if g < NG:
                    for j in range(4):
                        i = 4 * g + j
                        DS["dx"].wait(eng, xmark(i))
                        eng.reduce_sum(sxb[:, lncol(g, j):lncol(g, j) + 1],
                                       xstage[i % NXS][:], axis=AXX)
                if 0 <= g - 1 < NG:
                    gg = g - 1
                    eng.wait_ge(S["sq"].h, 4 * (gg + 1))
                    if gg >= 2:
                        eng.wait_ge(S["std"].h, gg - 1)
                    c0 = (gg % 2) * 4
                    vw(eng)
                    vi(eng.tensor_scalar_mul(meanb[:, c0:c0 + 4],
                                             sxb[:, c0:c0 + 4], 1.0 / D))
                    vi(eng.tensor_scalar_mul(ex2b[:, c0:c0 + 4],
                                             sxqb[:, c0:c0 + 4], 1.0 / D))
                    vw(eng)
                    vi(eng.tensor_mul(m2b[:, c0:c0 + 4], meanb[:, c0:c0 + 4],
                                      meanb[:, c0:c0 + 4]))
                    vw(eng)
                    vi(eng.tensor_sub(varb[:, c0:c0 + 4], ex2b[:, c0:c0 + 4],
                                      m2b[:, c0:c0 + 4]))
                    vw(eng)
                    S["var"].inc(eng.tensor_scalar_add(
                        varb[:, c0:c0 + 4], varb[:, c0:c0 + 4], EPS))
                if 0 <= g - 2 < NG:
                    gg = g - 2
                    eng.wait_ge(S["std"].h, gg + 1)
                    if gg >= 2:
                        eng.wait_ge(S["norm"].h, 4 * (gg - 1))
                    c0 = (gg % 2) * 4
                    vi(eng.reciprocal(rstdb[:, c0:c0 + 4], stdb[:, c0:c0 + 4]))
                    vw(eng)
                    S["rstd"].inc(eng.scalar_tensor_tensor(
                        negmb[:, c0:c0 + 4], meanb[:, c0:c0 + 4], -1.0,
                        rstdb[:, c0:c0 + 4], ALU.mult, ALU.mult))
                if 0 <= g - 3 < NG:
                    for j in range(4):
                        i = 4 * (g - 3) + j
                        for half in range(2):
                            grp = 2 * i + half
                            eng.wait_ge(S["tp"].h, 4 * (grp + 1))
                            if i >= 2 and half == 0:
                                DS["dlns"].wait(eng, lnsmark(i - 2))
                            S["tpe"].inc(eng.tensor_copy(
                                lnstg[i % 2][:, half * 512:(half + 1) * 512],
                                psT0[grp % 2][:]))


        if 0 not in phases:
            p0.close()
        else:
            phase("p0", {
                "sync": (p0_sync, [DS["dx"], DS["dwl"]]),
                "tensor": (p0_tensor, None),
                "scalar": (p0_scalar, [DS["dlns"]]),
                "vector": (p0_vector, None),
            })
            p0.close()

        p1 = ExitStack()
        wstage = p1.enter_context(nc.sbuf_tensor("sb_wst", [128, D], F32))
        wqk_bf = p1.enter_context(nc.sbuf_tensor("sb_wqk", [128, 2048], BF16))
        lnxwin = [p1.enter_context(nc.sbuf_tensor(f"sb_lnw{i}", [128, 4096], BF16))
                  for i in range(2)]
        kfm2 = [p1.enter_context(nc.sbuf_tensor(f"sb_kfm{i}", [128, TB], BF16))
                for i in range(2)]
        vtm2 = [p1.enter_context(nc.sbuf_tensor(f"sb_vtm{i}", [128, NTK * 130], BF16))
                for i in range(2)]
        qfm2 = [p1.enter_context(nc.sbuf_tensor(f"sb_qfm{i}", [128, TQ], BF16))
                for i in range(2)]
        pt = [p1.enter_context(nc.sbuf_tensor(f"sb_pt{i}", [128, 1024], BF16))
              for i in range(4)]
        h1st = [p1.enter_context(nc.sbuf_tensor(f"sb_h1st{i}", [64, 512], BF16))
                for i in range(2)]
        h2st = [p1.enter_context(nc.sbuf_tensor(f"sb_h2st{i}", [64, 512], BF16))
                for i in range(2)]
        lrst = p1.enter_context(nc.sbuf_tensor("sb_lrst", [128, 1024], F32))
        psS = [p1.enter_context(nc.psum_tensor(f"ps_s{i}", [128, 1024], F32))
               for i in range(2)]
        psAV = [p1.enter_context(nc.psum_tensor(f"ps_av{i}", [128, 512], F32))
                for i in range(2)]
        psP = [p1.enter_context(nc.psum_tensor(f"ps_p{i}", [128, 512], F32))
               for i in range(2)]
        psPb = [psP[i][:, 0:256].bitcast(BF16) for i in range(2)]

        vtm4 = [vtm2[i][:].rearrange("p (t j s) -> p t j s", t=NTK, j=2, s=65)
                for i in range(2)]
        b = {k: S[k].n for k in S}
        bao = DS["dao"].sems[0].n
        ms = {}

        def p1_sync(eng):
            for p in range(NP):
                if p > 0:
                    eng.wait_ge(S["wcast"].h, b["wcast"] + 2 * p)
                for k2 in range(DC):
                    DS["dwl"].issue(0, eng.dma_start(
                        out=wstage[:, k2 * 128:(k2 + 1) * 128],
                        in_=wk_d[k2 * 128:(k2 + 1) * 128, p * 128:(p + 1) * 128]))
                ms[("wk", p)] = DS["dwl"].mark(0)
                for t16 in range(16):
                    if t16 == 0 and p > 0:
                        eng.wait_ge(S["qproj"].h, b["qproj"] + 4 * p)
                    if 16 * p + t16 - 1 > 0:
                        eng.wait_ge(S["kproj"].h, b["kproj"] + 16 * p + t16 - 1)
                    for k2 in range(DC):
                        DS["dlnw"].issue(t16, eng.dma_start(
                            out=lnxwin[t16 % 2][:, k2 * 512:(k2 + 1) * 512],
                            in_=lnx_hbm[:, k2 * TB + t16 * 512:
                                        k2 * TB + (t16 + 1) * 512]))
                    ms[("kw", p, t16)] = DS["dlnw"].mark(t16)
                eng.wait_ge(S["wcast"].h, b["wcast"] + 2 * p + 1)
                for k2 in range(DC):
                    DS["dwl"].issue(0, eng.dma_start(
                        out=wstage[:, k2 * 128:(k2 + 1) * 128],
                        in_=wq_d[k2 * 128:(k2 + 1) * 128, p * 128:(p + 1) * 128]))
                ms[("wq", p)] = DS["dwl"].mark(0)
                for c in range(NC4):
                    if c == 0:
                        eng.wait_ge(S["kproj"].h, b["kproj"] + 16 * (p + 1))
                    if c >= 2:
                        eng.wait_ge(S["qproj"].h, b["qproj"] + 4 * p + c - 1)
                    for k2 in range(DC):
                        DS["dlnw"].issue(c, eng.dma_start(
                            out=lnxwin[c % 2][:, k2 * 512:(k2 + 1) * 512],
                            in_=lnmy_hbm[:, k2 * TQ + c * 512:
                                         k2 * TQ + (c + 1) * 512]))
                    ms[("qw", p, c)] = DS["dlnw"].mark(c)

        def prep_jobs(eng, q):
            def kjob(t16):
                def f():
                    if t16 == 0:
                        eng.wait_ge(S["wcast"].h, b["wcast"] + 2 * q + 1)
                    DS["dlnw"].wait(eng, ms[("kw", q, t16)])
                    if 16 * q + t16 - 1 > 0:
                        eng.wait_ge(S["kevac"].h, b["kevac"] + 16 * q + t16 - 1)
                    for k2 in range(DC):
                        mmi = eng.matmul(
                            psP[t16 % 2][:],
                            wqk_bf[:, k2 * 128:(k2 + 1) * 128],
                            lnxwin[t16 % 2][:, k2 * 512:(k2 + 1) * 512],
                            start=(k2 == 0), stop=(k2 == DC - 1))
                    S["kproj"].inc(mmi)
                return f

            def vjob(grp):
                def f():
                    eng.wait_ge(S["kevac"].h, b["kevac"] + 16 * (q + 1))
                    if grp >= 2:
                        eng.wait_ge(S["vte"].h, b["vte"] + 16 * q + grp - 1)
                    for sub in range(4):
                        t = 4 * grp + sub
                        S["vt"].inc(eng.transpose(
                            psPb[grp % 2][:, sub * 128:(sub + 1) * 128],
                            kfm2[q % 2][:, t * 128:(t + 1) * 128], idm[:]))
                return f

            def qjob(c):
                def f():
                    if c == 0:
                        eng.wait_ge(S["wcast"].h, b["wcast"] + 2 * q + 2)
                        eng.wait_ge(S["vte"].h, b["vte"] + 16 * (q + 1))
                    DS["dlnw"].wait(eng, ms[("qw", q, c)])
                    if c >= 2:
                        eng.wait_ge(S["qevac"].h, b["qevac"] + 4 * q + c - 1)
                    for k2 in range(DC):
                        mmi = eng.matmul(
                            psP[c % 2][:],
                            wqk_bf[:, 1024 + k2 * 128: 1024 + (k2 + 1) * 128],
                            lnxwin[c % 2][:, k2 * 512:(k2 + 1) * 512],
                            start=(k2 == 0), stop=(k2 == DC - 1))
                    S["qproj"].inc(mmi)
                return f

            jobs = {}
            for t16 in range(16):
                jobs[7 * t16 + 3] = kjob(t16)
            for grp in range(8):
                jobs[130 + 4 * grp] = vjob(grp)
            for grp in range(8, 16):
                jobs[196 + 4 * (grp - 8)] = vjob(grp)
            for c in range(NC4):
                jobs[229 + 7 * c] = qjob(c)
            return jobs

        def p1_tensor(eng):
            jobs0 = prep_jobs(eng, 0)
            for slot in sorted(jobs0):
                jobs0[slot]()
            for q in range(NP):
                jobs = prep_jobs(eng, q + 1) if q + 1 < NP else {}
                kfm = kfm2[q % 2]
                qfm = qfm2[q % 2]
                eng.wait_ge(S["vte"].h, b["vte"] + 16 * (q + 1))
                eng.wait_ge(S["qevac"].h, b["qevac"] + 4 * (q + 1))
                for c in range(NC4):
                    base = 256 * q + 64 * c

                    def scores(t, c=c, base=base):
                        if base + t - 1 > 0:
                            eng.wait_ge(S["exp"].h, b["exp"] + base + t - 1)
                        eng.matmul(psS[t % 2][:, 0:512],
                                   kfm[0:64, t * 128:(t + 1) * 128],
                                   qfm[0:64, c * 512:(c + 1) * 512],
                                   start=True, stop=True)
                        S["sc"].inc(eng.matmul(
                            psS[t % 2][:, 512:1024],
                            kfm[64:128, t * 128:(t + 1) * 128],
                            qfm[64:128, c * 512:(c + 1) * 512],
                            start=True, stop=True))

                    def av(t, c=c, base=base):
                        eng.wait_ge(S["exp"].h, b["exp"] + base + t + 1)
                        if t == 0 and (q > 0 or c > 0):
                            eng.wait_ge(S["ave"].h, b["ave"] + 4 * q + c)
                        eng.matmul(psAV[0][0:65, :],
                                   vtm4[q % 2][:, t, 0, 0:65],
                                   pt[t % 4][:, 0:512],
                                   start=(t == 0), stop=(t == NTK - 1))
                        S["avd"].inc(eng.matmul(
                            psAV[1][0:65, :],
                            vtm4[q % 2][:, t, 1, 0:65],
                            pt[t % 4][:, 512:1024],
                            start=(t == 0), stop=(t == NTK - 1)))

                    for t in range(NTK):
                        slot = 64 * c + t
                        scores(t)
                        if t >= 1:
                            av(t - 1)
                        if slot in jobs:
                            jobs.pop(slot)()
                    av(NTK - 1)
                for slot in sorted(jobs):
                    jobs.pop(slot)()

        def ao_store(eng, idx):
            col = idx * 512
            pp, cc = idx // 4, idx % 4
            row = pp * 8 + cc * 2
            DS["dao"].issue(0, eng.dma_start(
                out=attn_hbm[0:64, col:col + 512], in_=h1st[idx % 2][:]))
            DS["dao"].issue(0, eng.dma_start(
                out=attn_hbm[64:128, col:col + 512], in_=h2st[idx % 2][:]))
            DS["dao"].issue(0, eng.dma_start(
                out=lsum_hbm[row:row + 1, :], in_=lrst[64:65, 0:512]))
            DS["dao"].issue(0, eng.dma_start(
                out=lsum_hbm[row + 1:row + 2, :], in_=lrst[64:65, 512:1024]))

        def p1_scalar_full(eng):
            for p in range(NP):
                for c in range(NC4):
                    idx = 4 * p + c
                    if idx >= 1:
                        eng.wait_ge(S["ave"].h, b["ave"] + idx)
                        ao_store(eng, idx - 1)
                    for t in range(NTK):
                        base = 256 * p + 64 * c
                        eng.wait_ge(S["sc"].h, b["sc"] + base + t + 1)
                        if base + t - 3 > 0:
                            eng.wait_ge(S["avd"].h, b["avd"] + base + t - 3)
                        S["exp"].inc(eng.activation(
                            pt[t % 4][:], psS[t % 2][:], AF.Exp, scale=1.0 / 32.0))
            idx = 4 * NP - 1
            eng.wait_ge(S["ave"].h, b["ave"] + idx + 1)
            ao_store(eng, idx)

        def p1_vector_prep(eng, q, section):
            if section == 0:
                DS["dwl"].wait(eng, ms[("wk", q)])
                if q > 0:
                    eng.wait_ge(S["kproj"].h, b["kproj"] + 16 * q)
                S["wcast"].inc(eng.tensor_copy(wqk_bf[:, 0:1024], wstage[:]))
            elif section in (1, 2):
                for j in range(8 * (section - 1), 8 * section):
                    eng.wait_ge(S["kproj"].h, b["kproj"] + 16 * q + j + 1)
                    S["kevac"].inc(eng.tensor_copy(
                        kfm2[q % 2][:, j * 512:(j + 1) * 512], psP[j % 2][:]))
            elif section in (3, 4):
                if section == 4:
                    DS["dwl"].wait(eng, ms[("wq", q)])
                    eng.wait_ge(S["qproj"].h, b["qproj"] + 4 * q)
                    S["wcast"].inc(eng.tensor_copy(wqk_bf[:, 1024:2048],
                                                   wstage[:]))
                for grp in range(8 * (section - 3), 8 * (section - 2)):
                    eng.wait_ge(S["vt"].h, b["vt"] + 64 * q + 4 * (grp + 1))
                    if grp == 0:
                        if q >= 1:
                            eng.wait_ge(S["avd"].h, b["avd"] + 256 * (q - 1))
                        eng.memset(vtm4[q % 2][:, :, :, 64:65], 1.0)
                    S["vte"].inc(eng.tensor_copy(
                        vtm4[q % 2][:, 4 * grp:4 * grp + 4, :, 0:64],
                        psPb[grp % 2][:].rearrange("p (t j s) -> p t j s",
                                                   t=4, j=2, s=64)))
            elif section == 5:
                for c in range(2):
                    eng.wait_ge(S["qproj"].h, b["qproj"] + 4 * q + c + 1)
                    S["qevac"].inc(eng.tensor_copy(
                        qfm2[q % 2][:, c * 512:(c + 1) * 512], psP[c % 2][:]))
            elif section == 6:
                for c in range(2, NC4):
                    eng.wait_ge(S["qproj"].h, b["qproj"] + 4 * q + c + 1)
                    S["qevac"].inc(eng.tensor_copy(
                        qfm2[q % 2][:, c * 512:(c + 1) * 512], psP[c % 2][:]))

        def p1_vector(eng):
            for sec in range(7):
                p1_vector_prep(eng, 0, sec)
            for p in range(NP):
                for c in range(NC4):
                    if p + 1 < NP:
                        if c == 0:
                            p1_vector_prep(eng, p + 1, 0)
                            p1_vector_prep(eng, p + 1, 1)
                        elif c == 1:
                            p1_vector_prep(eng, p + 1, 2)
                        elif c == 2:
                            p1_vector_prep(eng, p + 1, 3)
                        else:
                            p1_vector_prep(eng, p + 1, 4)
                            p1_vector_prep(eng, p + 1, 5)
                    idx = 4 * p + c
                    eng.wait_ge(S["avd"].h, b["avd"] + 256 * p + 64 * (c + 1))
                    if idx >= 1:
                        DS["dao"].wait(eng, (0, bao + 64 * idx))
                    eng.tensor_copy(h1st[idx % 2][:], psAV[0][0:64, :])
                    eng.tensor_copy(h2st[idx % 2][:], psAV[1][0:64, :])
                    eng.tensor_copy(lrst[64:65, 0:512], psAV[0][64:65, :])
                    S["ave"].inc(eng.tensor_copy(
                        lrst[64:65, 512:1024], psAV[1][64:65, :]))
                if p + 1 < NP:
                    p1_vector_prep(eng, p + 1, 6)

        if 1 not in phases:
            p1.close()
        else:
            phase("p1", {
                "sync": (p1_sync, None),
                "tensor": (p1_tensor, None),
                "scalar": (p1_scalar_full, [DS["dao"]]),
                "vector": (p1_vector, None),
            })
            p1.close()

        p2 = ExitStack()
        wostage = [p2.enter_context(nc.sbuf_tensor(f"sb_wos{i}", [128, D], F32))
                   for i in range(2)]
        wo_bf = p2.enter_context(nc.sbuf_tensor("sb_wobf", [128, DC * D], BF16))
        bo_rep = p2.enter_context(nc.sbuf_tensor("sb_borep", [128, D], F32))
        lsum_sb = p2.enter_context(nc.sbuf_tensor("sb_lsum", [64, 512], F32))
        rsum_sb = p2.enter_context(nc.sbuf_tensor("sb_rsum", [64, 512], F32))
        R128 = [p2.enter_context(nc.sbuf_tensor(f"sb_r128_{p}", [128, 512], F32))
                for p in range(NP)]
        attnstg = [p2.enter_context(nc.sbuf_tensor(f"sb_astg{i}", [128, D], BF16))
                   for i in range(2)]
        attndiv = [p2.enter_context(nc.sbuf_tensor(f"sb_adiv{i}", [128, D], BF16))
                   for i in range(2)]
        xmystg = [p2.enter_context(nc.sbuf_tensor(f"sb_xmys{i}", [128, D], F32))
                  for i in range(2)]
        residb = [p2.enter_context(nc.sbuf_tensor(f"sb_res{i}", [128, D], F32))
                  for i in range(4)]
        ln2tm = [p2.enter_context(nc.sbuf_tensor(f"sb_ln2tm{i}", [128, D], BF16))
                 for i in range(2)]
        xsq2 = p2.enter_context(nc.sbuf_tensor("sb_xsq2", [128, D], F32))
        st2 = p2.enter_context(nc.sbuf_tensor("sb_st2", [128, 16], F32))
        psWo = [p2.enter_context(nc.psum_tensor(f"ps_wo{i}", [128, D], F32))
                for i in range(2)]
        ln2stg = [p2.enter_context(nc.sbuf_tensor(f"sb_l2st{i}", [128, 1024],
                                                   BF16)) for i in range(2)]
        psT2 = [p2.enter_context(nc.psum_tensor(f"ps_t2{i}", [128, 512], BF16))
                for i in range(2)]

        b = {k: S[k].n for k in S}
        bspill = DS["dspill"].sems[0].n
        ms = {}

        def p2_sync(eng):
            for k2 in range(DC):
                if k2 >= 2:
                    eng.wait_ge(S["wcast"].h, b["wcast"] + k2 - 1)
                DS["dwo"].issue(k2, eng.dma_start(
                    out=wostage[k2 % 2][:],
                    in_=wo_d[k2 * 128:(k2 + 1) * 128, :]))
                ms[("wo", k2)] = DS["dwo"].mark(k2)
            DS["dbo"].issue(0, eng.dma_start(
                out=bo_rep[:], in_=bo_d[None, :].broadcast_to([128, D])))
            ms["bo"] = DS["dbo"].mark(0)
            DS["dlsum"].issue(0, eng.dma_start(out=lsum_sb[:], in_=lsum_hbm))
            ms["lsum"] = DS["dlsum"].mark(0)
            for c in range(NC4):
                eng.wait_ge(S["rs"].h, b["rs"] + 1)
                DS["drs"].wait(eng, (0, 16))
                if c > 0:
                    eng.wait_ge(S["div"].h, b["div"] + 32 * c)
                for p in range(NP):
                    row = p * 8 + c * 2
                    DS["dbc"].issue(0, eng.dma_start(
                        out=R128[p][0:64, :],
                        in_=rsum_hbm[row:row + 1, :].broadcast_to([64, 512])))
                    DS["dbc"].issue(0, eng.dma_start(
                        out=R128[p][64:128, :],
                        in_=rsum_hbm[row + 1:row + 2, :].broadcast_to([64, 512])))
                ms[("bc", c)] = DS["dbc"].mark(0)
                for m in range(4 * c, 4 * c + 4):
                    if m >= 2:
                        eng.wait_ge(S["resid"].h, b["resid"] + m - 1)
                    DS["dxmy"].issue(m, eng.dma_start(
                        out=xmystg[m % 2][:], in_=xmy_d[m * 128:(m + 1) * 128, :]))
                    ms[("xmy", m)] = DS["dxmy"].mark(m)
                    if m >= 2:
                        eng.wait_ge(S["div"].h, b["div"] + 8 * (m - 1))
                    for p in range(NP):
                        DS["dattn"].issue(m, eng.dma_start(
                            out=attnstg[m % 2][:, p * 128:(p + 1) * 128],
                            in_=attn_hbm[:, p * TQ + m * 128:
                                         p * TQ + (m + 1) * 128]))
                    ms[("attn", m)] = DS["dattn"].mark(m)

        def p2_tensor(eng):
            eng.wait_ge(S["wcast"].h, b["wcast"] + DC)
            for m in range(NM):
                eng.wait_ge(S["div"].h, b["div"] + 8 * (m + 1))
                if m >= 2:
                    eng.wait_ge(S["resid"].h, b["resid"] + m - 1)
                for half in range(2):
                    for p in range(NP):
                        mmi = eng.matmul(
                            psWo[m % 2][:, half * 512:(half + 1) * 512],
                            attndiv[m % 2][:, p * 128:(p + 1) * 128],
                            wo_bf[:, p * D + half * 512:
                                  p * D + half * 512 + 512],
                            start=(p == 0), stop=(p == NP - 1))
                S["womm"].inc(mmi)
                if m >= 2:
                    mm = m - 2
                    eng.wait_ge(S["norm"].h, b["norm"] + mm + 1)
                    for k2 in range(DC):
                        grp, sub = (8 * mm + k2) // 4, k2 % 4
                        if grp >= 2 and sub == 0:
                            eng.wait_ge(S["tpe"].h, b["tpe"] + grp - 1)
                        S["tp"].inc(eng.transpose(
                            psT2[grp % 2][:, sub * 128:(sub + 1) * 128],
                            ln2tm[mm % 2][:, k2 * 128:(k2 + 1) * 128], idm[:]))
            for mm in (NM - 2, NM - 1):
                eng.wait_ge(S["norm"].h, b["norm"] + mm + 1)
                for k2 in range(DC):
                    grp, sub = (8 * mm + k2) // 4, k2 % 4
                    if grp >= 2 and sub == 0:
                        eng.wait_ge(S["tpe"].h, b["tpe"] + grp - 1)
                    S["tp"].inc(eng.transpose(
                        psT2[grp % 2][:, sub * 128:(sub + 1) * 128],
                        ln2tm[mm % 2][:, k2 * 128:(k2 + 1) * 128], idm[:]))

        def ln2_dst(mm):
            return ln2_hbm[:].rearrange("p (k s) -> p k s", k=DC, s=TQ)[
                :, :, mm * 128:(mm + 1) * 128]

        def p2_scalar(eng):
            eng.wait_ge(S["rs"].h, b["rs"] + 1)
            DS["drs"].issue(0, eng.dma_start(out=rsum_hbm, in_=rsum_sb[:]))
            for m in range(NM + 3):
                if 0 <= m - 3 < NM:
                    mm = m - 3
                    eng.wait_ge(S["tpe"].h, b["tpe"] + 2 * (mm + 1))
                    DS["dlns"].issue(mm, eng.dma_start(
                        out=ln2_dst(mm),
                        in_=ln2stg[mm % 2][:].rearrange("p (k s) -> p k s",
                                                        k=DC, s=128)))
                    ms[("l2s", mm)] = DS["dlns"].mark(mm)
                if m < NM:
                    eng.wait_ge(S["resid"].h, b["resid"] + m + 1)
                    DS["dspill"].issue(m, eng.dma_start(
                        out=resid_hbm[m * 128:(m + 1) * 128, :],
                        in_=residb[m % 4][:]))
                    S["sq"].inc(eng.activation(
                        xsq2[:], residb[m % 4][:], AF.Square,
                        accum_out=st2[:, 2 + m % 2: 3 + m % 2]))
                if 0 <= m - 1 < NM:
                    mm = m - 1
                    eng.wait_ge(S["var"].h, b["var"] + mm + 1)
                    S["std"].inc(eng.activation(
                        st2[:, 10 + mm % 2: 11 + mm % 2],
                        st2[:, 8 + mm % 2: 9 + mm % 2], AF.Sqrt))
                if 0 <= m - 2 < NM:
                    mm = m - 2
                    eng.wait_ge(S["rstd"].h, b["rstd"] + mm + 1)
                    if mm >= 2:
                        eng.wait_ge(S["tp"].h, b["tp"] + 8 * (mm - 1))
                    S["norm"].inc(eng.activation(
                        ln2tm[mm % 2][:], residb[mm % 4][:], AF.Identity,
                        scale=st2[:, 12 + mm % 2: 13 + mm % 2],
                        bias=st2[:, 6 + mm % 2: 7 + mm % 2]))

        def p2_vector(eng):
            for k2 in range(DC):
                DS["dwo"].wait(eng, ms[("wo", k2)])
                S["wcast"].inc(eng.tensor_copy(
                    wo_bf[:, k2 * D:(k2 + 1) * D], wostage[k2 % 2][:]))
            DS["dlsum"].wait(eng, ms["lsum"])
            S["rs"].inc(eng.reciprocal(rsum_sb[:], lsum_sb[:]), 1)
            for m in range(NM + 2):
                if m < NM:
                    c = m // 4
                    DS["dattn"].wait(eng, ms[("attn", m)])
                    DS["dbc"].wait(eng, ms[("bc", c)])
                    for p in range(NP):
                        S["div"].inc(eng.tensor_mul(
                            attndiv[m % 2][:, p * 128:(p + 1) * 128],
                            attnstg[m % 2][:, p * 128:(p + 1) * 128],
                            R128[p][:, (m % 4) * 128:(m % 4 + 1) * 128]))
                    eng.wait_ge(S["womm"].h, b["womm"] + m + 1)
                    DS["dxmy"].wait(eng, ms[("xmy", m)])
                    if m >= 4:
                        eng.wait_ge(S["norm"].h, b["norm"] + m - 3)
                        DS["dspill"].wait(eng, ((m - 4) % 4,
                                                bspill + 16 * ((m - 4) // 4 + 1)))
                    vi(eng.tensor_add(residb[m % 4][:], psWo[m % 2][:],
                                      xmystg[m % 2][:]))
                    DS["dbo"].wait(eng, ms["bo"])
                    vw(eng)
                    S["resid"].inc(eng.tensor_add(
                        residb[m % 4][:], residb[m % 4][:], bo_rep[:]))
                    eng.wait_ge(S["resid"].h, b["resid"] + m + 1)
                    vi(eng.reduce_sum(st2[:, m % 2:m % 2 + 1], residb[m % 4][:],
                                      axis=AXX))
                    eng.wait_ge(S["sq"].h, b["sq"] + m + 1)
                    vw(eng)
                    vi(eng.tensor_scalar_mul(st2[:, 4 + m % 2:5 + m % 2],
                                             st2[:, m % 2:m % 2 + 1], 1.0 / D))
                    vi(eng.tensor_scalar_mul(st2[:, 8 + m % 2:9 + m % 2],
                                             st2[:, 2 + m % 2:3 + m % 2], 1.0 / D))
                    vw(eng)
                    vi(eng.tensor_mul(st2[:, 14:15], st2[:, 4 + m % 2:5 + m % 2],
                                      st2[:, 4 + m % 2:5 + m % 2]))
                    vw(eng)
                    vi(eng.tensor_sub(st2[:, 8 + m % 2:9 + m % 2],
                                      st2[:, 8 + m % 2:9 + m % 2], st2[:, 14:15]))
                    vw(eng)
                    S["var"].inc(eng.tensor_scalar_add(
                        st2[:, 8 + m % 2:9 + m % 2], st2[:, 8 + m % 2:9 + m % 2],
                        EPS))
                if 0 <= m - 1 < NM:
                    mm = m - 1
                    eng.wait_ge(S["std"].h, b["std"] + mm + 1)
                    if mm >= 2:
                        eng.wait_ge(S["norm"].h, b["norm"] + mm - 1)
                    vi(eng.reciprocal(st2[:, 12 + mm % 2:13 + mm % 2],
                                      st2[:, 10 + mm % 2:11 + mm % 2]))
                    vw(eng)
                    S["rstd"].inc(eng.scalar_tensor_tensor(
                        st2[:, 6 + mm % 2:7 + mm % 2],
                        st2[:, 4 + mm % 2:5 + mm % 2], -1.0,
                        st2[:, 12 + mm % 2:13 + mm % 2], ALU.mult, ALU.mult))
                if 0 <= m - 2 < NM:
                    mm = m - 2
                    for half in range(2):
                        grp = 2 * mm + half
                        eng.wait_ge(S["tp"].h, b["tp"] + 4 * (grp + 1))
                        if mm >= 2 and half == 0:
                            DS["dlns"].wait(eng, ms[("l2s", mm - 2)])
                        S["tpe"].inc(eng.tensor_copy(
                            ln2stg[mm % 2][:, half * 512:(half + 1) * 512],
                            psT2[grp % 2][:]))

        if 2 not in phases:
            p2.close()
        else:
            phase("p2", {
                "sync": (p2_sync, None),
                "tensor": (p2_tensor, None),
                "scalar": (p2_scalar, [DS["dlns"], DS["dspill"], DS["drs"]]),
                "vector": (p2_vector, None),
            })
            p2.close()

        p3 = ExitStack()
        wfcT = p3.enter_context(nc.sbuf_tensor("sb_wfcT", [128, DC * 4096], BF16))
        wprT = p3.enter_context(nc.sbuf_tensor("sb_wprT", [128, 32 * D], BF16))
        hfm = p3.enter_context(nc.sbuf_tensor("sb_hfm", [128, 32 * 512], BF16))
        relutmp = [p3.enter_context(nc.sbuf_tensor(f"sb_rel{i}", [128, 512], BF16))
                   for i in range(2)]
        memb = p3.enter_context(nc.sbuf_tensor("sb_mem", [128, 1024], F32))
        spkb = p3.enter_context(nc.sbuf_tensor("sb_spk", [128, 1024], F32))
        resstg = [p3.enter_context(nc.sbuf_tensor(f"sb_rstg{i}", [128, D], F32))
                  for i in range(2)]
        outsb = [p3.enter_context(nc.sbuf_tensor(f"sb_outs{i}", [128, D], F32))
                 for i in range(2)]
        ln2win = [p3.enter_context(nc.sbuf_tensor(f"sb_l2w{i}", [128, DC * 512],
                                                  BF16)) for i in range(2)]
        membBF = memb[:].bitcast(BF16)
        spkbBF = spkb[:].bitcast(BF16)
        wcst_fc = [membBF[:, 0:1024], membBF[:, 1024:2048]]
        wcst_pr = [spkbBF[:, 0:1024], spkbBF[:, 1024:2048]]
        psT3 = [p3.enter_context(nc.psum_tensor(f"ps_t3{i}", [128, 512], BF16))
                for i in range(2)]
        psH = [p3.enter_context(nc.psum_tensor(f"ps_h{i}", [128, 512], F32))
               for i in range(2)]
        psY = [p3.enter_context(nc.psum_tensor(f"ps_y{i}", [128, D], F32))
               for i in range(2)]

        b = {k: S[k].n for k in S}
        ms = {}
        NG4 = 4

        def wpr_src(p):
            return wpr_d[(p // 4) * 128:(p // 4 + 1) * 128,
                         (p % 4) * 1024:(p % 4 + 1) * 1024]

        def wdst4(w, k2):
            if w < 32:
                return wfcT[:].rearrange("p (k s) -> p k s", k=DC, s=4096)[
                    :, k2:k2 + 4, w * 128:(w + 1) * 128]
            i, jc = (w - 32) // 4, (w - 32) % 4
            return wprT[:].rearrange("p (j s) -> p j s", j=32, s=D)[
                :, jc * 8 + k2: jc * 8 + k2 + 4, i * 128:(i + 1) * 128]

        def lnw_load(eng, g4):
            for k2 in range(DC):
                DS["dlnw"].issue(g4, eng.dma_start(
                    out=ln2win[g4 % 2][:, k2 * 512:(k2 + 1) * 512],
                    in_=ln2_hbm[:, k2 * TQ + g4 * 512:
                                k2 * TQ + (g4 + 1) * 512]))
            ms[("lnw", g4)] = DS["dlnw"].mark(g4)

        def p3_sync(eng):
            for g4 in range(2):
                lnw_load(eng, g4)
            for j in range(32):
                if j >= 2:
                    eng.wait_ge(S["wcast"].h, b["wcast"] + 2 * (j - 2) + 1)
                DS["dw3"].issue(j, eng.dma_start(
                    out=resstg[j % 2][:],
                    in_=wfc_d[j * 128:(j + 1) * 128, :]))
                ms[("w3", j)] = DS["dw3"].mark(j)
            eng.wait_ge(S["wcast"].h, b["wcast"] + 64)
            for g in range(NM):
                if g >= 2:
                    eng.wait_ge(S["snn"].h, b["snn"] + 2 * (g - 1))
                DS["dres"].issue(g, eng.dma_start(
                    out=resstg[g % 2][:],
                    in_=resid_hbm[g * 128:(g + 1) * 128, :]))
                ms[("res", g)] = DS["dres"].mark(g)
                if g >= 1:
                    eng.wait_ge(S["snn"].h, b["snn"] + 2 * g)
                    DS["dout"].issue(g - 1, eng.dma_start(
                        out=out_d[(g - 1) * 128:g * 128, :],
                        in_=outsb[(g - 1) % 2][:]))
                    ms[("out", g - 1)] = DS["dout"].mark(g - 1)
                if g % 4 == 2 and g // 4 + 2 < NG4:
                    gw = g // 4 + 2
                    eng.wait_ge(S["fcmm"].h, b["fcmm"] + 32 * (gw - 1))
                    lnw_load(eng, gw)
            eng.wait_ge(S["snn"].h, b["snn"] + 2 * NM)
            DS["dout"].issue(NM - 1, eng.dma_start(
                out=out_d[(NM - 1) * 128:NM * 128, :],
                in_=outsb[(NM - 1) % 2][:]))

        def trans8(eng, q):
            eng.wait_ge(S["wcast"].h, b["wcast"] + q + 1)
            src = wcst_fc[(q // 2) % 2] if q % 2 == 0 else wcst_pr[(q // 2) % 2]
            for k2 in range(DC):
                grp, sub = 2 * q + k2 // 4, k2 % 4
                if grp >= 2 and sub == 0:
                    eng.wait_ge(S["tpe"].h, b["tpe"] + grp - 1)
                S["tp"].inc(eng.transpose(
                    psT3[grp % 2][:, sub * 128:(sub + 1) * 128],
                    src[:, k2 * 128:(k2 + 1) * 128], idm[:]))

        def p3_tensor(eng):
            for g4 in range(NG4):
                for j in range(32):
                    jj = 32 * g4 + j
                    if g4 == 0:
                        trans8(eng, 2 * j)
                        eng.wait_ge(S["tpe"].h, b["tpe"] + 4 * j + 2)
                    if j == 0:
                        DS["dlnw"].wait(eng, ms[("lnw", g4)])
                    if jj >= 2:
                        eng.wait_ge(S["relu"].h, b["relu"] + jj - 1)
                    for k2 in range(DC):
                        mmi = eng.matmul(
                            psH[jj % 2][:],
                            wfcT[:, k2 * 4096 + j * 128: k2 * 4096 + (j + 1) * 128],
                            ln2win[g4 % 2][:, k2 * 512:(k2 + 1) * 512],
                            start=(k2 == 0), stop=(k2 == DC - 1))
                    S["fcmm"].inc(mmi)
                    if g4 == 0:
                        trans8(eng, 2 * j + 1)
                for sub in range(4):
                    g = 4 * g4 + sub
                    if g == 0:
                        eng.wait_ge(S["tpe"].h, b["tpe"] + 128)
                    if g >= 2:
                        eng.wait_ge(S["snn"].h, b["snn"] + 2 * (g - 1))
                    for j in range(32):
                        if sub == 0:
                            eng.wait_ge(S["h"].h, b["h"] + 32 * g4 + j + 1)
                        for half in range(2):
                            mmi = eng.matmul(
                                psY[sub % 2][:, half * 512:(half + 1) * 512],
                                hfm[:, j * 512 + sub * 128:
                                    j * 512 + sub * 128 + 128],
                                wprT[:, j * D + half * 512:
                                     j * D + half * 512 + 512],
                                start=(j == 0), stop=(j == 31))
                    S["prmm"].inc(mmi)

        def p3_scalar(eng):
            for p in range(2):
                DS["dwp"].issue(p, eng.dma_start(out=outsb[p % 2][:],
                                                 in_=wpr_src(p)))
                ms[("wp", p)] = DS["dwp"].mark(p)
            for g4 in range(NG4):
                for j in range(32):
                    jj = 32 * g4 + j
                    if g4 == 0 and j < 30:
                        p = j + 2
                        eng.wait_ge(S["wcast"].h, b["wcast"] + 2 * j + 2)
                        DS["dwp"].issue(p, eng.dma_start(out=outsb[p % 2][:],
                                                         in_=wpr_src(p)))
                        ms[("wp", p)] = DS["dwp"].mark(p)
                    def sq(s):
                        if s % 32 == 0 and s >= 32:
                            eng.wait_ge(S["prmm"].h, b["prmm"] + 4 * (s // 32))
                        S["h"].inc(eng.activation(
                            hfm[:, (s % 32) * 512:(s % 32 + 1) * 512],
                            relutmp[s % 2][:], AF.Square))

                    if j == 0 and jj > 0:
                        sq(jj - 1)
                    eng.wait_ge(S["fcmm"].h, b["fcmm"] + jj + 1)
                    S["relu"].inc(eng.activation(
                        relutmp[jj % 2][:], psH[jj % 2][:], AF.Relu))
                    if jj >= 1 and j != 0:
                        sq(jj - 1)
            sq(127)

        def p3_vector(eng):
            for j in range(33):
                if j >= 1:
                    for q in (2 * (j - 1), 2 * (j - 1) + 1):
                        w = q // 2 if q % 2 == 0 else 32 + q // 2
                        for halfg in range(2):
                            grp = 2 * q + halfg
                            eng.wait_ge(S["tp"].h, b["tp"] + 4 * (grp + 1))
                            S["tpe"].inc(eng.tensor_copy(
                                wdst4(w, halfg * 4),
                                psT3[grp % 2][:].rearrange("p (j s) -> p j s",
                                                           j=4, s=128)))
                if j < 32:
                    DS["dw3"].wait(eng, ms[("w3", j)])
                    if j >= 2:
                        eng.wait_ge(S["tp"].h, b["tp"] + 8 * (2 * (j - 2)) + 8)
                    S["wcast"].inc(eng.tensor_copy(wcst_fc[j % 2],
                                                   resstg[j % 2][:]))
                    DS["dwp"].wait(eng, ms[("wp", j)])
                    if j >= 2:
                        eng.wait_ge(S["tp"].h,
                                    b["tp"] + 8 * (2 * (j - 2) + 1) + 8)
                    S["wcast"].inc(eng.tensor_copy(wcst_pr[j % 2],
                                                   outsb[j % 2][:]))
            for g4 in range(NG4):
                for sub in range(4):
                    g = 4 * g4 + sub
                    eng.wait_ge(S["prmm"].h, b["prmm"] + g + 1)
                    DS["dres"].wait(eng, ms[("res", g)])
                    if g >= 2:
                        DS["dout"].wait(eng, ms[("out", g - 2)])
                    y2 = [psY[sub % 2][:, h * 512:(h + 1) * 512] for h in range(2)]
                    mem2 = [memb[:, 0:512], spkb[:, 0:512]]
                    spk2 = [memb[:, 512:1024], spkb[:, 512:1024]]
                    res2 = [resstg[g % 2][:, 0:512], resstg[g % 2][:, 512:1024]]
                    acc2 = [outsb[g % 2][:, 0:512], outsb[g % 2][:, 512:1024]]

                    def step(ops, inc=None):
                        for h in range(2):
                            inst = ops(h)
                            if inc is not None:
                                inc(inst)

                    step(lambda h: eng.tensor_copy(mem2[h], y2[h]))
                    step(lambda h: eng.tensor_single_scalar(spk2[h], mem2[h],
                                                            1.0, ALU.is_gt))
                    step(lambda h: eng.scalar_tensor_tensor(
                        acc2[h], spk2[h], 0.2, res2[h], ALU.mult, ALU.add))
                    for _s in range(4):
                        step(lambda h: eng.scalar_tensor_tensor(
                            mem2[h], mem2[h], 0.9, spk2[h],
                            ALU.mult, ALU.subtract))
                        step(lambda h: eng.tensor_add(mem2[h], mem2[h], y2[h]))
                        step(lambda h: eng.tensor_single_scalar(
                            spk2[h], mem2[h], 1.0, ALU.is_gt))
                        if _s < 3:
                            step(lambda h: eng.scalar_tensor_tensor(
                                acc2[h], spk2[h], 0.2, acc2[h],
                                ALU.mult, ALU.add))
                        else:
                            step(lambda h: eng.scalar_tensor_tensor(
                                acc2[h], spk2[h], 0.2, acc2[h],
                                ALU.mult, ALU.add),
                                inc=lambda inst: S["snn"].inc(inst))

        if 3 not in phases:
            p3.close()
        else:
            phase("p3", {
                "sync": (p3_sync, [DS["dw3"], DS["dres"], DS["dout"],
                         DS["dlnw"]]),
                "tensor": (p3_tensor, None),
                "scalar": (p3_scalar, [DS["dwp"]]),
                "vector": (p3_vector, None),
            })
            p3.close()

    return nc


_NC_CACHE = []


def kernel(L0_attn_x, L0_attn_Wk, L0_attn_Wq, L0_attn_Wo, L0_attn_bo,
           L0_ffn_W_fc, L0_ffn_W_proj):
    x = np.ascontiguousarray(np.asarray(L0_attn_x, dtype=np.float32))
    wk = np.ascontiguousarray(np.asarray(L0_attn_Wk, dtype=np.float32))
    wq = np.ascontiguousarray(np.asarray(L0_attn_Wq, dtype=np.float32))
    wo = np.ascontiguousarray(np.asarray(L0_attn_Wo, dtype=np.float32))
    bo = np.ascontiguousarray(np.asarray(L0_attn_bo, dtype=np.float32))
    wfc = np.ascontiguousarray(np.asarray(L0_ffn_W_fc, dtype=np.float32))
    wpr = np.ascontiguousarray(np.asarray(L0_ffn_W_proj, dtype=np.float32))
    ident = np.eye(128, dtype=ml_dtypes.bfloat16)

    if not _NC_CACHE:
        _NC_CACHE.append(build_nc())
    nc = _NC_CACHE[0]

    in_maps = []
    for c in range(8):
        bidx, qi = c // 4, c % 4
        in_maps.append({
            "xfull": x[bidx],
            "xmy": x[bidx, qi * TQ:(qi + 1) * TQ],
            "wq": wq, "wk": wk, "wo": wo, "bo": bo,
            "wfc": wfc, "wproj": wpr, "ident": ident,
        })
    res = run_bass_kernel_spmd(nc, in_maps, core_ids=list(range(8)))
    out = np.empty((2, TB, D), dtype=np.float32)
    for c in range(8):
        bidx, qi = c // 4, c % 4
        out[bidx, qi * TQ:(qi + 1) * TQ] = np.asarray(res.results[c]["out"])
    return out


# version 13
# speedup vs baseline: 1.0178x; 1.0178x over previous
import numpy as np
import ml_dtypes
from contextlib import ExitStack

import concourse.bass as bass
import concourse.mybir as mybir
from concourse.bass_utils import run_bass_kernel_spmd

F32 = mybir.dt.float32
BF16 = mybir.dt.bfloat16
AF = mybir.ActivationFunctionType
ALU = mybir.AluOpType
AXX = mybir.AxisListType.X

TB = 8192
TQ = 2048
D = 1024
DC = 8
NP = 8
NTK = 64
NC4 = 4
NM = 16
EPS = 1e-5


class Sem:
    def __init__(self, ctx, nc, name):
        self.h = ctx.enter_context(nc.semaphore(name))
        self.n = 0

    def inc(self, inst, k=1):
        inst.then_inc(self.h, k)
        self.n += k
        return inst


class DStream:

    def __init__(self, ctx, nc, name, nslots):
        self.sems = [Sem(ctx, nc, f"{name}{j}") for j in range(nslots)]
        self.ns = nslots

    def issue(self, slot, inst):
        self.sems[slot % self.ns].inc(inst, 16)

    def mark(self, slot):
        s = self.sems[slot % self.ns]
        return (slot % self.ns, s.n)

    def wait(self, eng, mark):
        j, v = mark
        eng.wait_ge(self.sems[j].h, v)

    def drain(self, eng):
        for s in self.sems:
            eng.wait_ge(s.h, s.n)


def build_nc(phases=(0, 1, 2, 3), p1_np=NP, p1_attn=True, p1_parts="kvq",
             dbg_scratch=False):
    nc = bass.Bass("TRN2", target_bir_lowering=False,
                   detect_race_conditions=False)

    xfull_d = nc.dram_tensor("xfull", [TB, D], F32, kind="ExternalInput").ap()
    xmy_d = nc.dram_tensor("xmy", [TQ, D], F32, kind="ExternalInput").ap()
    wq_d = nc.dram_tensor("wq", [D, D], F32, kind="ExternalInput").ap()
    wk_d = nc.dram_tensor("wk", [D, D], F32, kind="ExternalInput").ap()
    wo_d = nc.dram_tensor("wo", [D, D], F32, kind="ExternalInput").ap()
    bo_d = nc.dram_tensor("bo", [D], F32, kind="ExternalInput").ap()
    wfc_d = nc.dram_tensor("wfc", [4 * D, D], F32, kind="ExternalInput").ap()
    wpr_d = nc.dram_tensor("wproj", [D, 4 * D], F32, kind="ExternalInput").ap()
    idm_d = nc.dram_tensor("ident", [128, 128], BF16, kind="ExternalInput").ap()
    out_d = nc.dram_tensor("out", [TQ, D], F32, kind="ExternalOutput").ap()

    skind = "ExternalOutput" if dbg_scratch else "Internal"
    lnx_hbm = nc.dram_tensor("lnx_scr", [128, DC * TB], BF16, kind=skind).ap()
    lnmy_hbm = nc.dram_tensor("lnmy_scr", [128, DC * TQ], BF16, kind=skind).ap()
    attn_hbm = nc.dram_tensor("attn_scr", [128, NP * TQ], BF16, kind=skind).ap()
    lsum_hbm = nc.dram_tensor("lsum_scr", [64, 512], F32, kind=skind).ap()
    rsum_hbm = nc.dram_tensor("rsum_scr", [64, 512], F32, kind=skind).ap()
    resid_hbm = nc.dram_tensor("resid_scr", [TQ, D], F32, kind=skind).ap()
    ln2_hbm = nc.dram_tensor("ln2_scr", [128, DC * TQ], BF16, kind=skind).ap()

    ctx = ExitStack()
    with ctx:
        idm = ctx.enter_context(nc.sbuf_tensor("sb_idm", [128, 128], BF16))

        S = {}
        for name in ("ph sq var std rstd norm tp tpe wcast "
                     "kproj kevac vt vte qproj qevac sc exp avd ave div "
                     "womm resid rs fcmm relu h prmm snn vch").split():
            S[name] = Sem(ctx, nc, "sem_" + name)
        DS = {}
        for name, nsl in (("dx", 16), ("dlns", 2), ("dlnw", 2), ("dwl", 1),
                          ("dao", 1), ("dwo", 2), ("dbo", 1), ("dlsum", 1),
                          ("dxmy", 2), ("dattn", 2), ("dbc", 1), ("drs", 1),
                          ("dspill", 4), ("dw3", 2), ("dres", 2), ("dout", 2),
                          ("dwp", 2)):
            DS[name] = DStream(ctx, nc, name, nsl)

        phase_k = [0]

        def vi(inst):
            S["vch"].inc(inst)
            return inst

        def vw(eng):
            eng.wait_ge(S["vch"].h, S["vch"].n)

        def phase(block_name, bodies):
            k = phase_k[0]
            with nc.Block(block_name) as blk:
                for engname in ("sync", "tensor", "scalar", "vector", "gpsimd"):
                    body, drain = bodies.get(engname, (None, None))

                    def run(eng, body=body, drain=drain, k=k):
                        if k > 0:
                            eng.wait_ge(S["ph"].h, 5 * k)
                        if body is not None:
                            body(eng)
                        for d in (drain or []):
                            if isinstance(d, Sem):
                                eng.wait_ge(d.h, d.n)
                            else:
                                d.drain(eng)
                        S["ph"].inc(eng.nop())

                    getattr(blk, engname)(run)
            phase_k[0] += 1

        p0 = ExitStack()
        NXS = 16
        xstage = [p0.enter_context(nc.sbuf_tensor(f"sb_xs{i}", [128, D], F32))
                  for i in range(NXS)]
        lntm = [p0.enter_context(nc.sbuf_tensor(f"sb_lntm{i}", [128, D], BF16))
                for i in range(2)]
        xsq = p0.enter_context(nc.sbuf_tensor("sb_xsq", [128, D], F32))
        sxb = p0.enter_context(nc.sbuf_tensor("sb_sx", [128, 8], F32))
        sxqb = p0.enter_context(nc.sbuf_tensor("sb_sxq", [128, 8], F32))
        meanb = p0.enter_context(nc.sbuf_tensor("sb_mean", [128, 8], F32))
        ex2b = p0.enter_context(nc.sbuf_tensor("sb_ex2", [128, 8], F32))
        m2b = p0.enter_context(nc.sbuf_tensor("sb_m2", [128, 8], F32))
        varb = p0.enter_context(nc.sbuf_tensor("sb_var", [128, 8], F32))
        stdb = p0.enter_context(nc.sbuf_tensor("sb_std", [128, 8], F32))
        rstdb = p0.enter_context(nc.sbuf_tensor("sb_rstd", [128, 8], F32))
        negmb = p0.enter_context(nc.sbuf_tensor("sb_negm", [128, 8], F32))
        lnstg = [p0.enter_context(nc.sbuf_tensor(f"sb_lnst{i}", [128, 1024], BF16))
                 for i in range(2)]
        psT0 = [p0.enter_context(nc.psum_tensor(f"ps_t0{i}", [128, 512], BF16))
                for i in range(2)]

        NT = 80
        NG = 20

        def src_rows(i):
            return xfull_d[i * 128:(i + 1) * 128, :] if i < 64 else \
                xmy_d[(i - 64) * 128:(i - 63) * 128, :]

        def ln_dst(i):
            if i < 64:
                return lnx_hbm[:].rearrange("p (k s) -> p k s", k=DC, s=TB)[
                    :, :, i * 128:(i + 1) * 128]
            im = i - 64
            return lnmy_hbm[:].rearrange("p (k s) -> p k s", k=DC, s=TQ)[
                :, :, im * 128:(im + 1) * 128]

        def lncol(g, j):
            return (g % 2) * 4 + j

        def xmark(i):
            return (i % NXS, 16 * (i // NXS + 1))

        def lnsmark(i):
            return (i % 2, 16 * (i // 2 + 1))

        def p0_sync(eng):
            DS["dwl"].issue(0, eng.dma_start(out=idm[:], in_=idm_d))
            for i in range(NT):
                if i % 4 == 3:
                    continue
                if i >= NXS:
                    eng.wait_ge(S["norm"].h, i - NXS + 1)
                DS["dx"].issue(i, eng.dma_start(out=xstage[i % NXS][:],
                                                in_=src_rows(i)))

        def p0_tensor(eng):
            DS["dwl"].wait(eng, (0, 16))
            for i in range(NT):
                eng.wait_ge(S["norm"].h, i + 1)
                for k2 in range(DC):
                    t = 8 * i + k2
                    grp, sub = t // 4, t % 4
                    if grp >= 2 and sub == 0:
                        eng.wait_ge(S["tpe"].h, grp - 1)
                    S["tp"].inc(eng.transpose(
                        psT0[grp % 2][:, sub * 128:(sub + 1) * 128],
                        lntm[i % 2][:, k2 * 128:(k2 + 1) * 128],
                        idm[:]))

        def p0_scalar(eng):
            DS["dx"].issue(3, eng.dma_start(out=xstage[3][:], in_=src_rows(3)))
            for g in range(NG + 3):
                i4 = 4 * g + 7
                if i4 < NT:
                    DS["dx"].issue(i4, eng.dma_start(
                        out=xstage[i4 % NXS][:], in_=src_rows(i4)))
                if 0 <= g - 3 < NG:
                    for j in range(4):
                        i = 4 * (g - 3) + j
                        eng.wait_ge(S["tpe"].h, 2 * (i + 1))
                        DS["dlns"].issue(i, eng.dma_start(
                            out=ln_dst(i),
                            in_=lnstg[i % 2][:].rearrange("p (k s) -> p k s",
                                                          k=DC, s=128)))
                if g < NG:
                    if g >= 2:
                        eng.wait_ge(S["var"].h, g - 1)
                    for j in range(4):
                        i = 4 * g + j
                        DS["dx"].wait(eng, xmark(i))
                        S["sq"].inc(eng.activation(
                            xsq[:], xstage[i % NXS][:], AF.Square,
                            accum_out=sxqb[:, lncol(g, j):lncol(g, j) + 1]))
                if 0 <= g - 1 < NG:
                    gg = g - 1
                    eng.wait_ge(S["var"].h, gg + 1)
                    c0 = (gg % 2) * 4
                    S["std"].inc(eng.activation(
                        stdb[:, c0:c0 + 4], varb[:, c0:c0 + 4], AF.Sqrt))
                if 0 <= g - 2 < NG:
                    gg = g - 2
                    eng.wait_ge(S["rstd"].h, gg + 1)
                    for j in range(4):
                        i = 4 * gg + j
                        if i >= 2:
                            eng.wait_ge(S["tp"].h, 8 * (i - 1))
                        c = lncol(gg, j)
                        S["norm"].inc(eng.activation(
                            lntm[i % 2][:], xstage[i % NXS][:], AF.Identity,
                            scale=rstdb[:, c:c + 1], bias=negmb[:, c:c + 1]))

        def p0_vector(eng):
            for g in range(NG + 3):
                if g < NG:
                    for j in range(4):
                        i = 4 * g + j
                        DS["dx"].wait(eng, xmark(i))
                        eng.reduce_sum(sxb[:, lncol(g, j):lncol(g, j) + 1],
                                       xstage[i % NXS][:], axis=AXX)
                ga = g - 1 if 0 <= g - 1 < NG else None
                gb = g - 2 if 0 <= g - 2 < NG else None
                if ga is not None:
                    eng.wait_ge(S["sq"].h, 4 * (ga + 1))
                    if ga >= 2:
                        eng.wait_ge(S["std"].h, ga - 1)
                if gb is not None:
                    eng.wait_ge(S["std"].h, gb + 1)
                    if gb >= 2:
                        eng.wait_ge(S["norm"].h, 4 * (gb - 1))
                cb = (gb % 2) * 4 if gb is not None else 0
                if ga is not None:
                    ca = (ga % 2) * 4

                    def a_ex2():
                        eng.tensor_scalar_mul(ex2b[:, ca:ca + 4],
                                              sxqb[:, ca:ca + 4], 1.0 / D)
                    eng.tensor_scalar_mul(meanb[:, ca:ca + 4],
                                          sxb[:, ca:ca + 4], 1.0 / D)
                    a_ex2()
                    eng.tensor_mul(m2b[:, ca:ca + 4], meanb[:, ca:ca + 4],
                                   meanb[:, ca:ca + 4])
                    if gb is not None:
                        eng.reciprocal(rstdb[:, cb:cb + 4], stdb[:, cb:cb + 4])
                    else:
                        a_ex2()
                    eng.tensor_sub(varb[:, ca:ca + 4], ex2b[:, ca:ca + 4],
                                   m2b[:, ca:ca + 4])
                    if gb is not None:
                        S["rstd"].inc(eng.scalar_tensor_tensor(
                            negmb[:, cb:cb + 4], meanb[:, cb:cb + 4], -1.0,
                            rstdb[:, cb:cb + 4], ALU.mult, ALU.mult))
                    else:
                        a_ex2()
                    S["var"].inc(eng.tensor_scalar_add(
                        varb[:, ca:ca + 4], varb[:, ca:ca + 4], EPS))
                elif gb is not None:
                    eng.reciprocal(rstdb[:, cb:cb + 4], stdb[:, cb:cb + 4])
                    eng.reciprocal(rstdb[:, cb:cb + 4], stdb[:, cb:cb + 4])
                    S["rstd"].inc(eng.scalar_tensor_tensor(
                        negmb[:, cb:cb + 4], meanb[:, cb:cb + 4], -1.0,
                        rstdb[:, cb:cb + 4], ALU.mult, ALU.mult))
                if 0 <= g - 3 < NG:
                    for j in range(4):
                        i = 4 * (g - 3) + j
                        for half in range(2):
                            grp = 2 * i + half
                            eng.wait_ge(S["tp"].h, 4 * (grp + 1))
                            if i >= 2 and half == 0:
                                DS["dlns"].wait(eng, lnsmark(i - 2))
                            S["tpe"].inc(eng.tensor_copy(
                                lnstg[i % 2][:, half * 512:(half + 1) * 512],
                                psT0[grp % 2][:]))


        if 0 not in phases:
            p0.close()
        else:
            phase("p0", {
                "sync": (p0_sync, [DS["dx"], DS["dwl"]]),
                "tensor": (p0_tensor, None),
                "scalar": (p0_scalar, [DS["dlns"]]),
                "vector": (p0_vector, None),
            })
            p0.close()

        p1 = ExitStack()
        wstage = p1.enter_context(nc.sbuf_tensor("sb_wst", [128, D], F32))
        wqk_bf = p1.enter_context(nc.sbuf_tensor("sb_wqk", [128, 2048], BF16))
        lnxwin = [p1.enter_context(nc.sbuf_tensor(f"sb_lnw{i}", [128, 4096], BF16))
                  for i in range(2)]
        kfm2 = [p1.enter_context(nc.sbuf_tensor(f"sb_kfm{i}", [128, TB], BF16))
                for i in range(2)]
        vtm2 = [p1.enter_context(nc.sbuf_tensor(f"sb_vtm{i}", [128, NTK * 130], BF16))
                for i in range(2)]
        qfm2 = [p1.enter_context(nc.sbuf_tensor(f"sb_qfm{i}", [128, TQ], BF16))
                for i in range(2)]
        pt = [p1.enter_context(nc.sbuf_tensor(f"sb_pt{i}", [128, 1024], BF16))
              for i in range(4)]
        h1st = [p1.enter_context(nc.sbuf_tensor(f"sb_h1st{i}", [64, 512], BF16))
                for i in range(2)]
        h2st = [p1.enter_context(nc.sbuf_tensor(f"sb_h2st{i}", [64, 512], BF16))
                for i in range(2)]
        lrst = p1.enter_context(nc.sbuf_tensor("sb_lrst", [128, 1024], F32))
        psS = [p1.enter_context(nc.psum_tensor(f"ps_s{i}", [128, 1024], F32))
               for i in range(2)]
        psAV = [p1.enter_context(nc.psum_tensor(f"ps_av{i}", [128, 512], F32))
                for i in range(2)]
        psP = [p1.enter_context(nc.psum_tensor(f"ps_p{i}", [128, 512], F32))
               for i in range(2)]
        psPb = [psP[i][:, 0:256].bitcast(BF16) for i in range(2)]

        vtm4 = [vtm2[i][:].rearrange("p (t j s) -> p t j s", t=NTK, j=2, s=65)
                for i in range(2)]
        b = {k: S[k].n for k in S}
        bao = DS["dao"].sems[0].n
        ms = {}

        def p1_sync(eng):
            for p in range(NP):
                if p > 0:
                    eng.wait_ge(S["wcast"].h, b["wcast"] + 2 * p)
                for k2 in range(DC):
                    DS["dwl"].issue(0, eng.dma_start(
                        out=wstage[:, k2 * 128:(k2 + 1) * 128],
                        in_=wk_d[k2 * 128:(k2 + 1) * 128, p * 128:(p + 1) * 128]))
                ms[("wk", p)] = DS["dwl"].mark(0)
                for t16 in range(16):
                    if t16 == 0 and p > 0:
                        eng.wait_ge(S["qproj"].h, b["qproj"] + 4 * p)
                    if 16 * p + t16 - 1 > 0:
                        eng.wait_ge(S["kproj"].h, b["kproj"] + 16 * p + t16 - 1)
                    for k2 in range(DC):
                        DS["dlnw"].issue(t16, eng.dma_start(
                            out=lnxwin[t16 % 2][:, k2 * 512:(k2 + 1) * 512],
                            in_=lnx_hbm[:, k2 * TB + t16 * 512:
                                        k2 * TB + (t16 + 1) * 512]))
                    ms[("kw", p, t16)] = DS["dlnw"].mark(t16)
                eng.wait_ge(S["wcast"].h, b["wcast"] + 2 * p + 1)
                for k2 in range(DC):
                    DS["dwl"].issue(0, eng.dma_start(
                        out=wstage[:, k2 * 128:(k2 + 1) * 128],
                        in_=wq_d[k2 * 128:(k2 + 1) * 128, p * 128:(p + 1) * 128]))
                ms[("wq", p)] = DS["dwl"].mark(0)
                for c in range(NC4):
                    if c == 0:
                        eng.wait_ge(S["kproj"].h, b["kproj"] + 16 * (p + 1))
                    if c >= 2:
                        eng.wait_ge(S["qproj"].h, b["qproj"] + 4 * p + c - 1)
                    for k2 in range(DC):
                        DS["dlnw"].issue(c, eng.dma_start(
                            out=lnxwin[c % 2][:, k2 * 512:(k2 + 1) * 512],
                            in_=lnmy_hbm[:, k2 * TQ + c * 512:
                                         k2 * TQ + (c + 1) * 512]))
                    ms[("qw", p, c)] = DS["dlnw"].mark(c)

        def prep_jobs(eng, q):
            def kjob(t16):
                def f():
                    if t16 == 0:
                        eng.wait_ge(S["wcast"].h, b["wcast"] + 2 * q + 1)
                    DS["dlnw"].wait(eng, ms[("kw", q, t16)])
                    if 16 * q + t16 - 1 > 0:
                        eng.wait_ge(S["kevac"].h, b["kevac"] + 16 * q + t16 - 1)
                    for k2 in range(DC):
                        mmi = eng.matmul(
                            psP[t16 % 2][:],
                            wqk_bf[:, k2 * 128:(k2 + 1) * 128],
                            lnxwin[t16 % 2][:, k2 * 512:(k2 + 1) * 512],
                            start=(k2 == 0), stop=(k2 == DC - 1))
                    S["kproj"].inc(mmi)
                return f

            def vjob(grp):
                def f():
                    eng.wait_ge(S["kevac"].h, b["kevac"] + 16 * (q + 1))
                    if grp >= 2:
                        eng.wait_ge(S["vte"].h, b["vte"] + 16 * q + grp - 1)
                    for sub in range(4):
                        t = 4 * grp + sub
                        S["vt"].inc(eng.transpose(
                            psPb[grp % 2][:, sub * 128:(sub + 1) * 128],
                            kfm2[q % 2][:, t * 128:(t + 1) * 128], idm[:]))
                return f

            def qjob(c):
                def f():
                    if c == 0:
                        eng.wait_ge(S["wcast"].h, b["wcast"] + 2 * q + 2)
                        eng.wait_ge(S["vte"].h, b["vte"] + 16 * (q + 1))
                    DS["dlnw"].wait(eng, ms[("qw", q, c)])
                    if c >= 2:
                        eng.wait_ge(S["qevac"].h, b["qevac"] + 4 * q + c - 1)
                    for k2 in range(DC):
                        mmi = eng.matmul(
                            psP[c % 2][:],
                            wqk_bf[:, 1024 + k2 * 128: 1024 + (k2 + 1) * 128],
                            lnxwin[c % 2][:, k2 * 512:(k2 + 1) * 512],
                            start=(k2 == 0), stop=(k2 == DC - 1))
                    S["qproj"].inc(mmi)
                return f

            jobs = {}
            for t16 in range(16):
                jobs[7 * t16 + 3] = kjob(t16)
            for grp in range(8):
                jobs[130 + 4 * grp] = vjob(grp)
            for grp in range(8, 16):
                jobs[196 + 4 * (grp - 8)] = vjob(grp)
            for c in range(NC4):
                jobs[229 + 7 * c] = qjob(c)
            return jobs

        def p1_tensor(eng):
            jobs0 = prep_jobs(eng, 0)
            for slot in sorted(jobs0):
                jobs0[slot]()
            for q in range(NP):
                jobs = prep_jobs(eng, q + 1) if q + 1 < NP else {}
                kfm = kfm2[q % 2]
                qfm = qfm2[q % 2]
                eng.wait_ge(S["vte"].h, b["vte"] + 16 * (q + 1))
                eng.wait_ge(S["qevac"].h, b["qevac"] + 4 * (q + 1))
                for c in range(NC4):
                    base = 256 * q + 64 * c

                    def scores(t, c=c, base=base):
                        if base + t - 1 > 0:
                            eng.wait_ge(S["exp"].h, b["exp"] + base + t - 1)
                        eng.matmul(psS[t % 2][:, 0:512],
                                   kfm[0:64, t * 128:(t + 1) * 128],
                                   qfm[0:64, c * 512:(c + 1) * 512],
                                   start=True, stop=True)
                        S["sc"].inc(eng.matmul(
                            psS[t % 2][:, 512:1024],
                            kfm[64:128, t * 128:(t + 1) * 128],
                            qfm[64:128, c * 512:(c + 1) * 512],
                            start=True, stop=True))

                    def av(t, c=c, base=base):
                        eng.wait_ge(S["exp"].h, b["exp"] + base + t + 1)
                        if t == 0 and (q > 0 or c > 0):
                            eng.wait_ge(S["ave"].h, b["ave"] + 4 * q + c)
                        eng.matmul(psAV[0][0:65, :],
                                   vtm4[q % 2][:, t, 0, 0:65],
                                   pt[t % 4][:, 0:512],
                                   start=(t == 0), stop=(t == NTK - 1))
                        S["avd"].inc(eng.matmul(
                            psAV[1][0:65, :],
                            vtm4[q % 2][:, t, 1, 0:65],
                            pt[t % 4][:, 512:1024],
                            start=(t == 0), stop=(t == NTK - 1)))

                    for t in range(NTK):
                        slot = 64 * c + t
                        scores(t)
                        if t >= 1:
                            av(t - 1)
                        if slot in jobs:
                            jobs.pop(slot)()
                    av(NTK - 1)
                for slot in sorted(jobs):
                    jobs.pop(slot)()

        def ao_store(eng, idx):
            col = idx * 512
            pp, cc = idx // 4, idx % 4
            row = pp * 8 + cc * 2
            DS["dao"].issue(0, eng.dma_start(
                out=attn_hbm[0:64, col:col + 512], in_=h1st[idx % 2][:]))
            DS["dao"].issue(0, eng.dma_start(
                out=attn_hbm[64:128, col:col + 512], in_=h2st[idx % 2][:]))
            DS["dao"].issue(0, eng.dma_start(
                out=lsum_hbm[row:row + 1, :], in_=lrst[64:65, 0:512]))
            DS["dao"].issue(0, eng.dma_start(
                out=lsum_hbm[row + 1:row + 2, :], in_=lrst[64:65, 512:1024]))

        def p1_scalar_full(eng):
            for p in range(NP):
                for c in range(NC4):
                    idx = 4 * p + c
                    if idx >= 1:
                        eng.wait_ge(S["ave"].h, b["ave"] + idx)
                        ao_store(eng, idx - 1)
                    for t in range(NTK):
                        base = 256 * p + 64 * c
                        eng.wait_ge(S["sc"].h, b["sc"] + base + t + 1)
                        if base + t - 3 > 0:
                            eng.wait_ge(S["avd"].h, b["avd"] + base + t - 3)
                        S["exp"].inc(eng.activation(
                            pt[t % 4][:], psS[t % 2][:], AF.Exp, scale=1.0 / 32.0))
            idx = 4 * NP - 1
            eng.wait_ge(S["ave"].h, b["ave"] + idx + 1)
            ao_store(eng, idx)

        def p1_vector_prep(eng, q, section):
            if section == 0:
                DS["dwl"].wait(eng, ms[("wk", q)])
                if q > 0:
                    eng.wait_ge(S["kproj"].h, b["kproj"] + 16 * q)
                S["wcast"].inc(eng.tensor_copy(wqk_bf[:, 0:1024], wstage[:]))
            elif section in (1, 2):
                for j in range(8 * (section - 1), 8 * section):
                    eng.wait_ge(S["kproj"].h, b["kproj"] + 16 * q + j + 1)
                    S["kevac"].inc(eng.tensor_copy(
                        kfm2[q % 2][:, j * 512:(j + 1) * 512], psP[j % 2][:]))
            elif section in (3, 4):
                if section == 4:
                    DS["dwl"].wait(eng, ms[("wq", q)])
                    eng.wait_ge(S["qproj"].h, b["qproj"] + 4 * q)
                    S["wcast"].inc(eng.tensor_copy(wqk_bf[:, 1024:2048],
                                                   wstage[:]))
                for grp in range(8 * (section - 3), 8 * (section - 2)):
                    eng.wait_ge(S["vt"].h, b["vt"] + 64 * q + 4 * (grp + 1))
                    if grp == 0:
                        if q >= 1:
                            eng.wait_ge(S["avd"].h, b["avd"] + 256 * (q - 1))
                        eng.memset(vtm4[q % 2][:, :, :, 64:65], 1.0)
                    S["vte"].inc(eng.tensor_copy(
                        vtm4[q % 2][:, 4 * grp:4 * grp + 4, :, 0:64],
                        psPb[grp % 2][:].rearrange("p (t j s) -> p t j s",
                                                   t=4, j=2, s=64)))
            elif section == 5:
                for c in range(2):
                    eng.wait_ge(S["qproj"].h, b["qproj"] + 4 * q + c + 1)
                    S["qevac"].inc(eng.tensor_copy(
                        qfm2[q % 2][:, c * 512:(c + 1) * 512], psP[c % 2][:]))
            elif section == 6:
                for c in range(2, NC4):
                    eng.wait_ge(S["qproj"].h, b["qproj"] + 4 * q + c + 1)
                    S["qevac"].inc(eng.tensor_copy(
                        qfm2[q % 2][:, c * 512:(c + 1) * 512], psP[c % 2][:]))

        def p1_vector(eng):
            for sec in range(7):
                p1_vector_prep(eng, 0, sec)
            for p in range(NP):
                for c in range(NC4):
                    if p + 1 < NP:
                        if c == 0:
                            p1_vector_prep(eng, p + 1, 0)
                            p1_vector_prep(eng, p + 1, 1)
                        elif c == 1:
                            p1_vector_prep(eng, p + 1, 2)
                        elif c == 2:
                            p1_vector_prep(eng, p + 1, 3)
                        else:
                            p1_vector_prep(eng, p + 1, 4)
                            p1_vector_prep(eng, p + 1, 5)
                    idx = 4 * p + c
                    eng.wait_ge(S["avd"].h, b["avd"] + 256 * p + 64 * (c + 1))
                    if idx >= 1:
                        DS["dao"].wait(eng, (0, bao + 64 * idx))
                    eng.tensor_copy(h1st[idx % 2][:], psAV[0][0:64, :])
                    eng.tensor_copy(h2st[idx % 2][:], psAV[1][0:64, :])
                    eng.tensor_copy(lrst[64:65, 0:512], psAV[0][64:65, :])
                    S["ave"].inc(eng.tensor_copy(
                        lrst[64:65, 512:1024], psAV[1][64:65, :]))
                if p + 1 < NP:
                    p1_vector_prep(eng, p + 1, 6)

        if 1 not in phases:
            p1.close()
        else:
            phase("p1", {
                "sync": (p1_sync, None),
                "tensor": (p1_tensor, None),
                "scalar": (p1_scalar_full, [DS["dao"]]),
                "vector": (p1_vector, None),
            })
            p1.close()

        p2 = ExitStack()
        wostage = [p2.enter_context(nc.sbuf_tensor(f"sb_wos{i}", [128, D], F32))
                   for i in range(2)]
        wo_bf = p2.enter_context(nc.sbuf_tensor("sb_wobf", [128, DC * D], BF16))
        bo_rep = p2.enter_context(nc.sbuf_tensor("sb_borep", [128, D], F32))
        lsum_sb = p2.enter_context(nc.sbuf_tensor("sb_lsum", [64, 512], F32))
        rsum_sb = p2.enter_context(nc.sbuf_tensor("sb_rsum", [64, 512], F32))
        R128 = [p2.enter_context(nc.sbuf_tensor(f"sb_r128_{p}", [128, 512], F32))
                for p in range(NP)]
        attnstg = [p2.enter_context(nc.sbuf_tensor(f"sb_astg{i}", [128, D], BF16))
                   for i in range(2)]
        attndiv = [p2.enter_context(nc.sbuf_tensor(f"sb_adiv{i}", [128, D], BF16))
                   for i in range(2)]
        xmystg = [p2.enter_context(nc.sbuf_tensor(f"sb_xmys{i}", [128, D], F32))
                  for i in range(2)]
        residb = [p2.enter_context(nc.sbuf_tensor(f"sb_res{i}", [128, D], F32))
                  for i in range(4)]
        ln2tm = [p2.enter_context(nc.sbuf_tensor(f"sb_ln2tm{i}", [128, D], BF16))
                 for i in range(2)]
        xsq2 = p2.enter_context(nc.sbuf_tensor("sb_xsq2", [128, D], F32))
        st2 = p2.enter_context(nc.sbuf_tensor("sb_st2", [128, 16], F32))
        psWo = [p2.enter_context(nc.psum_tensor(f"ps_wo{i}", [128, D], F32))
                for i in range(2)]
        ln2stg = [p2.enter_context(nc.sbuf_tensor(f"sb_l2st{i}", [128, 1024],
                                                   BF16)) for i in range(2)]
        psT2 = [p2.enter_context(nc.psum_tensor(f"ps_t2{i}", [128, 512], BF16))
                for i in range(2)]

        b = {k: S[k].n for k in S}
        bspill = DS["dspill"].sems[0].n
        ms = {}

        def p2_sync(eng):
            for k2 in range(DC):
                if k2 >= 2:
                    eng.wait_ge(S["wcast"].h, b["wcast"] + k2 - 1)
                DS["dwo"].issue(k2, eng.dma_start(
                    out=wostage[k2 % 2][:],
                    in_=wo_d[k2 * 128:(k2 + 1) * 128, :]))
                ms[("wo", k2)] = DS["dwo"].mark(k2)
            DS["dbo"].issue(0, eng.dma_start(
                out=bo_rep[:], in_=bo_d[None, :].broadcast_to([128, D])))
            ms["bo"] = DS["dbo"].mark(0)
            DS["dlsum"].issue(0, eng.dma_start(out=lsum_sb[:], in_=lsum_hbm))
            ms["lsum"] = DS["dlsum"].mark(0)
            for c in range(NC4):
                eng.wait_ge(S["rs"].h, b["rs"] + 1)
                DS["drs"].wait(eng, (0, 16))
                if c > 0:
                    eng.wait_ge(S["div"].h, b["div"] + 32 * c)
                for p in range(NP):
                    row = p * 8 + c * 2
                    DS["dbc"].issue(0, eng.dma_start(
                        out=R128[p][0:64, :],
                        in_=rsum_hbm[row:row + 1, :].broadcast_to([64, 512])))
                    DS["dbc"].issue(0, eng.dma_start(
                        out=R128[p][64:128, :],
                        in_=rsum_hbm[row + 1:row + 2, :].broadcast_to([64, 512])))
                ms[("bc", c)] = DS["dbc"].mark(0)
                for m in range(4 * c, 4 * c + 4):
                    if m >= 2:
                        eng.wait_ge(S["resid"].h, b["resid"] + m - 1)
                    DS["dxmy"].issue(m, eng.dma_start(
                        out=xmystg[m % 2][:], in_=xmy_d[m * 128:(m + 1) * 128, :]))
                    ms[("xmy", m)] = DS["dxmy"].mark(m)
                    if m >= 2:
                        eng.wait_ge(S["div"].h, b["div"] + 8 * (m - 1))
                    for p in range(NP):
                        DS["dattn"].issue(m, eng.dma_start(
                            out=attnstg[m % 2][:, p * 128:(p + 1) * 128],
                            in_=attn_hbm[:, p * TQ + m * 128:
                                         p * TQ + (m + 1) * 128]))
                    ms[("attn", m)] = DS["dattn"].mark(m)

        def p2_tensor(eng):
            eng.wait_ge(S["wcast"].h, b["wcast"] + DC)
            for m in range(NM):
                eng.wait_ge(S["div"].h, b["div"] + 8 * (m + 1))
                if m >= 2:
                    eng.wait_ge(S["resid"].h, b["resid"] + m - 1)
                for half in range(2):
                    for p in range(NP):
                        mmi = eng.matmul(
                            psWo[m % 2][:, half * 512:(half + 1) * 512],
                            attndiv[m % 2][:, p * 128:(p + 1) * 128],
                            wo_bf[:, p * D + half * 512:
                                  p * D + half * 512 + 512],
                            start=(p == 0), stop=(p == NP - 1))
                S["womm"].inc(mmi)
                if m >= 2:
                    mm = m - 2
                    eng.wait_ge(S["norm"].h, b["norm"] + mm + 1)
                    for k2 in range(DC):
                        grp, sub = (8 * mm + k2) // 4, k2 % 4
                        if grp >= 2 and sub == 0:
                            eng.wait_ge(S["tpe"].h, b["tpe"] + grp - 1)
                        S["tp"].inc(eng.transpose(
                            psT2[grp % 2][:, sub * 128:(sub + 1) * 128],
                            ln2tm[mm % 2][:, k2 * 128:(k2 + 1) * 128], idm[:]))
            for mm in (NM - 2, NM - 1):
                eng.wait_ge(S["norm"].h, b["norm"] + mm + 1)
                for k2 in range(DC):
                    grp, sub = (8 * mm + k2) // 4, k2 % 4
                    if grp >= 2 and sub == 0:
                        eng.wait_ge(S["tpe"].h, b["tpe"] + grp - 1)
                    S["tp"].inc(eng.transpose(
                        psT2[grp % 2][:, sub * 128:(sub + 1) * 128],
                        ln2tm[mm % 2][:, k2 * 128:(k2 + 1) * 128], idm[:]))

        def ln2_dst(mm):
            return ln2_hbm[:].rearrange("p (k s) -> p k s", k=DC, s=TQ)[
                :, :, mm * 128:(mm + 1) * 128]

        def p2_scalar(eng):
            eng.wait_ge(S["rs"].h, b["rs"] + 1)
            DS["drs"].issue(0, eng.dma_start(out=rsum_hbm, in_=rsum_sb[:]))
            for m in range(NM + 3):
                if 0 <= m - 3 < NM:
                    mm = m - 3
                    eng.wait_ge(S["tpe"].h, b["tpe"] + 2 * (mm + 1))
                    DS["dlns"].issue(mm, eng.dma_start(
                        out=ln2_dst(mm),
                        in_=ln2stg[mm % 2][:].rearrange("p (k s) -> p k s",
                                                        k=DC, s=128)))
                    ms[("l2s", mm)] = DS["dlns"].mark(mm)
                if m < NM:
                    eng.wait_ge(S["resid"].h, b["resid"] + m + 1)
                    DS["dspill"].issue(m, eng.dma_start(
                        out=resid_hbm[m * 128:(m + 1) * 128, :],
                        in_=residb[m % 4][:]))
                    S["sq"].inc(eng.activation(
                        xsq2[:], residb[m % 4][:], AF.Square,
                        accum_out=st2[:, 2 + m % 2: 3 + m % 2]))
                if 0 <= m - 1 < NM:
                    mm = m - 1
                    eng.wait_ge(S["var"].h, b["var"] + mm + 1)
                    S["std"].inc(eng.activation(
                        st2[:, 10 + mm % 2: 11 + mm % 2],
                        st2[:, 8 + mm % 2: 9 + mm % 2], AF.Sqrt))
                if 0 <= m - 2 < NM:
                    mm = m - 2
                    eng.wait_ge(S["rstd"].h, b["rstd"] + mm + 1)
                    if mm >= 2:
                        eng.wait_ge(S["tp"].h, b["tp"] + 8 * (mm - 1))
                    S["norm"].inc(eng.activation(
                        ln2tm[mm % 2][:], residb[mm % 4][:], AF.Identity,
                        scale=st2[:, 12 + mm % 2: 13 + mm % 2],
                        bias=st2[:, 6 + mm % 2: 7 + mm % 2]))

        def p2_vector(eng):
            for k2 in range(DC):
                DS["dwo"].wait(eng, ms[("wo", k2)])
                S["wcast"].inc(eng.tensor_copy(
                    wo_bf[:, k2 * D:(k2 + 1) * D], wostage[k2 % 2][:]))
            DS["dlsum"].wait(eng, ms["lsum"])
            S["rs"].inc(eng.reciprocal(rsum_sb[:], lsum_sb[:]), 1)
            for m in range(NM + 2):
                if m < NM:
                    c = m // 4
                    DS["dattn"].wait(eng, ms[("attn", m)])
                    DS["dbc"].wait(eng, ms[("bc", c)])
                    for p in range(NP):
                        S["div"].inc(eng.tensor_mul(
                            attndiv[m % 2][:, p * 128:(p + 1) * 128],
                            attnstg[m % 2][:, p * 128:(p + 1) * 128],
                            R128[p][:, (m % 4) * 128:(m % 4 + 1) * 128]))
                    eng.wait_ge(S["womm"].h, b["womm"] + m + 1)
                    DS["dxmy"].wait(eng, ms[("xmy", m)])
                    DS["dbo"].wait(eng, ms["bo"])
                    if m >= 4:
                        eng.wait_ge(S["norm"].h, b["norm"] + m - 3)
                        DS["dspill"].wait(eng, ((m - 4) % 4,
                                                bspill + 16 * ((m - 4) // 4 + 1)))
                    rb, ps2m, xm = residb[m % 4], psWo[m % 2], xmystg[m % 2]
                    eng.tensor_add(rb[:, 0:512], ps2m[:, 0:512], xm[:, 0:512])
                    eng.tensor_add(rb[:, 512:1024], ps2m[:, 512:1024],
                                   xm[:, 512:1024])
                    eng.tensor_add(rb[:, 0:512], rb[:, 0:512],
                                   bo_rep[:, 0:512])
                    S["resid"].inc(eng.tensor_add(
                        rb[:, 512:1024], rb[:, 512:1024], bo_rep[:, 512:1024]))
                    mmB = m - 1 if 1 <= m <= NM else None
                    if mmB is not None:
                        eng.wait_ge(S["std"].h, b["std"] + mmB + 1)
                        if mmB >= 2:
                            eng.wait_ge(S["norm"].h, b["norm"] + mmB - 1)
                        eng.reciprocal(st2[:, 12 + mmB % 2:13 + mmB % 2],
                                       st2[:, 10 + mmB % 2:11 + mmB % 2])
                    eng.wait_ge(S["sq"].h, b["sq"] + m + 1)

                    def a_ex2():
                        eng.tensor_scalar_mul(st2[:, 8 + m % 2:9 + m % 2],
                                              st2[:, 2 + m % 2:3 + m % 2],
                                              1.0 / D)
                    if mmB is None:
                        a_ex2()
                    eng.reduce_sum(st2[:, m % 2:m % 2 + 1], rb[:], axis=AXX)
                    if mmB is None:
                        a_ex2()
                    else:
                        a_ex2()
                    eng.tensor_scalar_mul(st2[:, 4 + m % 2:5 + m % 2],
                                          st2[:, m % 2:m % 2 + 1], 1.0 / D)
                    if mmB is not None:
                        S["rstd"].inc(eng.scalar_tensor_tensor(
                            st2[:, 6 + mmB % 2:7 + mmB % 2],
                            st2[:, 4 + mmB % 2:5 + mmB % 2], -1.0,
                            st2[:, 12 + mmB % 2:13 + mmB % 2],
                            ALU.mult, ALU.mult))
                    else:
                        a_ex2()
                    eng.tensor_mul(st2[:, 14:15], st2[:, 4 + m % 2:5 + m % 2],
                                   st2[:, 4 + m % 2:5 + m % 2])
                    a_ex2()
                    eng.tensor_sub(st2[:, 8 + m % 2:9 + m % 2],
                                   st2[:, 8 + m % 2:9 + m % 2], st2[:, 14:15])
                    eng.tensor_mul(st2[:, 15:16], st2[:, 4 + m % 2:5 + m % 2],
                                   st2[:, 4 + m % 2:5 + m % 2])
                    S["var"].inc(eng.tensor_scalar_add(
                        st2[:, 8 + m % 2:9 + m % 2],
                        st2[:, 8 + m % 2:9 + m % 2], EPS))
                if m == NM:
                    mmB = m - 1
                    eng.wait_ge(S["std"].h, b["std"] + mmB + 1)
                    if mmB >= 2:
                        eng.wait_ge(S["norm"].h, b["norm"] + mmB - 1)
                    eng.reciprocal(st2[:, 12 + mmB % 2:13 + mmB % 2],
                                   st2[:, 10 + mmB % 2:11 + mmB % 2])
                    eng.reciprocal(st2[:, 12 + mmB % 2:13 + mmB % 2],
                                   st2[:, 10 + mmB % 2:11 + mmB % 2])
                    S["rstd"].inc(eng.scalar_tensor_tensor(
                        st2[:, 6 + mmB % 2:7 + mmB % 2],
                        st2[:, 4 + mmB % 2:5 + mmB % 2], -1.0,
                        st2[:, 12 + mmB % 2:13 + mmB % 2],
                        ALU.mult, ALU.mult))
                if 0 <= m - 2 < NM:
                    mm = m - 2
                    for half in range(2):
                        grp = 2 * mm + half
                        eng.wait_ge(S["tp"].h, b["tp"] + 4 * (grp + 1))
                        if mm >= 2 and half == 0:
                            DS["dlns"].wait(eng, ms[("l2s", mm - 2)])
                        S["tpe"].inc(eng.tensor_copy(
                            ln2stg[mm % 2][:, half * 512:(half + 1) * 512],
                            psT2[grp % 2][:]))

        if 2 not in phases:
            p2.close()
        else:
            phase("p2", {
                "sync": (p2_sync, None),
                "tensor": (p2_tensor, None),
                "scalar": (p2_scalar, [DS["dlns"], DS["dspill"], DS["drs"]]),
                "vector": (p2_vector, None),
            })
            p2.close()

        p3 = ExitStack()
        wfcT = p3.enter_context(nc.sbuf_tensor("sb_wfcT", [128, DC * 4096], BF16))
        wprT = p3.enter_context(nc.sbuf_tensor("sb_wprT", [128, 32 * D], BF16))
        hfm = p3.enter_context(nc.sbuf_tensor("sb_hfm", [128, 32 * 512], BF16))
        relutmp = [p3.enter_context(nc.sbuf_tensor(f"sb_rel{i}", [128, 512], BF16))
                   for i in range(2)]
        memb = p3.enter_context(nc.sbuf_tensor("sb_mem", [128, 1024], F32))
        spkb = p3.enter_context(nc.sbuf_tensor("sb_spk", [128, 1024], F32))
        resstg = [p3.enter_context(nc.sbuf_tensor(f"sb_rstg{i}", [128, D], F32))
                  for i in range(2)]
        outsb = [p3.enter_context(nc.sbuf_tensor(f"sb_outs{i}", [128, D], F32))
                 for i in range(2)]
        ln2win = [p3.enter_context(nc.sbuf_tensor(f"sb_l2w{i}", [128, DC * 512],
                                                  BF16)) for i in range(2)]
        membBF = memb[:].bitcast(BF16)
        spkbBF = spkb[:].bitcast(BF16)
        wcst_fc = [membBF[:, 0:1024], membBF[:, 1024:2048]]
        wcst_pr = [spkbBF[:, 0:1024], spkbBF[:, 1024:2048]]
        psT3 = [p3.enter_context(nc.psum_tensor(f"ps_t3{i}", [128, 512], BF16))
                for i in range(2)]
        psH = [p3.enter_context(nc.psum_tensor(f"ps_h{i}", [128, 512], F32))
               for i in range(2)]
        psY = [p3.enter_context(nc.psum_tensor(f"ps_y{i}", [128, D], F32))
               for i in range(2)]

        b = {k: S[k].n for k in S}
        ms = {}
        NG4 = 4

        def wpr_src(p):
            return wpr_d[(p // 4) * 128:(p // 4 + 1) * 128,
                         (p % 4) * 1024:(p % 4 + 1) * 1024]

        def wdst4(w, k2):
            if w < 32:
                return wfcT[:].rearrange("p (k s) -> p k s", k=DC, s=4096)[
                    :, k2:k2 + 4, w * 128:(w + 1) * 128]
            i, jc = (w - 32) // 4, (w - 32) % 4
            return wprT[:].rearrange("p (j s) -> p j s", j=32, s=D)[
                :, jc * 8 + k2: jc * 8 + k2 + 4, i * 128:(i + 1) * 128]

        def lnw_load(eng, g4):
            for k2 in range(DC):
                DS["dlnw"].issue(g4, eng.dma_start(
                    out=ln2win[g4 % 2][:, k2 * 512:(k2 + 1) * 512],
                    in_=ln2_hbm[:, k2 * TQ + g4 * 512:
                                k2 * TQ + (g4 + 1) * 512]))
            ms[("lnw", g4)] = DS["dlnw"].mark(g4)

        def p3_sync(eng):
            for g4 in range(2):
                lnw_load(eng, g4)
            for j in range(32):
                if j >= 2:
                    eng.wait_ge(S["wcast"].h, b["wcast"] + 2 * (j - 2) + 1)
                DS["dw3"].issue(j, eng.dma_start(
                    out=resstg[j % 2][:],
                    in_=wfc_d[j * 128:(j + 1) * 128, :]))
                ms[("w3", j)] = DS["dw3"].mark(j)
            eng.wait_ge(S["wcast"].h, b["wcast"] + 64)
            for g in range(NM):
                if g >= 2:
                    eng.wait_ge(S["snn"].h, b["snn"] + 2 * (g - 1))
                DS["dres"].issue(g, eng.dma_start(
                    out=resstg[g % 2][:],
                    in_=resid_hbm[g * 128:(g + 1) * 128, :]))
                ms[("res", g)] = DS["dres"].mark(g)
                if g >= 1:
                    eng.wait_ge(S["snn"].h, b["snn"] + 2 * g)
                    DS["dout"].issue(g - 1, eng.dma_start(
                        out=out_d[(g - 1) * 128:g * 128, :],
                        in_=outsb[(g - 1) % 2][:]))
                    ms[("out", g - 1)] = DS["dout"].mark(g - 1)
                if g % 4 == 2 and g // 4 + 2 < NG4:
                    gw = g // 4 + 2
                    eng.wait_ge(S["fcmm"].h, b["fcmm"] + 32 * (gw - 1))
                    lnw_load(eng, gw)
            eng.wait_ge(S["snn"].h, b["snn"] + 2 * NM)
            DS["dout"].issue(NM - 1, eng.dma_start(
                out=out_d[(NM - 1) * 128:NM * 128, :],
                in_=outsb[(NM - 1) % 2][:]))

        def trans8(eng, q):
            eng.wait_ge(S["wcast"].h, b["wcast"] + q + 1)
            src = wcst_fc[(q // 2) % 2] if q % 2 == 0 else wcst_pr[(q // 2) % 2]
            for k2 in range(DC):
                grp, sub = 2 * q + k2 // 4, k2 % 4
                if grp >= 2 and sub == 0:
                    eng.wait_ge(S["tpe"].h, b["tpe"] + grp - 1)
                S["tp"].inc(eng.transpose(
                    psT3[grp % 2][:, sub * 128:(sub + 1) * 128],
                    src[:, k2 * 128:(k2 + 1) * 128], idm[:]))

        def p3_tensor(eng):
            for g4 in range(NG4):
                for j in range(32):
                    jj = 32 * g4 + j
                    if g4 == 0:
                        trans8(eng, 2 * j)
                        eng.wait_ge(S["tpe"].h, b["tpe"] + 4 * j + 2)
                    if j == 0:
                        DS["dlnw"].wait(eng, ms[("lnw", g4)])
                    if jj >= 2:
                        eng.wait_ge(S["relu"].h, b["relu"] + jj - 1)
                    for k2 in range(DC):
                        mmi = eng.matmul(
                            psH[jj % 2][:],
                            wfcT[:, k2 * 4096 + j * 128: k2 * 4096 + (j + 1) * 128],
                            ln2win[g4 % 2][:, k2 * 512:(k2 + 1) * 512],
                            start=(k2 == 0), stop=(k2 == DC - 1))
                    S["fcmm"].inc(mmi)
                    if g4 == 0:
                        trans8(eng, 2 * j + 1)
                for sub in range(4):
                    g = 4 * g4 + sub
                    if g == 0:
                        eng.wait_ge(S["tpe"].h, b["tpe"] + 128)
                    if g >= 2:
                        eng.wait_ge(S["snn"].h, b["snn"] + 2 * (g - 1))
                    for j in range(32):
                        if sub == 0:
                            eng.wait_ge(S["h"].h, b["h"] + 32 * g4 + j + 1)
                        for half in range(2):
                            mmi = eng.matmul(
                                psY[sub % 2][:, half * 512:(half + 1) * 512],
                                hfm[:, j * 512 + sub * 128:
                                    j * 512 + sub * 128 + 128],
                                wprT[:, j * D + half * 512:
                                     j * D + half * 512 + 512],
                                start=(j == 0), stop=(j == 31))
                    S["prmm"].inc(mmi)

        def p3_scalar(eng):
            for p in range(2):
                DS["dwp"].issue(p, eng.dma_start(out=outsb[p % 2][:],
                                                 in_=wpr_src(p)))
                ms[("wp", p)] = DS["dwp"].mark(p)
            for g4 in range(NG4):
                for j in range(32):
                    jj = 32 * g4 + j
                    if g4 == 0 and j < 30:
                        p = j + 2
                        eng.wait_ge(S["wcast"].h, b["wcast"] + 2 * j + 2)
                        DS["dwp"].issue(p, eng.dma_start(out=outsb[p % 2][:],
                                                         in_=wpr_src(p)))
                        ms[("wp", p)] = DS["dwp"].mark(p)
                    def sq(s):
                        if s % 32 == 0 and s >= 32:
                            eng.wait_ge(S["prmm"].h, b["prmm"] + 4 * (s // 32))
                        S["h"].inc(eng.activation(
                            hfm[:, (s % 32) * 512:(s % 32 + 1) * 512],
                            relutmp[s % 2][:], AF.Square))

                    if j == 0 and jj > 0:
                        sq(jj - 1)
                    eng.wait_ge(S["fcmm"].h, b["fcmm"] + jj + 1)
                    S["relu"].inc(eng.activation(
                        relutmp[jj % 2][:], psH[jj % 2][:], AF.Relu))
                    if jj >= 1 and j != 0:
                        sq(jj - 1)
            sq(127)

        def p3_vector(eng):
            for j in range(33):
                if j >= 1:
                    for q in (2 * (j - 1), 2 * (j - 1) + 1):
                        w = q // 2 if q % 2 == 0 else 32 + q // 2
                        for halfg in range(2):
                            grp = 2 * q + halfg
                            eng.wait_ge(S["tp"].h, b["tp"] + 4 * (grp + 1))
                            S["tpe"].inc(eng.tensor_copy(
                                wdst4(w, halfg * 4),
                                psT3[grp % 2][:].rearrange("p (j s) -> p j s",
                                                           j=4, s=128)))
                if j < 32:
                    DS["dw3"].wait(eng, ms[("w3", j)])
                    if j >= 2:
                        eng.wait_ge(S["tp"].h, b["tp"] + 8 * (2 * (j - 2)) + 8)
                    S["wcast"].inc(eng.tensor_copy(wcst_fc[j % 2],
                                                   resstg[j % 2][:]))
                    DS["dwp"].wait(eng, ms[("wp", j)])
                    if j >= 2:
                        eng.wait_ge(S["tp"].h,
                                    b["tp"] + 8 * (2 * (j - 2) + 1) + 8)
                    S["wcast"].inc(eng.tensor_copy(wcst_pr[j % 2],
                                                   outsb[j % 2][:]))
            for g4 in range(NG4):
                for sub in range(4):
                    g = 4 * g4 + sub
                    eng.wait_ge(S["prmm"].h, b["prmm"] + g + 1)
                    DS["dres"].wait(eng, ms[("res", g)])
                    if g >= 2:
                        DS["dout"].wait(eng, ms[("out", g - 2)])
                    y2 = [psY[sub % 2][:, h * 512:(h + 1) * 512] for h in range(2)]
                    mem2 = [memb[:, 0:512], spkb[:, 0:512]]
                    spk2 = [memb[:, 512:1024], spkb[:, 512:1024]]
                    res2 = [resstg[g % 2][:, 0:512], resstg[g % 2][:, 512:1024]]
                    acc2 = [outsb[g % 2][:, 0:512], outsb[g % 2][:, 512:1024]]

                    def step(ops, inc=None):
                        for h in range(2):
                            inst = ops(h)
                            if inc is not None:
                                inc(inst)

                    step(lambda h: eng.tensor_copy(mem2[h], y2[h]))
                    step(lambda h: eng.tensor_single_scalar(spk2[h], mem2[h],
                                                            1.0, ALU.is_gt))
                    step(lambda h: eng.scalar_tensor_tensor(
                        acc2[h], spk2[h], 0.2, res2[h], ALU.mult, ALU.add))
                    for _s in range(4):
                        step(lambda h: eng.scalar_tensor_tensor(
                            mem2[h], mem2[h], 0.9, spk2[h],
                            ALU.mult, ALU.subtract))
                        step(lambda h: eng.tensor_add(mem2[h], mem2[h], y2[h]))
                        step(lambda h: eng.tensor_single_scalar(
                            spk2[h], mem2[h], 1.0, ALU.is_gt))
                        if _s < 3:
                            step(lambda h: eng.scalar_tensor_tensor(
                                acc2[h], spk2[h], 0.2, acc2[h],
                                ALU.mult, ALU.add))
                        else:
                            step(lambda h: eng.scalar_tensor_tensor(
                                acc2[h], spk2[h], 0.2, acc2[h],
                                ALU.mult, ALU.add),
                                inc=lambda inst: S["snn"].inc(inst))

        if 3 not in phases:
            p3.close()
        else:
            phase("p3", {
                "sync": (p3_sync, [DS["dw3"], DS["dres"], DS["dout"],
                         DS["dlnw"]]),
                "tensor": (p3_tensor, None),
                "scalar": (p3_scalar, [DS["dwp"]]),
                "vector": (p3_vector, None),
            })
            p3.close()

    return nc


_NC_CACHE = []


def kernel(L0_attn_x, L0_attn_Wk, L0_attn_Wq, L0_attn_Wo, L0_attn_bo,
           L0_ffn_W_fc, L0_ffn_W_proj):
    x = np.ascontiguousarray(np.asarray(L0_attn_x, dtype=np.float32))
    wk = np.ascontiguousarray(np.asarray(L0_attn_Wk, dtype=np.float32))
    wq = np.ascontiguousarray(np.asarray(L0_attn_Wq, dtype=np.float32))
    wo = np.ascontiguousarray(np.asarray(L0_attn_Wo, dtype=np.float32))
    bo = np.ascontiguousarray(np.asarray(L0_attn_bo, dtype=np.float32))
    wfc = np.ascontiguousarray(np.asarray(L0_ffn_W_fc, dtype=np.float32))
    wpr = np.ascontiguousarray(np.asarray(L0_ffn_W_proj, dtype=np.float32))
    ident = np.eye(128, dtype=ml_dtypes.bfloat16)

    if not _NC_CACHE:
        _NC_CACHE.append(build_nc())
    nc = _NC_CACHE[0]

    in_maps = []
    for c in range(8):
        bidx, qi = c // 4, c % 4
        in_maps.append({
            "xfull": x[bidx],
            "xmy": x[bidx, qi * TQ:(qi + 1) * TQ],
            "wq": wq, "wk": wk, "wo": wo, "bo": bo,
            "wfc": wfc, "wproj": wpr, "ident": ident,
        })
    res = run_bass_kernel_spmd(nc, in_maps, core_ids=list(range(8)))
    out = np.empty((2, TB, D), dtype=np.float32)
    for c in range(8):
        bidx, qi = c // 4, c % 4
        out[bidx, qi * TQ:(qi + 1) * TQ] = np.asarray(res.results[c]["out"])
    return out


# version 14
# speedup vs baseline: 1.0336x; 1.0155x over previous
import numpy as np
import ml_dtypes
from contextlib import ExitStack

import concourse.bass as bass
import concourse.mybir as mybir
from concourse.bass_utils import run_bass_kernel_spmd

F32 = mybir.dt.float32
BF16 = mybir.dt.bfloat16
AF = mybir.ActivationFunctionType
ALU = mybir.AluOpType
AXX = mybir.AxisListType.X

TB = 8192
TQ = 2048
D = 1024
DC = 8
NP = 8
NTK = 64
NC4 = 4
NM = 16
EPS = 1e-5


class Sem:
    def __init__(self, ctx, nc, name):
        self.h = ctx.enter_context(nc.semaphore(name))
        self.n = 0

    def inc(self, inst, k=1):
        inst.then_inc(self.h, k)
        self.n += k
        return inst


class DStream:

    def __init__(self, ctx, nc, name, nslots):
        self.sems = [Sem(ctx, nc, f"{name}{j}") for j in range(nslots)]
        self.ns = nslots

    def issue(self, slot, inst):
        self.sems[slot % self.ns].inc(inst, 16)

    def mark(self, slot):
        s = self.sems[slot % self.ns]
        return (slot % self.ns, s.n)

    def wait(self, eng, mark):
        j, v = mark
        eng.wait_ge(self.sems[j].h, v)

    def drain(self, eng):
        for s in self.sems:
            eng.wait_ge(s.h, s.n)


def build_nc(phases=(0, 1, 2, 3), p1_np=NP, p1_attn=True, p1_parts="kvq",
             dbg_scratch=False):
    nc = bass.Bass("TRN2", target_bir_lowering=False,
                   detect_race_conditions=False)

    xfull_d = nc.dram_tensor("xfull", [TB, D], F32, kind="ExternalInput").ap()
    xmy_d = nc.dram_tensor("xmy", [TQ, D], F32, kind="ExternalInput").ap()
    wq_d = nc.dram_tensor("wq", [D, D], F32, kind="ExternalInput").ap()
    wk_d = nc.dram_tensor("wk", [D, D], F32, kind="ExternalInput").ap()
    wo_d = nc.dram_tensor("wo", [D, D], F32, kind="ExternalInput").ap()
    bo_d = nc.dram_tensor("bo", [D], F32, kind="ExternalInput").ap()
    wfc_d = nc.dram_tensor("wfc", [4 * D, D], F32, kind="ExternalInput").ap()
    wpr_d = nc.dram_tensor("wproj", [D, 4 * D], F32, kind="ExternalInput").ap()
    idm_d = nc.dram_tensor("ident", [128, 128], BF16, kind="ExternalInput").ap()
    out_d = nc.dram_tensor("out", [TQ, D], F32, kind="ExternalOutput").ap()

    skind = "ExternalOutput" if dbg_scratch else "Internal"
    lnx_hbm = nc.dram_tensor("lnx_scr", [128, DC * TB], BF16, kind=skind).ap()
    lnmy_hbm = nc.dram_tensor("lnmy_scr", [128, DC * TQ], BF16, kind=skind).ap()
    attn_hbm = nc.dram_tensor("attn_scr", [128, NP * TQ], BF16, kind=skind).ap()
    lsum_hbm = nc.dram_tensor("lsum_scr", [64, 512], F32, kind=skind).ap()
    rsum_hbm = nc.dram_tensor("rsum_scr", [64, 512], F32, kind=skind).ap()
    resid_hbm = nc.dram_tensor("resid_scr", [TQ, D], F32, kind=skind).ap()
    ln2_hbm = nc.dram_tensor("ln2_scr", [128, DC * TQ], BF16, kind=skind).ap()

    ctx = ExitStack()
    with ctx:
        idm = ctx.enter_context(nc.sbuf_tensor("sb_idm", [128, 128], BF16))

        S = {}
        for name in ("ph sq var std rstd norm tp tpe wcast "
                     "kproj kevac vt vte qproj qevac sc exp avd ave div "
                     "womm resid rs fcmm relu h prmm snn vch").split():
            S[name] = Sem(ctx, nc, "sem_" + name)
        DS = {}
        for name, nsl in (("dx", 16), ("dlns", 2), ("dlnw", 2), ("dwl", 1),
                          ("dao", 1), ("dwo", 2), ("dbo", 1), ("dlsum", 1),
                          ("dxmy", 2), ("dattn", 2), ("dbc", 1), ("drs", 1),
                          ("dspill", 4), ("dw3", 2), ("dres", 2), ("dout", 2),
                          ("dwp", 2)):
            DS[name] = DStream(ctx, nc, name, nsl)

        phase_k = [0]

        def vi(inst):
            S["vch"].inc(inst)
            return inst

        def vw(eng):
            eng.wait_ge(S["vch"].h, S["vch"].n)

        def phase(block_name, bodies):
            k = phase_k[0]
            with nc.Block(block_name) as blk:
                for engname in ("sync", "tensor", "scalar", "vector", "gpsimd"):
                    body, drain = bodies.get(engname, (None, None))

                    def run(eng, body=body, drain=drain, k=k):
                        if k > 0:
                            eng.wait_ge(S["ph"].h, 5 * k)
                        if body is not None:
                            body(eng)
                        for d in (drain or []):
                            if isinstance(d, Sem):
                                eng.wait_ge(d.h, d.n)
                            else:
                                d.drain(eng)
                        S["ph"].inc(eng.nop())

                    getattr(blk, engname)(run)
            phase_k[0] += 1

        p0 = ExitStack()
        NXS = 16
        xstage = [p0.enter_context(nc.sbuf_tensor(f"sb_xs{i}", [128, D], F32))
                  for i in range(NXS)]
        lntm = [p0.enter_context(nc.sbuf_tensor(f"sb_lntm{i}", [128, D], BF16))
                for i in range(2)]
        xsq = p0.enter_context(nc.sbuf_tensor("sb_xsq", [128, D], F32))
        sxb = p0.enter_context(nc.sbuf_tensor("sb_sx", [128, 8], F32))
        sxqb = p0.enter_context(nc.sbuf_tensor("sb_sxq", [128, 8], F32))
        meanb = p0.enter_context(nc.sbuf_tensor("sb_mean", [128, 8], F32))
        ex2b = p0.enter_context(nc.sbuf_tensor("sb_ex2", [128, 8], F32))
        m2b = p0.enter_context(nc.sbuf_tensor("sb_m2", [128, 8], F32))
        varb = p0.enter_context(nc.sbuf_tensor("sb_var", [128, 8], F32))
        stdb = p0.enter_context(nc.sbuf_tensor("sb_std", [128, 8], F32))
        rstdb = p0.enter_context(nc.sbuf_tensor("sb_rstd", [128, 8], F32))
        negmb = p0.enter_context(nc.sbuf_tensor("sb_negm", [128, 8], F32))
        lnstg = [p0.enter_context(nc.sbuf_tensor(f"sb_lnst{i}", [128, 1024], BF16))
                 for i in range(2)]
        psT0 = [p0.enter_context(nc.psum_tensor(f"ps_t0{i}", [128, 512], BF16))
                for i in range(2)]

        NT = 80
        NG = 20

        def src_rows(i):
            return xfull_d[i * 128:(i + 1) * 128, :] if i < 64 else \
                xmy_d[(i - 64) * 128:(i - 63) * 128, :]

        def ln_dst(i):
            if i < 64:
                return lnx_hbm[:].rearrange("p (k s) -> p k s", k=DC, s=TB)[
                    :, :, i * 128:(i + 1) * 128]
            im = i - 64
            return lnmy_hbm[:].rearrange("p (k s) -> p k s", k=DC, s=TQ)[
                :, :, im * 128:(im + 1) * 128]

        def lncol(g, j):
            return (g % 2) * 4 + j

        def xmark(i):
            return (i % NXS, 16 * (i // NXS + 1))

        def lnsmark(i):
            return (i % 2, 16 * (i // 2 + 1))

        def p0_sync(eng):
            DS["dwl"].issue(0, eng.dma_start(out=idm[:], in_=idm_d))
            for i in range(NT):
                if i % 4 == 3:
                    continue
                if i >= NXS:
                    eng.wait_ge(S["norm"].h, i - NXS + 1)
                DS["dx"].issue(i, eng.dma_start(out=xstage[i % NXS][:],
                                                in_=src_rows(i)))

        def p0_tensor(eng):
            DS["dwl"].wait(eng, (0, 16))
            for i in range(NT):
                eng.wait_ge(S["norm"].h, i + 1)
                for k2 in range(DC):
                    t = 8 * i + k2
                    grp, sub = t // 4, t % 4
                    if grp >= 2 and sub == 0:
                        eng.wait_ge(S["tpe"].h, grp - 1)
                    S["tp"].inc(eng.transpose(
                        psT0[grp % 2][:, sub * 128:(sub + 1) * 128],
                        lntm[i % 2][:, k2 * 128:(k2 + 1) * 128],
                        idm[:]))

        def p0_scalar(eng):
            DS["dx"].issue(3, eng.dma_start(out=xstage[3][:], in_=src_rows(3)))
            for g in range(NG + 3):
                i4 = 4 * g + 7
                if i4 < NT:
                    DS["dx"].issue(i4, eng.dma_start(
                        out=xstage[i4 % NXS][:], in_=src_rows(i4)))
                if 0 <= g - 3 < NG:
                    for j in range(4):
                        i = 4 * (g - 3) + j
                        eng.wait_ge(S["tpe"].h, 2 * (i + 1))
                        DS["dlns"].issue(i, eng.dma_start(
                            out=ln_dst(i),
                            in_=lnstg[i % 2][:].rearrange("p (k s) -> p k s",
                                                          k=DC, s=128)))
                if g < NG:
                    if g >= 2:
                        eng.wait_ge(S["var"].h, g - 1)
                    for j in range(4):
                        i = 4 * g + j
                        DS["dx"].wait(eng, xmark(i))
                        S["sq"].inc(eng.activation(
                            xsq[:], xstage[i % NXS][:], AF.Square,
                            accum_out=sxqb[:, lncol(g, j):lncol(g, j) + 1]))
                if 0 <= g - 1 < NG:
                    gg = g - 1
                    eng.wait_ge(S["var"].h, gg + 1)
                    c0 = (gg % 2) * 4
                    S["std"].inc(eng.activation(
                        stdb[:, c0:c0 + 4], varb[:, c0:c0 + 4], AF.Sqrt))
                if 0 <= g - 2 < NG:
                    gg = g - 2
                    eng.wait_ge(S["rstd"].h, gg + 1)
                    for j in range(4):
                        i = 4 * gg + j
                        if i >= 2:
                            eng.wait_ge(S["tp"].h, 8 * (i - 1))
                        c = lncol(gg, j)
                        S["norm"].inc(eng.activation(
                            lntm[i % 2][:], xstage[i % NXS][:], AF.Identity,
                            scale=rstdb[:, c:c + 1], bias=negmb[:, c:c + 1]))

        def p0_vector(eng):
            for g in range(NG + 3):
                if g < NG:
                    for j in range(4):
                        i = 4 * g + j
                        DS["dx"].wait(eng, xmark(i))
                        eng.reduce_sum(sxb[:, lncol(g, j):lncol(g, j) + 1],
                                       xstage[i % NXS][:], axis=AXX)
                ga = g - 1 if 0 <= g - 1 < NG else None
                gb = g - 2 if 0 <= g - 2 < NG else None
                if ga is not None:
                    eng.wait_ge(S["sq"].h, 4 * (ga + 1))
                    if ga >= 2:
                        eng.wait_ge(S["std"].h, ga - 1)
                if gb is not None:
                    eng.wait_ge(S["std"].h, gb + 1)
                    if gb >= 2:
                        eng.wait_ge(S["norm"].h, 4 * (gb - 1))
                cb = (gb % 2) * 4 if gb is not None else 0
                if ga is not None:
                    ca = (ga % 2) * 4

                    def a_ex2():
                        eng.tensor_scalar_mul(ex2b[:, ca:ca + 4],
                                              sxqb[:, ca:ca + 4], 1.0 / D)
                    eng.tensor_scalar_mul(meanb[:, ca:ca + 4],
                                          sxb[:, ca:ca + 4], 1.0 / D)
                    a_ex2()
                    eng.tensor_mul(m2b[:, ca:ca + 4], meanb[:, ca:ca + 4],
                                   meanb[:, ca:ca + 4])
                    if gb is not None:
                        eng.reciprocal(rstdb[:, cb:cb + 4], stdb[:, cb:cb + 4])
                    else:
                        a_ex2()
                    eng.tensor_sub(varb[:, ca:ca + 4], ex2b[:, ca:ca + 4],
                                   m2b[:, ca:ca + 4])
                    if gb is not None:
                        S["rstd"].inc(eng.scalar_tensor_tensor(
                            negmb[:, cb:cb + 4], meanb[:, cb:cb + 4], -1.0,
                            rstdb[:, cb:cb + 4], ALU.mult, ALU.mult))
                    else:
                        a_ex2()
                    S["var"].inc(eng.tensor_scalar_add(
                        varb[:, ca:ca + 4], varb[:, ca:ca + 4], EPS))
                elif gb is not None:
                    eng.reciprocal(rstdb[:, cb:cb + 4], stdb[:, cb:cb + 4])
                    eng.reciprocal(rstdb[:, cb:cb + 4], stdb[:, cb:cb + 4])
                    S["rstd"].inc(eng.scalar_tensor_tensor(
                        negmb[:, cb:cb + 4], meanb[:, cb:cb + 4], -1.0,
                        rstdb[:, cb:cb + 4], ALU.mult, ALU.mult))
                if 0 <= g - 3 < NG:
                    for j in range(4):
                        i = 4 * (g - 3) + j
                        for half in range(2):
                            grp = 2 * i + half
                            eng.wait_ge(S["tp"].h, 4 * (grp + 1))
                            if i >= 2 and half == 0:
                                DS["dlns"].wait(eng, lnsmark(i - 2))
                            S["tpe"].inc(eng.tensor_copy(
                                lnstg[i % 2][:, half * 512:(half + 1) * 512],
                                psT0[grp % 2][:]))


        if 0 not in phases:
            p0.close()
        else:
            phase("p0", {
                "sync": (p0_sync, [DS["dx"], DS["dwl"]]),
                "tensor": (p0_tensor, None),
                "scalar": (p0_scalar, [DS["dlns"]]),
                "vector": (p0_vector, None),
            })
            p0.close()

        p1 = ExitStack()
        wstage = p1.enter_context(nc.sbuf_tensor("sb_wst", [128, D], F32))
        wqk_bf = p1.enter_context(nc.sbuf_tensor("sb_wqk", [128, 2048], BF16))
        lnxwin = [p1.enter_context(nc.sbuf_tensor(f"sb_lnw{i}", [128, 4096], BF16))
                  for i in range(2)]
        kfm2 = [p1.enter_context(nc.sbuf_tensor(f"sb_kfm{i}", [128, TB], BF16))
                for i in range(2)]
        vtm2 = [p1.enter_context(nc.sbuf_tensor(f"sb_vtm{i}", [128, NTK * 130], BF16))
                for i in range(2)]
        qfm2 = [p1.enter_context(nc.sbuf_tensor(f"sb_qfm{i}", [128, TQ], BF16))
                for i in range(2)]
        pt = [p1.enter_context(nc.sbuf_tensor(f"sb_pt{i}", [128, 1024], BF16))
              for i in range(4)]
        h1st = [p1.enter_context(nc.sbuf_tensor(f"sb_h1st{i}", [64, 512], BF16))
                for i in range(2)]
        h2st = [p1.enter_context(nc.sbuf_tensor(f"sb_h2st{i}", [64, 512], BF16))
                for i in range(2)]
        lrst = p1.enter_context(nc.sbuf_tensor("sb_lrst", [128, 1024], F32))
        psS = [p1.enter_context(nc.psum_tensor(f"ps_s{i}", [128, 1024], F32))
               for i in range(2)]
        psAV = [p1.enter_context(nc.psum_tensor(f"ps_av{i}", [128, 512], F32))
                for i in range(2)]
        psP = [p1.enter_context(nc.psum_tensor(f"ps_p{i}", [128, 512], F32))
               for i in range(2)]
        psPb = [psP[i][:, 0:256].bitcast(BF16) for i in range(2)]

        vtm4 = [vtm2[i][:].rearrange("p (t j s) -> p t j s", t=NTK, j=2, s=65)
                for i in range(2)]
        b = {k: S[k].n for k in S}
        bao = DS["dao"].sems[0].n
        ms = {}

        def p1_sync(eng):
            for p in range(NP):
                if p > 0:
                    eng.wait_ge(S["wcast"].h, b["wcast"] + 2 * p)
                for k2 in range(DC):
                    DS["dwl"].issue(0, eng.dma_start(
                        out=wstage[:, k2 * 128:(k2 + 1) * 128],
                        in_=wk_d[k2 * 128:(k2 + 1) * 128, p * 128:(p + 1) * 128]))
                ms[("wk", p)] = DS["dwl"].mark(0)
                for t16 in range(16):
                    if t16 == 0 and p > 0:
                        eng.wait_ge(S["qproj"].h, b["qproj"] + 4 * p)
                    if 16 * p + t16 - 1 > 0:
                        eng.wait_ge(S["kproj"].h, b["kproj"] + 16 * p + t16 - 1)
                    for k2 in range(DC):
                        DS["dlnw"].issue(t16, eng.dma_start(
                            out=lnxwin[t16 % 2][:, k2 * 512:(k2 + 1) * 512],
                            in_=lnx_hbm[:, k2 * TB + t16 * 512:
                                        k2 * TB + (t16 + 1) * 512]))
                    ms[("kw", p, t16)] = DS["dlnw"].mark(t16)
                eng.wait_ge(S["wcast"].h, b["wcast"] + 2 * p + 1)
                for k2 in range(DC):
                    DS["dwl"].issue(0, eng.dma_start(
                        out=wstage[:, k2 * 128:(k2 + 1) * 128],
                        in_=wq_d[k2 * 128:(k2 + 1) * 128, p * 128:(p + 1) * 128]))
                ms[("wq", p)] = DS["dwl"].mark(0)
                for c in range(NC4):
                    if c == 0:
                        eng.wait_ge(S["kproj"].h, b["kproj"] + 16 * (p + 1))
                    if c >= 2:
                        eng.wait_ge(S["qproj"].h, b["qproj"] + 4 * p + c - 1)
                    for k2 in range(DC):
                        DS["dlnw"].issue(c, eng.dma_start(
                            out=lnxwin[c % 2][:, k2 * 512:(k2 + 1) * 512],
                            in_=lnmy_hbm[:, k2 * TQ + c * 512:
                                         k2 * TQ + (c + 1) * 512]))
                    ms[("qw", p, c)] = DS["dlnw"].mark(c)

        def prep_jobs(eng, q):
            def kjob(t16):
                def f():
                    if t16 == 0:
                        eng.wait_ge(S["wcast"].h, b["wcast"] + 2 * q + 1)
                    DS["dlnw"].wait(eng, ms[("kw", q, t16)])
                    if 16 * q + t16 - 1 > 0:
                        eng.wait_ge(S["kevac"].h, b["kevac"] + 16 * q + t16 - 1)
                    for k2 in range(DC):
                        mmi = eng.matmul(
                            psP[t16 % 2][:],
                            wqk_bf[:, k2 * 128:(k2 + 1) * 128],
                            lnxwin[t16 % 2][:, k2 * 512:(k2 + 1) * 512],
                            start=(k2 == 0), stop=(k2 == DC - 1))
                    S["kproj"].inc(mmi)
                return f

            def vjob(grp):
                def f():
                    eng.wait_ge(S["kevac"].h, b["kevac"] + 16 * (q + 1))
                    if grp >= 2:
                        eng.wait_ge(S["vte"].h, b["vte"] + 16 * q + grp - 1)
                    for sub in range(4):
                        t = 4 * grp + sub
                        S["vt"].inc(eng.transpose(
                            psPb[grp % 2][:, sub * 128:(sub + 1) * 128],
                            kfm2[q % 2][:, t * 128:(t + 1) * 128], idm[:]))
                return f

            def qjob(c):
                def f():
                    if c == 0:
                        eng.wait_ge(S["wcast"].h, b["wcast"] + 2 * q + 2)
                        eng.wait_ge(S["vte"].h, b["vte"] + 16 * (q + 1))
                    DS["dlnw"].wait(eng, ms[("qw", q, c)])
                    if c >= 2:
                        eng.wait_ge(S["qevac"].h, b["qevac"] + 4 * q + c - 1)
                    for k2 in range(DC):
                        mmi = eng.matmul(
                            psP[c % 2][:],
                            wqk_bf[:, 1024 + k2 * 128: 1024 + (k2 + 1) * 128],
                            lnxwin[c % 2][:, k2 * 512:(k2 + 1) * 512],
                            start=(k2 == 0), stop=(k2 == DC - 1))
                    S["qproj"].inc(mmi)
                return f

            jobs = {}
            for t16 in range(16):
                jobs[7 * t16 + 3] = kjob(t16)
            for grp in range(8):
                jobs[130 + 4 * grp] = vjob(grp)
            for grp in range(8, 16):
                jobs[196 + 4 * (grp - 8)] = vjob(grp)
            for c in range(NC4):
                jobs[229 + 7 * c] = qjob(c)
            return jobs

        def p1_tensor(eng):
            jobs0 = prep_jobs(eng, 0)
            for slot in sorted(jobs0):
                jobs0[slot]()
            for q in range(NP):
                jobs = prep_jobs(eng, q + 1) if q + 1 < NP else {}
                kfm = kfm2[q % 2]
                qfm = qfm2[q % 2]
                eng.wait_ge(S["vte"].h, b["vte"] + 16 * (q + 1))
                eng.wait_ge(S["qevac"].h, b["qevac"] + 4 * (q + 1))
                for c in range(NC4):
                    base = 256 * q + 64 * c

                    def scores(t, c=c, base=base):
                        if base + t - 1 > 0:
                            eng.wait_ge(S["exp"].h, b["exp"] + base + t - 1)
                        eng.matmul(psS[t % 2][:, 0:512],
                                   kfm[0:64, t * 128:(t + 1) * 128],
                                   qfm[0:64, c * 512:(c + 1) * 512],
                                   start=True, stop=True)
                        S["sc"].inc(eng.matmul(
                            psS[t % 2][:, 512:1024],
                            kfm[64:128, t * 128:(t + 1) * 128],
                            qfm[64:128, c * 512:(c + 1) * 512],
                            start=True, stop=True))

                    def av(t, c=c, base=base):
                        eng.wait_ge(S["exp"].h, b["exp"] + base + t + 1)
                        if t == 0 and (q > 0 or c > 0):
                            eng.wait_ge(S["ave"].h, b["ave"] + 4 * q + c)
                        eng.matmul(psAV[0][0:65, :],
                                   vtm4[q % 2][:, t, 0, 0:65],
                                   pt[t % 4][:, 0:512],
                                   start=(t == 0), stop=(t == NTK - 1))
                        S["avd"].inc(eng.matmul(
                            psAV[1][0:65, :],
                            vtm4[q % 2][:, t, 1, 0:65],
                            pt[t % 4][:, 512:1024],
                            start=(t == 0), stop=(t == NTK - 1)))

                    for t in range(NTK):
                        slot = 64 * c + t
                        scores(t)
                        if t >= 1:
                            av(t - 1)
                        if slot in jobs:
                            jobs.pop(slot)()
                    av(NTK - 1)
                for slot in sorted(jobs):
                    jobs.pop(slot)()

        def ao_store(eng, idx):
            col = idx * 512
            pp, cc = idx // 4, idx % 4
            row = pp * 8 + cc * 2
            DS["dao"].issue(0, eng.dma_start(
                out=attn_hbm[0:64, col:col + 512], in_=h1st[idx % 2][:]))
            DS["dao"].issue(0, eng.dma_start(
                out=attn_hbm[64:128, col:col + 512], in_=h2st[idx % 2][:]))
            DS["dao"].issue(0, eng.dma_start(
                out=lsum_hbm[row:row + 1, :], in_=lrst[64:65, 0:512]))
            DS["dao"].issue(0, eng.dma_start(
                out=lsum_hbm[row + 1:row + 2, :], in_=lrst[64:65, 512:1024]))

        def p1_scalar_full(eng):
            for p in range(NP):
                for c in range(NC4):
                    idx = 4 * p + c
                    for t in range(NTK):
                        base = 256 * p + 64 * c
                        eng.wait_ge(S["sc"].h, b["sc"] + base + t + 1)
                        if base + t - 3 > 0:
                            eng.wait_ge(S["avd"].h, b["avd"] + base + t - 3)
                        S["exp"].inc(eng.activation(
                            pt[t % 4][:], psS[t % 2][:], AF.Exp, scale=1.0 / 32.0))
                    eng.wait_ge(S["avd"].h, b["avd"] + 256 * p + 64 * (c + 1))
                    if idx >= 1:
                        DS["dao"].wait(eng, (0, bao + 64 * idx))
                    eng.activation(h1st[idx % 2][:], psAV[0][0:64, :],
                                   AF.Identity)
                    eng.activation(h2st[idx % 2][:], psAV[1][0:64, :],
                                   AF.Identity)
                    eng.activation(lrst[64:65, 0:512], psAV[0][64:65, :],
                                   AF.Identity)
                    S["ave"].inc(eng.activation(
                        lrst[64:65, 512:1024], psAV[1][64:65, :], AF.Identity))
                    ao_store(eng, idx)

        def p1_vector_prep(eng, q, section):
            if section == 0:
                DS["dwl"].wait(eng, ms[("wk", q)])
                if q > 0:
                    eng.wait_ge(S["kproj"].h, b["kproj"] + 16 * q)
                S["wcast"].inc(eng.tensor_copy(wqk_bf[:, 0:1024], wstage[:]))
            elif section in (1, 2):
                for j in range(8 * (section - 1), 8 * section):
                    eng.wait_ge(S["kproj"].h, b["kproj"] + 16 * q + j + 1)
                    S["kevac"].inc(eng.tensor_copy(
                        kfm2[q % 2][:, j * 512:(j + 1) * 512], psP[j % 2][:]))
            elif section in (3, 4):
                if section == 4:
                    DS["dwl"].wait(eng, ms[("wq", q)])
                    eng.wait_ge(S["qproj"].h, b["qproj"] + 4 * q)
                    S["wcast"].inc(eng.tensor_copy(wqk_bf[:, 1024:2048],
                                                   wstage[:]))
                for grp in range(8 * (section - 3), 8 * (section - 2)):
                    eng.wait_ge(S["vt"].h, b["vt"] + 64 * q + 4 * (grp + 1))
                    if grp == 0:
                        if q >= 1:
                            eng.wait_ge(S["avd"].h, b["avd"] + 256 * (q - 1))
                        eng.memset(vtm4[q % 2][:, :, :, 64:65], 1.0)
                    S["vte"].inc(eng.tensor_copy(
                        vtm4[q % 2][:, 4 * grp:4 * grp + 4, :, 0:64],
                        psPb[grp % 2][:].rearrange("p (t j s) -> p t j s",
                                                   t=4, j=2, s=64)))
            elif section == 5:
                for c in range(2):
                    eng.wait_ge(S["qproj"].h, b["qproj"] + 4 * q + c + 1)
                    S["qevac"].inc(eng.tensor_copy(
                        qfm2[q % 2][:, c * 512:(c + 1) * 512], psP[c % 2][:]))
            elif section == 6:
                for c in range(2, NC4):
                    eng.wait_ge(S["qproj"].h, b["qproj"] + 4 * q + c + 1)
                    S["qevac"].inc(eng.tensor_copy(
                        qfm2[q % 2][:, c * 512:(c + 1) * 512], psP[c % 2][:]))

        def p1_vector(eng):
            for sec in range(7):
                p1_vector_prep(eng, 0, sec)
            for p in range(NP):
                for c in range(NC4):
                    if p + 1 < NP:
                        if c == 0:
                            p1_vector_prep(eng, p + 1, 0)
                            p1_vector_prep(eng, p + 1, 1)
                        elif c == 1:
                            p1_vector_prep(eng, p + 1, 2)
                        elif c == 2:
                            p1_vector_prep(eng, p + 1, 3)
                        else:
                            p1_vector_prep(eng, p + 1, 4)
                            p1_vector_prep(eng, p + 1, 5)
                if p + 1 < NP:
                    p1_vector_prep(eng, p + 1, 6)

        if 1 not in phases:
            p1.close()
        else:
            phase("p1", {
                "sync": (p1_sync, None),
                "tensor": (p1_tensor, None),
                "scalar": (p1_scalar_full, [DS["dao"]]),
                "vector": (p1_vector, None),
            })
            p1.close()

        p2 = ExitStack()
        wostage = [p2.enter_context(nc.sbuf_tensor(f"sb_wos{i}", [128, D], F32))
                   for i in range(2)]
        wo_bf = p2.enter_context(nc.sbuf_tensor("sb_wobf", [128, DC * D], BF16))
        bo_rep = p2.enter_context(nc.sbuf_tensor("sb_borep", [128, D], F32))
        lsum_sb = p2.enter_context(nc.sbuf_tensor("sb_lsum", [64, 512], F32))
        rsum_sb = p2.enter_context(nc.sbuf_tensor("sb_rsum", [64, 512], F32))
        R128 = [p2.enter_context(nc.sbuf_tensor(f"sb_r128_{p}", [128, 512], F32))
                for p in range(NP)]
        attnstg = [p2.enter_context(nc.sbuf_tensor(f"sb_astg{i}", [128, D], BF16))
                   for i in range(2)]
        attndiv = [p2.enter_context(nc.sbuf_tensor(f"sb_adiv{i}", [128, D], BF16))
                   for i in range(2)]
        xmystg = [p2.enter_context(nc.sbuf_tensor(f"sb_xmys{i}", [128, D], F32))
                  for i in range(2)]
        residb = [p2.enter_context(nc.sbuf_tensor(f"sb_res{i}", [128, D], F32))
                  for i in range(4)]
        ln2tm = [p2.enter_context(nc.sbuf_tensor(f"sb_ln2tm{i}", [128, D], BF16))
                 for i in range(2)]
        xsq2 = p2.enter_context(nc.sbuf_tensor("sb_xsq2", [128, D], F32))
        st2 = p2.enter_context(nc.sbuf_tensor("sb_st2", [128, 16], F32))
        psWo = [p2.enter_context(nc.psum_tensor(f"ps_wo{i}", [128, D], F32))
                for i in range(2)]
        ln2stg = [p2.enter_context(nc.sbuf_tensor(f"sb_l2st{i}", [128, 1024],
                                                   BF16)) for i in range(2)]
        psT2 = [p2.enter_context(nc.psum_tensor(f"ps_t2{i}", [128, 512], BF16))
                for i in range(2)]

        b = {k: S[k].n for k in S}
        bspill = DS["dspill"].sems[0].n
        ms = {}

        def p2_sync(eng):
            for k2 in range(DC):
                if k2 >= 2:
                    eng.wait_ge(S["wcast"].h, b["wcast"] + k2 - 1)
                DS["dwo"].issue(k2, eng.dma_start(
                    out=wostage[k2 % 2][:],
                    in_=wo_d[k2 * 128:(k2 + 1) * 128, :]))
                ms[("wo", k2)] = DS["dwo"].mark(k2)
            DS["dbo"].issue(0, eng.dma_start(
                out=bo_rep[:], in_=bo_d[None, :].broadcast_to([128, D])))
            ms["bo"] = DS["dbo"].mark(0)
            DS["dlsum"].issue(0, eng.dma_start(out=lsum_sb[:], in_=lsum_hbm))
            ms["lsum"] = DS["dlsum"].mark(0)
            for c in range(NC4):
                eng.wait_ge(S["rs"].h, b["rs"] + 1)
                DS["drs"].wait(eng, (0, 16))
                if c > 0:
                    eng.wait_ge(S["div"].h, b["div"] + 32 * c)
                for p in range(NP):
                    row = p * 8 + c * 2
                    DS["dbc"].issue(0, eng.dma_start(
                        out=R128[p][0:64, :],
                        in_=rsum_hbm[row:row + 1, :].broadcast_to([64, 512])))
                    DS["dbc"].issue(0, eng.dma_start(
                        out=R128[p][64:128, :],
                        in_=rsum_hbm[row + 1:row + 2, :].broadcast_to([64, 512])))
                ms[("bc", c)] = DS["dbc"].mark(0)
                for m in range(4 * c, 4 * c + 4):
                    if m >= 2:
                        eng.wait_ge(S["resid"].h, b["resid"] + m - 1)
                    DS["dxmy"].issue(m, eng.dma_start(
                        out=xmystg[m % 2][:], in_=xmy_d[m * 128:(m + 1) * 128, :]))
                    ms[("xmy", m)] = DS["dxmy"].mark(m)
                    if m >= 2:
                        eng.wait_ge(S["div"].h, b["div"] + 8 * (m - 1))
                    for p in range(NP):
                        DS["dattn"].issue(m, eng.dma_start(
                            out=attnstg[m % 2][:, p * 128:(p + 1) * 128],
                            in_=attn_hbm[:, p * TQ + m * 128:
                                         p * TQ + (m + 1) * 128]))
                    ms[("attn", m)] = DS["dattn"].mark(m)

        def p2_tensor(eng):
            eng.wait_ge(S["wcast"].h, b["wcast"] + DC)
            for m in range(NM):
                eng.wait_ge(S["div"].h, b["div"] + 8 * (m + 1))
                if m >= 2:
                    eng.wait_ge(S["resid"].h, b["resid"] + m - 1)
                for half in range(2):
                    for p in range(NP):
                        mmi = eng.matmul(
                            psWo[m % 2][:, half * 512:(half + 1) * 512],
                            attndiv[m % 2][:, p * 128:(p + 1) * 128],
                            wo_bf[:, p * D + half * 512:
                                  p * D + half * 512 + 512],
                            start=(p == 0), stop=(p == NP - 1))
                S["womm"].inc(mmi)
                if m >= 2:
                    mm = m - 2
                    eng.wait_ge(S["norm"].h, b["norm"] + mm + 1)
                    for k2 in range(DC):
                        grp, sub = (8 * mm + k2) // 4, k2 % 4
                        if grp >= 2 and sub == 0:
                            eng.wait_ge(S["tpe"].h, b["tpe"] + grp - 1)
                        S["tp"].inc(eng.transpose(
                            psT2[grp % 2][:, sub * 128:(sub + 1) * 128],
                            ln2tm[mm % 2][:, k2 * 128:(k2 + 1) * 128], idm[:]))
            for mm in (NM - 2, NM - 1):
                eng.wait_ge(S["norm"].h, b["norm"] + mm + 1)
                for k2 in range(DC):
                    grp, sub = (8 * mm + k2) // 4, k2 % 4
                    if grp >= 2 and sub == 0:
                        eng.wait_ge(S["tpe"].h, b["tpe"] + grp - 1)
                    S["tp"].inc(eng.transpose(
                        psT2[grp % 2][:, sub * 128:(sub + 1) * 128],
                        ln2tm[mm % 2][:, k2 * 128:(k2 + 1) * 128], idm[:]))

        def ln2_dst(mm):
            return ln2_hbm[:].rearrange("p (k s) -> p k s", k=DC, s=TQ)[
                :, :, mm * 128:(mm + 1) * 128]

        def p2_scalar(eng):
            eng.wait_ge(S["rs"].h, b["rs"] + 1)
            DS["drs"].issue(0, eng.dma_start(out=rsum_hbm, in_=rsum_sb[:]))
            for m in range(NM + 3):
                if 0 <= m - 3 < NM:
                    mm = m - 3
                    eng.wait_ge(S["tpe"].h, b["tpe"] + 2 * (mm + 1))
                    DS["dlns"].issue(mm, eng.dma_start(
                        out=ln2_dst(mm),
                        in_=ln2stg[mm % 2][:].rearrange("p (k s) -> p k s",
                                                        k=DC, s=128)))
                    ms[("l2s", mm)] = DS["dlns"].mark(mm)
                if m < NM:
                    eng.wait_ge(S["resid"].h, b["resid"] + m + 1)
                    DS["dspill"].issue(m, eng.dma_start(
                        out=resid_hbm[m * 128:(m + 1) * 128, :],
                        in_=residb[m % 4][:]))
                    S["sq"].inc(eng.activation(
                        xsq2[:], residb[m % 4][:], AF.Square,
                        accum_out=st2[:, 2 + m % 2: 3 + m % 2]))
                if 0 <= m - 1 < NM:
                    mm = m - 1
                    eng.wait_ge(S["var"].h, b["var"] + mm + 1)
                    S["std"].inc(eng.activation(
                        st2[:, 10 + mm % 2: 11 + mm % 2],
                        st2[:, 8 + mm % 2: 9 + mm % 2], AF.Sqrt))
                if 0 <= m - 2 < NM:
                    mm = m - 2
                    eng.wait_ge(S["rstd"].h, b["rstd"] + mm + 1)
                    if mm >= 2:
                        eng.wait_ge(S["tp"].h, b["tp"] + 8 * (mm - 1))
                    S["norm"].inc(eng.activation(
                        ln2tm[mm % 2][:], residb[mm % 4][:], AF.Identity,
                        scale=st2[:, 12 + mm % 2: 13 + mm % 2],
                        bias=st2[:, 6 + mm % 2: 7 + mm % 2]))

        def p2_vector(eng):
            for k2 in range(DC):
                DS["dwo"].wait(eng, ms[("wo", k2)])
                S["wcast"].inc(eng.tensor_copy(
                    wo_bf[:, k2 * D:(k2 + 1) * D], wostage[k2 % 2][:]))
            DS["dlsum"].wait(eng, ms["lsum"])
            S["rs"].inc(eng.reciprocal(rsum_sb[:], lsum_sb[:]), 1)
            for m in range(NM + 2):
                if m < NM:
                    c = m // 4
                    DS["dattn"].wait(eng, ms[("attn", m)])
                    DS["dbc"].wait(eng, ms[("bc", c)])
                    for p in range(NP):
                        S["div"].inc(eng.tensor_mul(
                            attndiv[m % 2][:, p * 128:(p + 1) * 128],
                            attnstg[m % 2][:, p * 128:(p + 1) * 128],
                            R128[p][:, (m % 4) * 128:(m % 4 + 1) * 128]))
                    eng.wait_ge(S["womm"].h, b["womm"] + m + 1)
                    DS["dxmy"].wait(eng, ms[("xmy", m)])
                    DS["dbo"].wait(eng, ms["bo"])
                    if m >= 4:
                        eng.wait_ge(S["norm"].h, b["norm"] + m - 3)
                        DS["dspill"].wait(eng, ((m - 4) % 4,
                                                bspill + 16 * ((m - 4) // 4 + 1)))
                    rb, ps2m, xm = residb[m % 4], psWo[m % 2], xmystg[m % 2]
                    eng.tensor_add(rb[:, 0:512], ps2m[:, 0:512], xm[:, 0:512])
                    eng.tensor_add(rb[:, 512:1024], ps2m[:, 512:1024],
                                   xm[:, 512:1024])
                    eng.tensor_add(rb[:, 0:512], rb[:, 0:512],
                                   bo_rep[:, 0:512])
                    S["resid"].inc(eng.tensor_add(
                        rb[:, 512:1024], rb[:, 512:1024], bo_rep[:, 512:1024]))
                    mmB = m - 1 if 1 <= m <= NM else None
                    if mmB is not None:
                        eng.wait_ge(S["std"].h, b["std"] + mmB + 1)
                        if mmB >= 2:
                            eng.wait_ge(S["norm"].h, b["norm"] + mmB - 1)
                        eng.reciprocal(st2[:, 12 + mmB % 2:13 + mmB % 2],
                                       st2[:, 10 + mmB % 2:11 + mmB % 2])
                    eng.wait_ge(S["sq"].h, b["sq"] + m + 1)

                    def a_ex2():
                        eng.tensor_scalar_mul(st2[:, 8 + m % 2:9 + m % 2],
                                              st2[:, 2 + m % 2:3 + m % 2],
                                              1.0 / D)
                    if mmB is None:
                        a_ex2()
                    eng.reduce_sum(st2[:, m % 2:m % 2 + 1], rb[:], axis=AXX)
                    if mmB is None:
                        a_ex2()
                    else:
                        a_ex2()
                    eng.tensor_scalar_mul(st2[:, 4 + m % 2:5 + m % 2],
                                          st2[:, m % 2:m % 2 + 1], 1.0 / D)
                    if mmB is not None:
                        S["rstd"].inc(eng.scalar_tensor_tensor(
                            st2[:, 6 + mmB % 2:7 + mmB % 2],
                            st2[:, 4 + mmB % 2:5 + mmB % 2], -1.0,
                            st2[:, 12 + mmB % 2:13 + mmB % 2],
                            ALU.mult, ALU.mult))
                    else:
                        a_ex2()
                    eng.tensor_mul(st2[:, 14:15], st2[:, 4 + m % 2:5 + m % 2],
                                   st2[:, 4 + m % 2:5 + m % 2])
                    a_ex2()
                    eng.tensor_sub(st2[:, 8 + m % 2:9 + m % 2],
                                   st2[:, 8 + m % 2:9 + m % 2], st2[:, 14:15])
                    eng.tensor_mul(st2[:, 15:16], st2[:, 4 + m % 2:5 + m % 2],
                                   st2[:, 4 + m % 2:5 + m % 2])
                    S["var"].inc(eng.tensor_scalar_add(
                        st2[:, 8 + m % 2:9 + m % 2],
                        st2[:, 8 + m % 2:9 + m % 2], EPS))
                if m == NM:
                    mmB = m - 1
                    eng.wait_ge(S["std"].h, b["std"] + mmB + 1)
                    if mmB >= 2:
                        eng.wait_ge(S["norm"].h, b["norm"] + mmB - 1)
                    eng.reciprocal(st2[:, 12 + mmB % 2:13 + mmB % 2],
                                   st2[:, 10 + mmB % 2:11 + mmB % 2])
                    eng.reciprocal(st2[:, 12 + mmB % 2:13 + mmB % 2],
                                   st2[:, 10 + mmB % 2:11 + mmB % 2])
                    S["rstd"].inc(eng.scalar_tensor_tensor(
                        st2[:, 6 + mmB % 2:7 + mmB % 2],
                        st2[:, 4 + mmB % 2:5 + mmB % 2], -1.0,
                        st2[:, 12 + mmB % 2:13 + mmB % 2],
                        ALU.mult, ALU.mult))
                if 0 <= m - 2 < NM:
                    mm = m - 2
                    for half in range(2):
                        grp = 2 * mm + half
                        eng.wait_ge(S["tp"].h, b["tp"] + 4 * (grp + 1))
                        if mm >= 2 and half == 0:
                            DS["dlns"].wait(eng, ms[("l2s", mm - 2)])
                        S["tpe"].inc(eng.tensor_copy(
                            ln2stg[mm % 2][:, half * 512:(half + 1) * 512],
                            psT2[grp % 2][:]))

        if 2 not in phases:
            p2.close()
        else:
            phase("p2", {
                "sync": (p2_sync, None),
                "tensor": (p2_tensor, None),
                "scalar": (p2_scalar, [DS["dlns"], DS["dspill"], DS["drs"]]),
                "vector": (p2_vector, None),
            })
            p2.close()

        p3 = ExitStack()
        wfcT = p3.enter_context(nc.sbuf_tensor("sb_wfcT", [128, DC * 4096], BF16))
        wprT = p3.enter_context(nc.sbuf_tensor("sb_wprT", [128, 32 * D], BF16))
        hfm = p3.enter_context(nc.sbuf_tensor("sb_hfm", [128, 32 * 512], BF16))
        relutmp = [p3.enter_context(nc.sbuf_tensor(f"sb_rel{i}", [128, 512], BF16))
                   for i in range(2)]
        memb = p3.enter_context(nc.sbuf_tensor("sb_mem", [128, 1024], F32))
        spkb = p3.enter_context(nc.sbuf_tensor("sb_spk", [128, 1024], F32))
        resstg = [p3.enter_context(nc.sbuf_tensor(f"sb_rstg{i}", [128, D], F32))
                  for i in range(2)]
        outsb = [p3.enter_context(nc.sbuf_tensor(f"sb_outs{i}", [128, D], F32))
                 for i in range(2)]
        ln2win = [p3.enter_context(nc.sbuf_tensor(f"sb_l2w{i}", [128, DC * 512],
                                                  BF16)) for i in range(2)]
        membBF = memb[:].bitcast(BF16)
        spkbBF = spkb[:].bitcast(BF16)
        wcst_fc = [membBF[:, 0:1024], membBF[:, 1024:2048]]
        wcst_pr = [spkbBF[:, 0:1024], spkbBF[:, 1024:2048]]
        psT3 = [p3.enter_context(nc.psum_tensor(f"ps_t3{i}", [128, 512], BF16))
                for i in range(2)]
        psH = [p3.enter_context(nc.psum_tensor(f"ps_h{i}", [128, 512], F32))
               for i in range(2)]
        psY = [p3.enter_context(nc.psum_tensor(f"ps_y{i}", [128, D], F32))
               for i in range(2)]

        b = {k: S[k].n for k in S}
        ms = {}
        NG4 = 4

        def wpr_src(p):
            return wpr_d[(p // 4) * 128:(p // 4 + 1) * 128,
                         (p % 4) * 1024:(p % 4 + 1) * 1024]

        def wdst4(w, k2):
            if w < 32:
                return wfcT[:].rearrange("p (k s) -> p k s", k=DC, s=4096)[
                    :, k2:k2 + 4, w * 128:(w + 1) * 128]
            i, jc = (w - 32) // 4, (w - 32) % 4
            return wprT[:].rearrange("p (j s) -> p j s", j=32, s=D)[
                :, jc * 8 + k2: jc * 8 + k2 + 4, i * 128:(i + 1) * 128]

        def lnw_load(eng, g4):
            for k2 in range(DC):
                DS["dlnw"].issue(g4, eng.dma_start(
                    out=ln2win[g4 % 2][:, k2 * 512:(k2 + 1) * 512],
                    in_=ln2_hbm[:, k2 * TQ + g4 * 512:
                                k2 * TQ + (g4 + 1) * 512]))
            ms[("lnw", g4)] = DS["dlnw"].mark(g4)

        def p3_sync(eng):
            for g4 in range(2):
                lnw_load(eng, g4)
            for j in range(32):
                if j >= 2:
                    eng.wait_ge(S["wcast"].h, b["wcast"] + 2 * (j - 2) + 1)
                DS["dw3"].issue(j, eng.dma_start(
                    out=resstg[j % 2][:],
                    in_=wfc_d[j * 128:(j + 1) * 128, :]))
                ms[("w3", j)] = DS["dw3"].mark(j)
            eng.wait_ge(S["wcast"].h, b["wcast"] + 64)
            for g in range(NM):
                if g >= 2:
                    eng.wait_ge(S["snn"].h, b["snn"] + 2 * (g - 1))
                DS["dres"].issue(g, eng.dma_start(
                    out=resstg[g % 2][:],
                    in_=resid_hbm[g * 128:(g + 1) * 128, :]))
                ms[("res", g)] = DS["dres"].mark(g)
                if g >= 1:
                    eng.wait_ge(S["snn"].h, b["snn"] + 2 * g)
                    DS["dout"].issue(g - 1, eng.dma_start(
                        out=out_d[(g - 1) * 128:g * 128, :],
                        in_=outsb[(g - 1) % 2][:]))
                    ms[("out", g - 1)] = DS["dout"].mark(g - 1)
                if g % 4 == 2 and g // 4 + 2 < NG4:
                    gw = g // 4 + 2
                    eng.wait_ge(S["fcmm"].h, b["fcmm"] + 32 * (gw - 1))
                    lnw_load(eng, gw)
            eng.wait_ge(S["snn"].h, b["snn"] + 2 * NM)
            DS["dout"].issue(NM - 1, eng.dma_start(
                out=out_d[(NM - 1) * 128:NM * 128, :],
                in_=outsb[(NM - 1) % 2][:]))

        def trans8(eng, q):
            eng.wait_ge(S["wcast"].h, b["wcast"] + q + 1)
            src = wcst_fc[(q // 2) % 2] if q % 2 == 0 else wcst_pr[(q // 2) % 2]
            for k2 in range(DC):
                grp, sub = 2 * q + k2 // 4, k2 % 4
                if grp >= 2 and sub == 0:
                    eng.wait_ge(S["tpe"].h, b["tpe"] + grp - 1)
                S["tp"].inc(eng.transpose(
                    psT3[grp % 2][:, sub * 128:(sub + 1) * 128],
                    src[:, k2 * 128:(k2 + 1) * 128], idm[:]))

        def p3_tensor(eng):
            for g4 in range(NG4):
                for j in range(32):
                    jj = 32 * g4 + j
                    if g4 == 0:
                        trans8(eng, 2 * j)
                        eng.wait_ge(S["tpe"].h, b["tpe"] + 4 * j + 2)
                    if j == 0:
                        DS["dlnw"].wait(eng, ms[("lnw", g4)])
                    if jj >= 2:
                        eng.wait_ge(S["relu"].h, b["relu"] + jj - 1)
                    for k2 in range(DC):
                        mmi = eng.matmul(
                            psH[jj % 2][:],
                            wfcT[:, k2 * 4096 + j * 128: k2 * 4096 + (j + 1) * 128],
                            ln2win[g4 % 2][:, k2 * 512:(k2 + 1) * 512],
                            start=(k2 == 0), stop=(k2 == DC - 1))
                    S["fcmm"].inc(mmi)
                    if g4 == 0:
                        trans8(eng, 2 * j + 1)
                for sub in range(4):
                    g = 4 * g4 + sub
                    if g == 0:
                        eng.wait_ge(S["tpe"].h, b["tpe"] + 128)
                    if g >= 2:
                        eng.wait_ge(S["snn"].h, b["snn"] + 2 * (g - 1))
                    for j in range(32):
                        if sub == 0:
                            eng.wait_ge(S["h"].h, b["h"] + 32 * g4 + j + 1)
                        for half in range(2):
                            mmi = eng.matmul(
                                psY[sub % 2][:, half * 512:(half + 1) * 512],
                                hfm[:, j * 512 + sub * 128:
                                    j * 512 + sub * 128 + 128],
                                wprT[:, j * D + half * 512:
                                     j * D + half * 512 + 512],
                                start=(j == 0), stop=(j == 31))
                    S["prmm"].inc(mmi)

        def p3_scalar(eng):
            for p in range(2):
                DS["dwp"].issue(p, eng.dma_start(out=outsb[p % 2][:],
                                                 in_=wpr_src(p)))
                ms[("wp", p)] = DS["dwp"].mark(p)
            for g4 in range(NG4):
                for j in range(32):
                    jj = 32 * g4 + j
                    if g4 == 0 and j < 30:
                        p = j + 2
                        eng.wait_ge(S["wcast"].h, b["wcast"] + 2 * j + 2)
                        DS["dwp"].issue(p, eng.dma_start(out=outsb[p % 2][:],
                                                         in_=wpr_src(p)))
                        ms[("wp", p)] = DS["dwp"].mark(p)
                    def sq(s):
                        if s % 32 == 0 and s >= 32:
                            eng.wait_ge(S["prmm"].h, b["prmm"] + 4 * (s // 32))
                        S["h"].inc(eng.activation(
                            hfm[:, (s % 32) * 512:(s % 32 + 1) * 512],
                            relutmp[s % 2][:], AF.Square))

                    if j == 0 and jj > 0:
                        sq(jj - 1)
                    eng.wait_ge(S["fcmm"].h, b["fcmm"] + jj + 1)
                    S["relu"].inc(eng.activation(
                        relutmp[jj % 2][:], psH[jj % 2][:], AF.Relu))
                    if jj >= 1 and j != 0:
                        sq(jj - 1)
            sq(127)

        def p3_vector(eng):
            for j in range(33):
                if j >= 1:
                    for q in (2 * (j - 1), 2 * (j - 1) + 1):
                        w = q // 2 if q % 2 == 0 else 32 + q // 2
                        for halfg in range(2):
                            grp = 2 * q + halfg
                            eng.wait_ge(S["tp"].h, b["tp"] + 4 * (grp + 1))
                            S["tpe"].inc(eng.tensor_copy(
                                wdst4(w, halfg * 4),
                                psT3[grp % 2][:].rearrange("p (j s) -> p j s",
                                                           j=4, s=128)))
                if j < 32:
                    DS["dw3"].wait(eng, ms[("w3", j)])
                    if j >= 2:
                        eng.wait_ge(S["tp"].h, b["tp"] + 8 * (2 * (j - 2)) + 8)
                    S["wcast"].inc(eng.tensor_copy(wcst_fc[j % 2],
                                                   resstg[j % 2][:]))
                    DS["dwp"].wait(eng, ms[("wp", j)])
                    if j >= 2:
                        eng.wait_ge(S["tp"].h,
                                    b["tp"] + 8 * (2 * (j - 2) + 1) + 8)
                    S["wcast"].inc(eng.tensor_copy(wcst_pr[j % 2],
                                                   outsb[j % 2][:]))
            for g4 in range(NG4):
                for sub in range(4):
                    g = 4 * g4 + sub
                    eng.wait_ge(S["prmm"].h, b["prmm"] + g + 1)
                    DS["dres"].wait(eng, ms[("res", g)])
                    if g >= 2:
                        DS["dout"].wait(eng, ms[("out", g - 2)])
                    y2 = [psY[sub % 2][:, h * 512:(h + 1) * 512] for h in range(2)]
                    mem2 = [memb[:, 0:512], spkb[:, 0:512]]
                    spk2 = [memb[:, 512:1024], spkb[:, 512:1024]]
                    res2 = [resstg[g % 2][:, 0:512], resstg[g % 2][:, 512:1024]]
                    acc2 = [outsb[g % 2][:, 0:512], outsb[g % 2][:, 512:1024]]

                    def step(ops, inc=None):
                        for h in range(2):
                            inst = ops(h)
                            if inc is not None:
                                inc(inst)

                    step(lambda h: eng.tensor_copy(mem2[h], y2[h]))
                    step(lambda h: eng.tensor_single_scalar(spk2[h], mem2[h],
                                                            1.0, ALU.is_gt))
                    step(lambda h: eng.scalar_tensor_tensor(
                        acc2[h], spk2[h], 0.2, res2[h], ALU.mult, ALU.add))
                    for _s in range(4):
                        step(lambda h: eng.scalar_tensor_tensor(
                            mem2[h], mem2[h], 0.9, spk2[h],
                            ALU.mult, ALU.subtract))
                        step(lambda h: eng.tensor_add(mem2[h], mem2[h], y2[h]))
                        step(lambda h: eng.tensor_single_scalar(
                            spk2[h], mem2[h], 1.0, ALU.is_gt))
                        if _s < 3:
                            step(lambda h: eng.scalar_tensor_tensor(
                                acc2[h], spk2[h], 0.2, acc2[h],
                                ALU.mult, ALU.add))
                        else:
                            step(lambda h: eng.scalar_tensor_tensor(
                                acc2[h], spk2[h], 0.2, acc2[h],
                                ALU.mult, ALU.add),
                                inc=lambda inst: S["snn"].inc(inst))

        if 3 not in phases:
            p3.close()
        else:
            phase("p3", {
                "sync": (p3_sync, [DS["dw3"], DS["dres"], DS["dout"],
                         DS["dlnw"]]),
                "tensor": (p3_tensor, None),
                "scalar": (p3_scalar, [DS["dwp"]]),
                "vector": (p3_vector, None),
            })
            p3.close()

    return nc


_NC_CACHE = []


def kernel(L0_attn_x, L0_attn_Wk, L0_attn_Wq, L0_attn_Wo, L0_attn_bo,
           L0_ffn_W_fc, L0_ffn_W_proj):
    x = np.ascontiguousarray(np.asarray(L0_attn_x, dtype=np.float32))
    wk = np.ascontiguousarray(np.asarray(L0_attn_Wk, dtype=np.float32))
    wq = np.ascontiguousarray(np.asarray(L0_attn_Wq, dtype=np.float32))
    wo = np.ascontiguousarray(np.asarray(L0_attn_Wo, dtype=np.float32))
    bo = np.ascontiguousarray(np.asarray(L0_attn_bo, dtype=np.float32))
    wfc = np.ascontiguousarray(np.asarray(L0_ffn_W_fc, dtype=np.float32))
    wpr = np.ascontiguousarray(np.asarray(L0_ffn_W_proj, dtype=np.float32))
    ident = np.eye(128, dtype=ml_dtypes.bfloat16)

    if not _NC_CACHE:
        _NC_CACHE.append(build_nc())
    nc = _NC_CACHE[0]

    in_maps = []
    for c in range(8):
        bidx, qi = c // 4, c % 4
        in_maps.append({
            "xfull": x[bidx],
            "xmy": x[bidx, qi * TQ:(qi + 1) * TQ],
            "wq": wq, "wk": wk, "wo": wo, "bo": bo,
            "wfc": wfc, "wproj": wpr, "ident": ident,
        })
    res = run_bass_kernel_spmd(nc, in_maps, core_ids=list(range(8)))
    out = np.empty((2, TB, D), dtype=np.float32)
    for c in range(8):
        bidx, qi = c // 4, c % 4
        out[bidx, qi * TQ:(qi + 1) * TQ] = np.asarray(res.results[c]["out"])
    return out


# version 15
# speedup vs baseline: 1.0489x; 1.0148x over previous
import numpy as np
import ml_dtypes
from contextlib import ExitStack

import concourse.bass as bass
import concourse.mybir as mybir
from concourse.bass_utils import run_bass_kernel_spmd

F32 = mybir.dt.float32
BF16 = mybir.dt.bfloat16
AF = mybir.ActivationFunctionType
ALU = mybir.AluOpType
AXX = mybir.AxisListType.X

TB = 8192
TQ = 2048
D = 1024
DC = 8
NP = 8
NTK = 64
NC4 = 4
NM = 16
EPS = 1e-5


class Sem:
    def __init__(self, ctx, nc, name):
        self.h = ctx.enter_context(nc.semaphore(name))
        self.n = 0

    def inc(self, inst, k=1):
        inst.then_inc(self.h, k)
        self.n += k
        return inst


class DStream:

    def __init__(self, ctx, nc, name, nslots):
        self.sems = [Sem(ctx, nc, f"{name}{j}") for j in range(nslots)]
        self.ns = nslots

    def issue(self, slot, inst):
        self.sems[slot % self.ns].inc(inst, 16)

    def mark(self, slot):
        s = self.sems[slot % self.ns]
        return (slot % self.ns, s.n)

    def wait(self, eng, mark):
        j, v = mark
        eng.wait_ge(self.sems[j].h, v)

    def drain(self, eng):
        for s in self.sems:
            eng.wait_ge(s.h, s.n)


def build_nc(phases=(0, 1, 2, 3), p1_np=NP, p1_attn=True, p1_parts="kvq",
             dbg_scratch=False):
    nc = bass.Bass("TRN2", target_bir_lowering=False,
                   detect_race_conditions=False)

    xfull_d = nc.dram_tensor("xfull", [TB, D], F32, kind="ExternalInput").ap()
    xmy_d = nc.dram_tensor("xmy", [TQ, D], F32, kind="ExternalInput").ap()
    wq_d = nc.dram_tensor("wq", [D, D], F32, kind="ExternalInput").ap()
    wk_d = nc.dram_tensor("wk", [D, D], F32, kind="ExternalInput").ap()
    wo_d = nc.dram_tensor("wo", [D, D], F32, kind="ExternalInput").ap()
    bo_d = nc.dram_tensor("bo", [D], F32, kind="ExternalInput").ap()
    wfc_d = nc.dram_tensor("wfc", [4 * D, D], F32, kind="ExternalInput").ap()
    wpr_d = nc.dram_tensor("wproj", [D, 4 * D], F32, kind="ExternalInput").ap()
    idm_d = nc.dram_tensor("ident", [128, 128], BF16, kind="ExternalInput").ap()
    out_d = nc.dram_tensor("out", [TQ, D], F32, kind="ExternalOutput").ap()

    skind = "ExternalOutput" if dbg_scratch else "Internal"
    lnx_hbm = nc.dram_tensor("lnx_scr", [128, DC * TB], BF16, kind=skind).ap()
    lnmy_hbm = nc.dram_tensor("lnmy_scr", [128, DC * TQ], BF16, kind=skind).ap()
    attn_hbm = nc.dram_tensor("attn_scr", [128, NP * TQ], BF16, kind=skind).ap()
    lsum_hbm = nc.dram_tensor("lsum_scr", [64, 512], F32, kind=skind).ap()
    rsum_hbm = nc.dram_tensor("rsum_scr", [64, 512], F32, kind=skind).ap()
    resid_hbm = nc.dram_tensor("resid_scr", [TQ, D], F32, kind=skind).ap()
    ln2_hbm = nc.dram_tensor("ln2_scr", [128, DC * TQ], BF16, kind=skind).ap()

    ctx = ExitStack()
    with ctx:
        idm = ctx.enter_context(nc.sbuf_tensor("sb_idm", [128, 128], BF16))

        S = {}
        for name in ("ph sq var std rstd norm tp tpe wcast "
                     "kproj kevac vt vte qproj qevac sc exp avd ave div "
                     "womm resid rs fcmm relu h prmm snn vch").split():
            S[name] = Sem(ctx, nc, "sem_" + name)
        DS = {}
        for name, nsl in (("dx", 16), ("dlns", 2), ("dlnw", 2), ("dwl", 1),
                          ("dao", 1), ("dwo", 2), ("dbo", 1), ("dlsum", 1),
                          ("dxmy", 2), ("dattn", 2), ("dbc", 1), ("drs", 1),
                          ("dspill", 4), ("dw3", 2), ("dres", 2), ("dout", 2),
                          ("dwp", 2)):
            DS[name] = DStream(ctx, nc, name, nsl)

        phase_k = [0]

        def vi(inst):
            S["vch"].inc(inst)
            return inst

        def vw(eng):
            eng.wait_ge(S["vch"].h, S["vch"].n)

        def phase(block_name, bodies):
            k = phase_k[0]
            with nc.Block(block_name) as blk:
                for engname in ("sync", "tensor", "scalar", "vector", "gpsimd"):
                    body, drain = bodies.get(engname, (None, None))

                    def run(eng, body=body, drain=drain, k=k):
                        if k > 0:
                            eng.wait_ge(S["ph"].h, 5 * k)
                        if body is not None:
                            body(eng)
                        for d in (drain or []):
                            if isinstance(d, Sem):
                                eng.wait_ge(d.h, d.n)
                            else:
                                d.drain(eng)
                        S["ph"].inc(eng.nop())

                    getattr(blk, engname)(run)
            phase_k[0] += 1

        p0 = ExitStack()
        NXS = 28
        xstage = [p0.enter_context(nc.sbuf_tensor(f"sb_xs{i}", [128, D], F32))
                  for i in range(NXS)]
        lntm = [p0.enter_context(nc.sbuf_tensor(f"sb_lntm{i}", [128, D], BF16))
                for i in range(4)]
        xsq = p0.enter_context(nc.sbuf_tensor("sb_xsq", [128, D], F32))
        sxb = p0.enter_context(nc.sbuf_tensor("sb_sx", [128, 16], F32))
        sxqb = p0.enter_context(nc.sbuf_tensor("sb_sxq", [128, 16], F32))
        meanb = p0.enter_context(nc.sbuf_tensor("sb_mean", [128, 16], F32))
        ex2b = p0.enter_context(nc.sbuf_tensor("sb_ex2", [128, 16], F32))
        m2b = p0.enter_context(nc.sbuf_tensor("sb_m2", [128, 16], F32))
        varb = p0.enter_context(nc.sbuf_tensor("sb_var", [128, 16], F32))
        stdb = p0.enter_context(nc.sbuf_tensor("sb_std", [128, 16], F32))
        rstdb = p0.enter_context(nc.sbuf_tensor("sb_rstd", [128, 16], F32))
        negmb = p0.enter_context(nc.sbuf_tensor("sb_negm", [128, 16], F32))
        lnstg = [p0.enter_context(nc.sbuf_tensor(f"sb_lnst{i}", [128, 1024], BF16))
                 for i in range(2)]
        psT0 = [p0.enter_context(nc.psum_tensor(f"ps_t0{i}", [128, 512], BF16))
                for i in range(4)]

        NT = 80
        NG = 20

        def src_rows(i):
            return xfull_d[i * 128:(i + 1) * 128, :] if i < 64 else \
                xmy_d[(i - 64) * 128:(i - 63) * 128, :]

        def ln_dst(i):
            if i < 64:
                return lnx_hbm[:].rearrange("p (k s) -> p k s", k=DC, s=TB)[
                    :, :, i * 128:(i + 1) * 128]
            im = i - 64
            return lnmy_hbm[:].rearrange("p (k s) -> p k s", k=DC, s=TQ)[
                :, :, im * 128:(im + 1) * 128]

        def lncol(g, j):
            return (g % 4) * 4 + j

        def xmark(i):
            return (i % 16, 16 * (i // 16 + 1))

        def lnsmark(i):
            return (i % 2, 16 * (i // 2 + 1))

        def p0_sync(eng):
            DS["dwl"].issue(0, eng.dma_start(out=idm[:], in_=idm_d))
            for i in range(NT):
                if i % 4 == 3:
                    continue
                if i >= NXS:
                    eng.wait_ge(S["norm"].h, i - NXS + 1)
                DS["dx"].issue(i, eng.dma_start(out=xstage[i % NXS][:],
                                                in_=src_rows(i)))

        def p0_tensor(eng):
            DS["dwl"].wait(eng, (0, 16))
            for i in range(NT):
                eng.wait_ge(S["norm"].h, i + 1)
                for k2 in range(DC):
                    t = 8 * i + k2
                    grp, sub = t // 4, t % 4
                    if grp >= 4 and sub == 0:
                        eng.wait_ge(S["tpe"].h, grp - 3)
                    S["tp"].inc(eng.transpose(
                        psT0[grp % 4][:, sub * 128:(sub + 1) * 128],
                        lntm[i % 4][:, k2 * 128:(k2 + 1) * 128],
                        idm[:]))

        def p0_scalar(eng):
            DS["dx"].issue(3, eng.dma_start(out=xstage[3][:], in_=src_rows(3)))
            for g in range(NG + 5):
                i4 = 4 * g + 7
                if i4 < NT:
                    DS["dx"].issue(i4, eng.dma_start(
                        out=xstage[i4 % NXS][:], in_=src_rows(i4)))
                if 0 <= g - 5 < NG:
                    for j in range(4):
                        i = 4 * (g - 5) + j
                        eng.wait_ge(S["tpe"].h, 2 * (i + 1))
                        DS["dlns"].issue(i, eng.dma_start(
                            out=ln_dst(i),
                            in_=lnstg[i % 2][:].rearrange("p (k s) -> p k s",
                                                          k=DC, s=128)))
                if 0 <= g - 4 < NG:
                    gg = g - 4
                    eng.wait_ge(S["rstd"].h, gg + 1)
                    for j in range(4):
                        i = 4 * gg + j
                        if i >= 4:
                            eng.wait_ge(S["tp"].h, 8 * (i - 3))
                        c = lncol(gg, j)
                        S["norm"].inc(eng.activation(
                            lntm[i % 4][:], xstage[i % NXS][:], AF.Identity,
                            scale=rstdb[:, c:c + 1], bias=negmb[:, c:c + 1]))
                if 0 <= g - 2 < NG:
                    gg = g - 2
                    eng.wait_ge(S["var"].h, gg + 1)
                    if gg >= 4:
                        eng.wait_ge(S["rstd"].h, gg - 3)
                    c0 = (gg % 4) * 4
                    S["std"].inc(eng.activation(
                        stdb[:, c0:c0 + 4], varb[:, c0:c0 + 4], AF.Sqrt))
                if g < NG:
                    if g >= 4:
                        eng.wait_ge(S["var"].h, g - 3)
                    for j in range(4):
                        i = 4 * g + j
                        DS["dx"].wait(eng, xmark(i))
                        S["sq"].inc(eng.activation(
                            xsq[:], xstage[i % NXS][:], AF.Square,
                            accum_out=sxqb[:, lncol(g, j):lncol(g, j) + 1]))

        def p0_vector(eng):
            for g in range(NG + 5):
                if g < NG:
                    for j in range(4):
                        i = 4 * g + j
                        DS["dx"].wait(eng, xmark(i))
                        eng.reduce_sum(sxb[:, lncol(g, j):lncol(g, j) + 1],
                                       xstage[i % NXS][:], axis=AXX)
                ga = g - 1 if 0 <= g - 1 < NG else None
                gb = g - 3 if 0 <= g - 3 < NG else None
                if ga is not None:
                    eng.wait_ge(S["sq"].h, 4 * (ga + 1))
                    if ga >= 4:
                        eng.wait_ge(S["std"].h, ga - 3)
                if gb is not None:
                    eng.wait_ge(S["std"].h, gb + 1)
                    if gb >= 4:
                        eng.wait_ge(S["norm"].h, 4 * (gb - 3))
                cb = (gb % 4) * 4 if gb is not None else 0
                if ga is not None:
                    ca = (ga % 4) * 4

                    def a_ex2():
                        eng.tensor_scalar_mul(ex2b[:, ca:ca + 4],
                                              sxqb[:, ca:ca + 4], 1.0 / D)
                    eng.tensor_scalar_mul(meanb[:, ca:ca + 4],
                                          sxb[:, ca:ca + 4], 1.0 / D)
                    a_ex2()
                    eng.tensor_mul(m2b[:, ca:ca + 4], meanb[:, ca:ca + 4],
                                   meanb[:, ca:ca + 4])
                    if gb is not None:
                        eng.reciprocal(rstdb[:, cb:cb + 4], stdb[:, cb:cb + 4])
                    else:
                        a_ex2()
                    eng.tensor_sub(varb[:, ca:ca + 4], ex2b[:, ca:ca + 4],
                                   m2b[:, ca:ca + 4])
                    if gb is not None:
                        S["rstd"].inc(eng.scalar_tensor_tensor(
                            negmb[:, cb:cb + 4], meanb[:, cb:cb + 4], -1.0,
                            rstdb[:, cb:cb + 4], ALU.mult, ALU.mult))
                    else:
                        a_ex2()
                    S["var"].inc(eng.tensor_scalar_add(
                        varb[:, ca:ca + 4], varb[:, ca:ca + 4], EPS))
                elif gb is not None:
                    eng.reciprocal(rstdb[:, cb:cb + 4], stdb[:, cb:cb + 4])
                    eng.reciprocal(rstdb[:, cb:cb + 4], stdb[:, cb:cb + 4])
                    S["rstd"].inc(eng.scalar_tensor_tensor(
                        negmb[:, cb:cb + 4], meanb[:, cb:cb + 4], -1.0,
                        rstdb[:, cb:cb + 4], ALU.mult, ALU.mult))
                if 0 <= g - 5 < NG:
                    for j in range(4):
                        i = 4 * (g - 5) + j
                        for half in range(2):
                            grp = 2 * i + half
                            eng.wait_ge(S["tp"].h, 4 * (grp + 1))
                            if i >= 2 and half == 0:
                                DS["dlns"].wait(eng, lnsmark(i - 2))
                            S["tpe"].inc(eng.tensor_copy(
                                lnstg[i % 2][:, half * 512:(half + 1) * 512],
                                psT0[grp % 4][:]))


        if 0 not in phases:
            p0.close()
        else:
            phase("p0", {
                "sync": (p0_sync, [DS["dx"], DS["dwl"]]),
                "tensor": (p0_tensor, None),
                "scalar": (p0_scalar, [DS["dlns"]]),
                "vector": (p0_vector, None),
            })
            p0.close()

        p1 = ExitStack()
        wstage = p1.enter_context(nc.sbuf_tensor("sb_wst", [128, D], F32))
        wqk_bf = p1.enter_context(nc.sbuf_tensor("sb_wqk", [128, 2048], BF16))
        lnxwin = [p1.enter_context(nc.sbuf_tensor(f"sb_lnw{i}", [128, 4096], BF16))
                  for i in range(2)]
        kfm2 = [p1.enter_context(nc.sbuf_tensor(f"sb_kfm{i}", [128, TB], BF16))
                for i in range(2)]
        vtm2 = [p1.enter_context(nc.sbuf_tensor(f"sb_vtm{i}", [128, NTK * 130], BF16))
                for i in range(2)]
        qfm2 = [p1.enter_context(nc.sbuf_tensor(f"sb_qfm{i}", [128, TQ], BF16))
                for i in range(2)]
        pt = [p1.enter_context(nc.sbuf_tensor(f"sb_pt{i}", [128, 1024], BF16))
              for i in range(4)]
        h1st = [p1.enter_context(nc.sbuf_tensor(f"sb_h1st{i}", [64, 512], BF16))
                for i in range(2)]
        h2st = [p1.enter_context(nc.sbuf_tensor(f"sb_h2st{i}", [64, 512], BF16))
                for i in range(2)]
        lrst = p1.enter_context(nc.sbuf_tensor("sb_lrst", [128, 1024], F32))
        psS = [p1.enter_context(nc.psum_tensor(f"ps_s{i}", [128, 1024], F32))
               for i in range(2)]
        psAV = [p1.enter_context(nc.psum_tensor(f"ps_av{i}", [128, 512], F32))
                for i in range(2)]
        psP = [p1.enter_context(nc.psum_tensor(f"ps_p{i}", [128, 512], F32))
               for i in range(2)]
        psPb = [psP[i][:, 0:256].bitcast(BF16) for i in range(2)]

        vtm4 = [vtm2[i][:].rearrange("p (t j s) -> p t j s", t=NTK, j=2, s=65)
                for i in range(2)]
        b = {k: S[k].n for k in S}
        bao = DS["dao"].sems[0].n
        ms = {}

        def p1_sync(eng):
            for p in range(NP):
                if p > 0:
                    eng.wait_ge(S["wcast"].h, b["wcast"] + 2 * p)
                for k2 in range(DC):
                    DS["dwl"].issue(0, eng.dma_start(
                        out=wstage[:, k2 * 128:(k2 + 1) * 128],
                        in_=wk_d[k2 * 128:(k2 + 1) * 128, p * 128:(p + 1) * 128]))
                ms[("wk", p)] = DS["dwl"].mark(0)
                for t16 in range(16):
                    if t16 == 0 and p > 0:
                        eng.wait_ge(S["qproj"].h, b["qproj"] + 4 * p)
                    if 16 * p + t16 - 1 > 0:
                        eng.wait_ge(S["kproj"].h, b["kproj"] + 16 * p + t16 - 1)
                    for k2 in range(DC):
                        DS["dlnw"].issue(t16, eng.dma_start(
                            out=lnxwin[t16 % 2][:, k2 * 512:(k2 + 1) * 512],
                            in_=lnx_hbm[:, k2 * TB + t16 * 512:
                                        k2 * TB + (t16 + 1) * 512]))
                    ms[("kw", p, t16)] = DS["dlnw"].mark(t16)
                eng.wait_ge(S["wcast"].h, b["wcast"] + 2 * p + 1)
                for k2 in range(DC):
                    DS["dwl"].issue(0, eng.dma_start(
                        out=wstage[:, k2 * 128:(k2 + 1) * 128],
                        in_=wq_d[k2 * 128:(k2 + 1) * 128, p * 128:(p + 1) * 128]))
                ms[("wq", p)] = DS["dwl"].mark(0)
                for c in range(NC4):
                    if c == 0:
                        eng.wait_ge(S["kproj"].h, b["kproj"] + 16 * (p + 1))
                    if c >= 2:
                        eng.wait_ge(S["qproj"].h, b["qproj"] + 4 * p + c - 1)
                    for k2 in range(DC):
                        DS["dlnw"].issue(c, eng.dma_start(
                            out=lnxwin[c % 2][:, k2 * 512:(k2 + 1) * 512],
                            in_=lnmy_hbm[:, k2 * TQ + c * 512:
                                         k2 * TQ + (c + 1) * 512]))
                    ms[("qw", p, c)] = DS["dlnw"].mark(c)

        def prep_jobs(eng, q):
            def kjob(t16):
                def f():
                    if t16 == 0:
                        eng.wait_ge(S["wcast"].h, b["wcast"] + 2 * q + 1)
                    DS["dlnw"].wait(eng, ms[("kw", q, t16)])
                    if 16 * q + t16 - 1 > 0:
                        eng.wait_ge(S["kevac"].h, b["kevac"] + 16 * q + t16 - 1)
                    for k2 in range(DC):
                        mmi = eng.matmul(
                            psP[t16 % 2][:],
                            wqk_bf[:, k2 * 128:(k2 + 1) * 128],
                            lnxwin[t16 % 2][:, k2 * 512:(k2 + 1) * 512],
                            start=(k2 == 0), stop=(k2 == DC - 1))
                    S["kproj"].inc(mmi)
                return f

            def vjob(grp):
                def f():
                    eng.wait_ge(S["kevac"].h, b["kevac"] + 16 * (q + 1))
                    if grp >= 2:
                        eng.wait_ge(S["vte"].h, b["vte"] + 16 * q + grp - 1)
                    for sub in range(4):
                        t = 4 * grp + sub
                        S["vt"].inc(eng.transpose(
                            psPb[grp % 2][:, sub * 128:(sub + 1) * 128],
                            kfm2[q % 2][:, t * 128:(t + 1) * 128], idm[:]))
                return f

            def qjob(c):
                def f():
                    if c == 0:
                        eng.wait_ge(S["wcast"].h, b["wcast"] + 2 * q + 2)
                        eng.wait_ge(S["vte"].h, b["vte"] + 16 * (q + 1))
                    DS["dlnw"].wait(eng, ms[("qw", q, c)])
                    if c >= 2:
                        eng.wait_ge(S["qevac"].h, b["qevac"] + 4 * q + c - 1)
                    for k2 in range(DC):
                        mmi = eng.matmul(
                            psP[c % 2][:],
                            wqk_bf[:, 1024 + k2 * 128: 1024 + (k2 + 1) * 128],
                            lnxwin[c % 2][:, k2 * 512:(k2 + 1) * 512],
                            start=(k2 == 0), stop=(k2 == DC - 1))
                    S["qproj"].inc(mmi)
                return f

            jobs = {}
            for t16 in range(16):
                jobs[7 * t16 + 3] = kjob(t16)
            for grp in range(8):
                jobs[130 + 4 * grp] = vjob(grp)
            for grp in range(8, 16):
                jobs[196 + 4 * (grp - 8)] = vjob(grp)
            for c in range(NC4):
                jobs[229 + 7 * c] = qjob(c)
            return jobs

        def p1_tensor(eng):
            jobs0 = prep_jobs(eng, 0)
            for slot in sorted(jobs0):
                jobs0[slot]()
            for q in range(NP):
                jobs = prep_jobs(eng, q + 1) if q + 1 < NP else {}
                kfm = kfm2[q % 2]
                qfm = qfm2[q % 2]
                eng.wait_ge(S["vte"].h, b["vte"] + 16 * (q + 1))
                eng.wait_ge(S["qevac"].h, b["qevac"] + 4 * (q + 1))
                for c in range(NC4):
                    base = 256 * q + 64 * c

                    def scores(t, c=c, base=base):
                        if base + t - 1 > 0:
                            eng.wait_ge(S["exp"].h, b["exp"] + base + t - 1)
                        eng.matmul(psS[t % 2][:, 0:512],
                                   kfm[0:64, t * 128:(t + 1) * 128],
                                   qfm[0:64, c * 512:(c + 1) * 512],
                                   start=True, stop=True)
                        S["sc"].inc(eng.matmul(
                            psS[t % 2][:, 512:1024],
                            kfm[64:128, t * 128:(t + 1) * 128],
                            qfm[64:128, c * 512:(c + 1) * 512],
                            start=True, stop=True))

                    def av(t, c=c, base=base):
                        eng.wait_ge(S["exp"].h, b["exp"] + base + t + 1)
                        if t == 0 and (q > 0 or c > 0):
                            eng.wait_ge(S["ave"].h, b["ave"] + 4 * q + c)
                        eng.matmul(psAV[0][0:65, :],
                                   vtm4[q % 2][:, t, 0, 0:65],
                                   pt[t % 4][:, 0:512],
                                   start=(t == 0), stop=(t == NTK - 1))
                        S["avd"].inc(eng.matmul(
                            psAV[1][0:65, :],
                            vtm4[q % 2][:, t, 1, 0:65],
                            pt[t % 4][:, 512:1024],
                            start=(t == 0), stop=(t == NTK - 1)))

                    for t in range(NTK):
                        slot = 64 * c + t
                        scores(t)
                        if t >= 1:
                            av(t - 1)
                        if slot in jobs:
                            jobs.pop(slot)()
                    av(NTK - 1)
                for slot in sorted(jobs):
                    jobs.pop(slot)()

        def ao_store(eng, idx):
            col = idx * 512
            pp, cc = idx // 4, idx % 4
            row = pp * 8 + cc * 2
            DS["dao"].issue(0, eng.dma_start(
                out=attn_hbm[0:64, col:col + 512], in_=h1st[idx % 2][:]))
            DS["dao"].issue(0, eng.dma_start(
                out=attn_hbm[64:128, col:col + 512], in_=h2st[idx % 2][:]))
            DS["dao"].issue(0, eng.dma_start(
                out=lsum_hbm[row:row + 1, :], in_=lrst[64:65, 0:512]))
            DS["dao"].issue(0, eng.dma_start(
                out=lsum_hbm[row + 1:row + 2, :], in_=lrst[64:65, 512:1024]))

        def p1_scalar_full(eng):
            for p in range(NP):
                for c in range(NC4):
                    idx = 4 * p + c
                    for t in range(NTK):
                        base = 256 * p + 64 * c
                        eng.wait_ge(S["sc"].h, b["sc"] + base + t + 1)
                        if base + t - 3 > 0:
                            eng.wait_ge(S["avd"].h, b["avd"] + base + t - 3)
                        S["exp"].inc(eng.activation(
                            pt[t % 4][:], psS[t % 2][:], AF.Exp, scale=1.0 / 32.0))
                    eng.wait_ge(S["avd"].h, b["avd"] + 256 * p + 64 * (c + 1))
                    if idx >= 1:
                        DS["dao"].wait(eng, (0, bao + 64 * idx))
                    eng.activation(h1st[idx % 2][:], psAV[0][0:64, :],
                                   AF.Identity)
                    eng.activation(h2st[idx % 2][:], psAV[1][0:64, :],
                                   AF.Identity)
                    eng.activation(lrst[64:65, 0:512], psAV[0][64:65, :],
                                   AF.Identity)
                    S["ave"].inc(eng.activation(
                        lrst[64:65, 512:1024], psAV[1][64:65, :], AF.Identity))
                    ao_store(eng, idx)

        def p1_vector_prep(eng, q, section):
            if section == 0:
                DS["dwl"].wait(eng, ms[("wk", q)])
                if q > 0:
                    eng.wait_ge(S["kproj"].h, b["kproj"] + 16 * q)
                S["wcast"].inc(eng.tensor_copy(wqk_bf[:, 0:1024], wstage[:]))
            elif section in (1, 2):
                for j in range(8 * (section - 1), 8 * section):
                    eng.wait_ge(S["kproj"].h, b["kproj"] + 16 * q + j + 1)
                    S["kevac"].inc(eng.tensor_copy(
                        kfm2[q % 2][:, j * 512:(j + 1) * 512], psP[j % 2][:]))
            elif section in (3, 4):
                if section == 4:
                    DS["dwl"].wait(eng, ms[("wq", q)])
                    eng.wait_ge(S["qproj"].h, b["qproj"] + 4 * q)
                    S["wcast"].inc(eng.tensor_copy(wqk_bf[:, 1024:2048],
                                                   wstage[:]))
                for grp in range(8 * (section - 3), 8 * (section - 2)):
                    eng.wait_ge(S["vt"].h, b["vt"] + 64 * q + 4 * (grp + 1))
                    if grp == 0:
                        if q >= 1:
                            eng.wait_ge(S["avd"].h, b["avd"] + 256 * (q - 1))
                        eng.memset(vtm4[q % 2][:, :, :, 64:65], 1.0)
                    S["vte"].inc(eng.tensor_copy(
                        vtm4[q % 2][:, 4 * grp:4 * grp + 4, :, 0:64],
                        psPb[grp % 2][:].rearrange("p (t j s) -> p t j s",
                                                   t=4, j=2, s=64)))
            elif section == 5:
                for c in range(2):
                    eng.wait_ge(S["qproj"].h, b["qproj"] + 4 * q + c + 1)
                    S["qevac"].inc(eng.tensor_copy(
                        qfm2[q % 2][:, c * 512:(c + 1) * 512], psP[c % 2][:]))
            elif section == 6:
                for c in range(2, NC4):
                    eng.wait_ge(S["qproj"].h, b["qproj"] + 4 * q + c + 1)
                    S["qevac"].inc(eng.tensor_copy(
                        qfm2[q % 2][:, c * 512:(c + 1) * 512], psP[c % 2][:]))

        def p1_vector(eng):
            for sec in range(7):
                p1_vector_prep(eng, 0, sec)
            for p in range(NP):
                for c in range(NC4):
                    if p + 1 < NP:
                        if c == 0:
                            p1_vector_prep(eng, p + 1, 0)
                            p1_vector_prep(eng, p + 1, 1)
                        elif c == 1:
                            p1_vector_prep(eng, p + 1, 2)
                        elif c == 2:
                            p1_vector_prep(eng, p + 1, 3)
                        else:
                            p1_vector_prep(eng, p + 1, 4)
                            p1_vector_prep(eng, p + 1, 5)
                if p + 1 < NP:
                    p1_vector_prep(eng, p + 1, 6)

        if 1 not in phases:
            p1.close()
        else:
            phase("p1", {
                "sync": (p1_sync, None),
                "tensor": (p1_tensor, None),
                "scalar": (p1_scalar_full, [DS["dao"]]),
                "vector": (p1_vector, None),
            })
            p1.close()

        p2 = ExitStack()
        wostage = [p2.enter_context(nc.sbuf_tensor(f"sb_wos{i}", [128, D], F32))
                   for i in range(2)]
        wo_bf = p2.enter_context(nc.sbuf_tensor("sb_wobf", [128, DC * D], BF16))
        bo_rep = p2.enter_context(nc.sbuf_tensor("sb_borep", [128, D], F32))
        lsum_sb = p2.enter_context(nc.sbuf_tensor("sb_lsum", [64, 512], F32))
        rsum_sb = p2.enter_context(nc.sbuf_tensor("sb_rsum", [64, 512], F32))
        R128 = [p2.enter_context(nc.sbuf_tensor(f"sb_r128_{p}", [128, 512], F32))
                for p in range(NP)]
        attnstg = [p2.enter_context(nc.sbuf_tensor(f"sb_astg{i}", [128, D], BF16))
                   for i in range(2)]
        attndiv = [p2.enter_context(nc.sbuf_tensor(f"sb_adiv{i}", [128, D], BF16))
                   for i in range(2)]
        xmystg = [p2.enter_context(nc.sbuf_tensor(f"sb_xmys{i}", [128, D], F32))
                  for i in range(2)]
        residb = [p2.enter_context(nc.sbuf_tensor(f"sb_res{i}", [128, D], F32))
                  for i in range(4)]
        ln2tm = [p2.enter_context(nc.sbuf_tensor(f"sb_ln2tm{i}", [128, D], BF16))
                 for i in range(2)]
        xsq2 = p2.enter_context(nc.sbuf_tensor("sb_xsq2", [128, D], F32))
        st2 = p2.enter_context(nc.sbuf_tensor("sb_st2", [128, 16], F32))
        psWo = [p2.enter_context(nc.psum_tensor(f"ps_wo{i}", [128, D], F32))
                for i in range(2)]
        ln2stg = [p2.enter_context(nc.sbuf_tensor(f"sb_l2st{i}", [128, 1024],
                                                   BF16)) for i in range(2)]
        psT2 = [p2.enter_context(nc.psum_tensor(f"ps_t2{i}", [128, 512], BF16))
                for i in range(2)]

        b = {k: S[k].n for k in S}
        bspill = DS["dspill"].sems[0].n
        ms = {}

        def p2_sync(eng):
            for k2 in range(DC):
                if k2 >= 2:
                    eng.wait_ge(S["wcast"].h, b["wcast"] + k2 - 1)
                DS["dwo"].issue(k2, eng.dma_start(
                    out=wostage[k2 % 2][:],
                    in_=wo_d[k2 * 128:(k2 + 1) * 128, :]))
                ms[("wo", k2)] = DS["dwo"].mark(k2)
            DS["dbo"].issue(0, eng.dma_start(
                out=bo_rep[:], in_=bo_d[None, :].broadcast_to([128, D])))
            ms["bo"] = DS["dbo"].mark(0)
            DS["dlsum"].issue(0, eng.dma_start(out=lsum_sb[:], in_=lsum_hbm))
            ms["lsum"] = DS["dlsum"].mark(0)
            for c in range(NC4):
                eng.wait_ge(S["rs"].h, b["rs"] + 1)
                DS["drs"].wait(eng, (0, 16))
                if c > 0:
                    eng.wait_ge(S["div"].h, b["div"] + 32 * c)
                for p in range(NP):
                    row = p * 8 + c * 2
                    DS["dbc"].issue(0, eng.dma_start(
                        out=R128[p][0:64, :],
                        in_=rsum_hbm[row:row + 1, :].broadcast_to([64, 512])))
                    DS["dbc"].issue(0, eng.dma_start(
                        out=R128[p][64:128, :],
                        in_=rsum_hbm[row + 1:row + 2, :].broadcast_to([64, 512])))
                ms[("bc", c)] = DS["dbc"].mark(0)
                for m in range(4 * c, 4 * c + 4):
                    if m >= 2:
                        eng.wait_ge(S["resid"].h, b["resid"] + m - 1)
                    DS["dxmy"].issue(m, eng.dma_start(
                        out=xmystg[m % 2][:], in_=xmy_d[m * 128:(m + 1) * 128, :]))
                    ms[("xmy", m)] = DS["dxmy"].mark(m)
                    if m >= 2:
                        eng.wait_ge(S["div"].h, b["div"] + 8 * (m - 1))
                    for p in range(NP):
                        DS["dattn"].issue(m, eng.dma_start(
                            out=attnstg[m % 2][:, p * 128:(p + 1) * 128],
                            in_=attn_hbm[:, p * TQ + m * 128:
                                         p * TQ + (m + 1) * 128]))
                    ms[("attn", m)] = DS["dattn"].mark(m)

        def p2_tensor(eng):
            eng.wait_ge(S["wcast"].h, b["wcast"] + DC)
            for m in range(NM):
                eng.wait_ge(S["div"].h, b["div"] + 8 * (m + 1))
                if m >= 2:
                    eng.wait_ge(S["resid"].h, b["resid"] + m - 1)
                for half in range(2):
                    for p in range(NP):
                        mmi = eng.matmul(
                            psWo[m % 2][:, half * 512:(half + 1) * 512],
                            attndiv[m % 2][:, p * 128:(p + 1) * 128],
                            wo_bf[:, p * D + half * 512:
                                  p * D + half * 512 + 512],
                            start=(p == 0), stop=(p == NP - 1))
                S["womm"].inc(mmi)
                if m >= 2:
                    mm = m - 2
                    eng.wait_ge(S["norm"].h, b["norm"] + mm + 1)
                    for k2 in range(DC):
                        grp, sub = (8 * mm + k2) // 4, k2 % 4
                        if grp >= 2 and sub == 0:
                            eng.wait_ge(S["tpe"].h, b["tpe"] + grp - 1)
                        S["tp"].inc(eng.transpose(
                            psT2[grp % 2][:, sub * 128:(sub + 1) * 128],
                            ln2tm[mm % 2][:, k2 * 128:(k2 + 1) * 128], idm[:]))
            for mm in (NM - 2, NM - 1):
                eng.wait_ge(S["norm"].h, b["norm"] + mm + 1)
                for k2 in range(DC):
                    grp, sub = (8 * mm + k2) // 4, k2 % 4
                    if grp >= 2 and sub == 0:
                        eng.wait_ge(S["tpe"].h, b["tpe"] + grp - 1)
                    S["tp"].inc(eng.transpose(
                        psT2[grp % 2][:, sub * 128:(sub + 1) * 128],
                        ln2tm[mm % 2][:, k2 * 128:(k2 + 1) * 128], idm[:]))

        def ln2_dst(mm):
            return ln2_hbm[:].rearrange("p (k s) -> p k s", k=DC, s=TQ)[
                :, :, mm * 128:(mm + 1) * 128]

        def p2_scalar(eng):
            eng.wait_ge(S["rs"].h, b["rs"] + 1)
            DS["drs"].issue(0, eng.dma_start(out=rsum_hbm, in_=rsum_sb[:]))
            for m in range(NM + 3):
                if 0 <= m - 3 < NM:
                    mm = m - 3
                    eng.wait_ge(S["tpe"].h, b["tpe"] + 2 * (mm + 1))
                    DS["dlns"].issue(mm, eng.dma_start(
                        out=ln2_dst(mm),
                        in_=ln2stg[mm % 2][:].rearrange("p (k s) -> p k s",
                                                        k=DC, s=128)))
                    ms[("l2s", mm)] = DS["dlns"].mark(mm)
                if m < NM:
                    eng.wait_ge(S["resid"].h, b["resid"] + m + 1)
                    DS["dspill"].issue(m, eng.dma_start(
                        out=resid_hbm[m * 128:(m + 1) * 128, :],
                        in_=residb[m % 4][:]))
                    S["sq"].inc(eng.activation(
                        xsq2[:], residb[m % 4][:], AF.Square,
                        accum_out=st2[:, 2 + m % 2: 3 + m % 2]))
                if 0 <= m - 1 < NM:
                    mm = m - 1
                    eng.wait_ge(S["var"].h, b["var"] + mm + 1)
                    S["std"].inc(eng.activation(
                        st2[:, 10 + mm % 2: 11 + mm % 2],
                        st2[:, 8 + mm % 2: 9 + mm % 2], AF.Sqrt))
                if 0 <= m - 2 < NM:
                    mm = m - 2
                    eng.wait_ge(S["rstd"].h, b["rstd"] + mm + 1)
                    if mm >= 2:
                        eng.wait_ge(S["tp"].h, b["tp"] + 8 * (mm - 1))
                    S["norm"].inc(eng.activation(
                        ln2tm[mm % 2][:], residb[mm % 4][:], AF.Identity,
                        scale=st2[:, 12 + mm % 2: 13 + mm % 2],
                        bias=st2[:, 6 + mm % 2: 7 + mm % 2]))

        def p2_vector(eng):
            for k2 in range(DC):
                DS["dwo"].wait(eng, ms[("wo", k2)])
                S["wcast"].inc(eng.tensor_copy(
                    wo_bf[:, k2 * D:(k2 + 1) * D], wostage[k2 % 2][:]))
            DS["dlsum"].wait(eng, ms["lsum"])
            S["rs"].inc(eng.reciprocal(rsum_sb[:], lsum_sb[:]), 1)
            for m in range(NM + 2):
                if m < NM:
                    c = m // 4
                    DS["dattn"].wait(eng, ms[("attn", m)])
                    DS["dbc"].wait(eng, ms[("bc", c)])
                    for p in range(NP):
                        S["div"].inc(eng.tensor_mul(
                            attndiv[m % 2][:, p * 128:(p + 1) * 128],
                            attnstg[m % 2][:, p * 128:(p + 1) * 128],
                            R128[p][:, (m % 4) * 128:(m % 4 + 1) * 128]))
                    eng.wait_ge(S["womm"].h, b["womm"] + m + 1)
                    DS["dxmy"].wait(eng, ms[("xmy", m)])
                    DS["dbo"].wait(eng, ms["bo"])
                    if m >= 4:
                        eng.wait_ge(S["norm"].h, b["norm"] + m - 3)
                        DS["dspill"].wait(eng, ((m - 4) % 4,
                                                bspill + 16 * ((m - 4) // 4 + 1)))
                    rb, ps2m, xm = residb[m % 4], psWo[m % 2], xmystg[m % 2]
                    eng.tensor_add(rb[:, 0:512], ps2m[:, 0:512], xm[:, 0:512])
                    eng.tensor_add(rb[:, 512:1024], ps2m[:, 512:1024],
                                   xm[:, 512:1024])
                    eng.tensor_add(rb[:, 0:512], rb[:, 0:512],
                                   bo_rep[:, 0:512])
                    S["resid"].inc(eng.tensor_add(
                        rb[:, 512:1024], rb[:, 512:1024], bo_rep[:, 512:1024]))
                    mmB = m - 1 if 1 <= m <= NM else None
                    if mmB is not None:
                        eng.wait_ge(S["std"].h, b["std"] + mmB + 1)
                        if mmB >= 2:
                            eng.wait_ge(S["norm"].h, b["norm"] + mmB - 1)
                        eng.reciprocal(st2[:, 12 + mmB % 2:13 + mmB % 2],
                                       st2[:, 10 + mmB % 2:11 + mmB % 2])
                    eng.wait_ge(S["sq"].h, b["sq"] + m + 1)

                    def a_ex2():
                        eng.tensor_scalar_mul(st2[:, 8 + m % 2:9 + m % 2],
                                              st2[:, 2 + m % 2:3 + m % 2],
                                              1.0 / D)
                    if mmB is None:
                        a_ex2()
                    eng.reduce_sum(st2[:, m % 2:m % 2 + 1], rb[:], axis=AXX)
                    if mmB is None:
                        a_ex2()
                    else:
                        a_ex2()
                    eng.tensor_scalar_mul(st2[:, 4 + m % 2:5 + m % 2],
                                          st2[:, m % 2:m % 2 + 1], 1.0 / D)
                    if mmB is not None:
                        S["rstd"].inc(eng.scalar_tensor_tensor(
                            st2[:, 6 + mmB % 2:7 + mmB % 2],
                            st2[:, 4 + mmB % 2:5 + mmB % 2], -1.0,
                            st2[:, 12 + mmB % 2:13 + mmB % 2],
                            ALU.mult, ALU.mult))
                    else:
                        a_ex2()
                    eng.tensor_mul(st2[:, 14:15], st2[:, 4 + m % 2:5 + m % 2],
                                   st2[:, 4 + m % 2:5 + m % 2])
                    a_ex2()
                    eng.tensor_sub(st2[:, 8 + m % 2:9 + m % 2],
                                   st2[:, 8 + m % 2:9 + m % 2], st2[:, 14:15])
                    eng.tensor_mul(st2[:, 15:16], st2[:, 4 + m % 2:5 + m % 2],
                                   st2[:, 4 + m % 2:5 + m % 2])
                    S["var"].inc(eng.tensor_scalar_add(
                        st2[:, 8 + m % 2:9 + m % 2],
                        st2[:, 8 + m % 2:9 + m % 2], EPS))
                if m == NM:
                    mmB = m - 1
                    eng.wait_ge(S["std"].h, b["std"] + mmB + 1)
                    if mmB >= 2:
                        eng.wait_ge(S["norm"].h, b["norm"] + mmB - 1)
                    eng.reciprocal(st2[:, 12 + mmB % 2:13 + mmB % 2],
                                   st2[:, 10 + mmB % 2:11 + mmB % 2])
                    eng.reciprocal(st2[:, 12 + mmB % 2:13 + mmB % 2],
                                   st2[:, 10 + mmB % 2:11 + mmB % 2])
                    S["rstd"].inc(eng.scalar_tensor_tensor(
                        st2[:, 6 + mmB % 2:7 + mmB % 2],
                        st2[:, 4 + mmB % 2:5 + mmB % 2], -1.0,
                        st2[:, 12 + mmB % 2:13 + mmB % 2],
                        ALU.mult, ALU.mult))
                if 0 <= m - 2 < NM:
                    mm = m - 2
                    for half in range(2):
                        grp = 2 * mm + half
                        eng.wait_ge(S["tp"].h, b["tp"] + 4 * (grp + 1))
                        if mm >= 2 and half == 0:
                            DS["dlns"].wait(eng, ms[("l2s", mm - 2)])
                        S["tpe"].inc(eng.tensor_copy(
                            ln2stg[mm % 2][:, half * 512:(half + 1) * 512],
                            psT2[grp % 2][:]))

        if 2 not in phases:
            p2.close()
        else:
            phase("p2", {
                "sync": (p2_sync, None),
                "tensor": (p2_tensor, None),
                "scalar": (p2_scalar, [DS["dlns"], DS["dspill"], DS["drs"]]),
                "vector": (p2_vector, None),
            })
            p2.close()

        p3 = ExitStack()
        wfcT = p3.enter_context(nc.sbuf_tensor("sb_wfcT", [128, DC * 4096], BF16))
        wprT = p3.enter_context(nc.sbuf_tensor("sb_wprT", [128, 32 * D], BF16))
        hfm = p3.enter_context(nc.sbuf_tensor("sb_hfm", [128, 32 * 512], BF16))
        relutmp = [p3.enter_context(nc.sbuf_tensor(f"sb_rel{i}", [128, 512], BF16))
                   for i in range(2)]
        memb = p3.enter_context(nc.sbuf_tensor("sb_mem", [128, 1024], F32))
        spkb = p3.enter_context(nc.sbuf_tensor("sb_spk", [128, 1024], F32))
        resstg = [p3.enter_context(nc.sbuf_tensor(f"sb_rstg{i}", [128, D], F32))
                  for i in range(2)]
        outsb = [p3.enter_context(nc.sbuf_tensor(f"sb_outs{i}", [128, D], F32))
                 for i in range(2)]
        ln2win = [p3.enter_context(nc.sbuf_tensor(f"sb_l2w{i}", [128, DC * 512],
                                                  BF16)) for i in range(2)]
        membBF = memb[:].bitcast(BF16)
        spkbBF = spkb[:].bitcast(BF16)
        wcst_fc = [membBF[:, 0:1024], membBF[:, 1024:2048]]
        wcst_pr = [spkbBF[:, 0:1024], spkbBF[:, 1024:2048]]
        psT3 = [p3.enter_context(nc.psum_tensor(f"ps_t3{i}", [128, 512], BF16))
                for i in range(2)]
        psH = [p3.enter_context(nc.psum_tensor(f"ps_h{i}", [128, 512], F32))
               for i in range(2)]
        psY = [p3.enter_context(nc.psum_tensor(f"ps_y{i}", [128, D], F32))
               for i in range(2)]

        b = {k: S[k].n for k in S}
        ms = {}
        NG4 = 4

        def wpr_src(p):
            return wpr_d[(p // 4) * 128:(p // 4 + 1) * 128,
                         (p % 4) * 1024:(p % 4 + 1) * 1024]

        def wdst4(w, k2):
            if w < 32:
                return wfcT[:].rearrange("p (k s) -> p k s", k=DC, s=4096)[
                    :, k2:k2 + 4, w * 128:(w + 1) * 128]
            i, jc = (w - 32) // 4, (w - 32) % 4
            return wprT[:].rearrange("p (j s) -> p j s", j=32, s=D)[
                :, jc * 8 + k2: jc * 8 + k2 + 4, i * 128:(i + 1) * 128]

        def lnw_load(eng, g4):
            for k2 in range(DC):
                DS["dlnw"].issue(g4, eng.dma_start(
                    out=ln2win[g4 % 2][:, k2 * 512:(k2 + 1) * 512],
                    in_=ln2_hbm[:, k2 * TQ + g4 * 512:
                                k2 * TQ + (g4 + 1) * 512]))
            ms[("lnw", g4)] = DS["dlnw"].mark(g4)

        def p3_sync(eng):
            for g4 in range(2):
                lnw_load(eng, g4)
            for j in range(32):
                if j >= 2:
                    eng.wait_ge(S["wcast"].h, b["wcast"] + 2 * (j - 2) + 1)
                DS["dw3"].issue(j, eng.dma_start(
                    out=resstg[j % 2][:],
                    in_=wfc_d[j * 128:(j + 1) * 128, :]))
                ms[("w3", j)] = DS["dw3"].mark(j)
            eng.wait_ge(S["wcast"].h, b["wcast"] + 64)
            for g in range(NM):
                if g >= 2:
                    eng.wait_ge(S["snn"].h, b["snn"] + 2 * (g - 1))
                DS["dres"].issue(g, eng.dma_start(
                    out=resstg[g % 2][:],
                    in_=resid_hbm[g * 128:(g + 1) * 128, :]))
                ms[("res", g)] = DS["dres"].mark(g)
                if g >= 1:
                    eng.wait_ge(S["snn"].h, b["snn"] + 2 * g)
                    DS["dout"].issue(g - 1, eng.dma_start(
                        out=out_d[(g - 1) * 128:g * 128, :],
                        in_=outsb[(g - 1) % 2][:]))
                    ms[("out", g - 1)] = DS["dout"].mark(g - 1)
                if g % 4 == 2 and g // 4 + 2 < NG4:
                    gw = g // 4 + 2
                    eng.wait_ge(S["fcmm"].h, b["fcmm"] + 32 * (gw - 1))
                    lnw_load(eng, gw)
            eng.wait_ge(S["snn"].h, b["snn"] + 2 * NM)
            DS["dout"].issue(NM - 1, eng.dma_start(
                out=out_d[(NM - 1) * 128:NM * 128, :],
                in_=outsb[(NM - 1) % 2][:]))

        def trans8(eng, q):
            eng.wait_ge(S["wcast"].h, b["wcast"] + q + 1)
            src = wcst_fc[(q // 2) % 2] if q % 2 == 0 else wcst_pr[(q // 2) % 2]
            for k2 in range(DC):
                grp, sub = 2 * q + k2 // 4, k2 % 4
                if grp >= 2 and sub == 0:
                    eng.wait_ge(S["tpe"].h, b["tpe"] + grp - 1)
                S["tp"].inc(eng.transpose(
                    psT3[grp % 2][:, sub * 128:(sub + 1) * 128],
                    src[:, k2 * 128:(k2 + 1) * 128], idm[:]))

        def p3_tensor(eng):
            for g4 in range(NG4):
                for j in range(32):
                    jj = 32 * g4 + j
                    if g4 == 0:
                        trans8(eng, 2 * j)
                        eng.wait_ge(S["tpe"].h, b["tpe"] + 4 * j + 2)
                    if j == 0:
                        DS["dlnw"].wait(eng, ms[("lnw", g4)])
                    if jj >= 2:
                        eng.wait_ge(S["relu"].h, b["relu"] + jj - 1)
                    for k2 in range(DC):
                        mmi = eng.matmul(
                            psH[jj % 2][:],
                            wfcT[:, k2 * 4096 + j * 128: k2 * 4096 + (j + 1) * 128],
                            ln2win[g4 % 2][:, k2 * 512:(k2 + 1) * 512],
                            start=(k2 == 0), stop=(k2 == DC - 1))
                    S["fcmm"].inc(mmi)
                    if g4 == 0:
                        trans8(eng, 2 * j + 1)
                for sub in range(4):
                    g = 4 * g4 + sub
                    if g == 0:
                        eng.wait_ge(S["tpe"].h, b["tpe"] + 128)
                    if g >= 2:
                        eng.wait_ge(S["snn"].h, b["snn"] + 2 * (g - 1))
                    for j in range(32):
                        if sub == 0:
                            eng.wait_ge(S["h"].h, b["h"] + 32 * g4 + j + 1)
                        for half in range(2):
                            mmi = eng.matmul(
                                psY[sub % 2][:, half * 512:(half + 1) * 512],
                                hfm[:, j * 512 + sub * 128:
                                    j * 512 + sub * 128 + 128],
                                wprT[:, j * D + half * 512:
                                     j * D + half * 512 + 512],
                                start=(j == 0), stop=(j == 31))
                    S["prmm"].inc(mmi)

        def p3_scalar(eng):
            for p in range(2):
                DS["dwp"].issue(p, eng.dma_start(out=outsb[p % 2][:],
                                                 in_=wpr_src(p)))
                ms[("wp", p)] = DS["dwp"].mark(p)
            for g4 in range(NG4):
                for j in range(32):
                    jj = 32 * g4 + j
                    if g4 == 0 and j < 30:
                        p = j + 2
                        eng.wait_ge(S["wcast"].h, b["wcast"] + 2 * j + 2)
                        DS["dwp"].issue(p, eng.dma_start(out=outsb[p % 2][:],
                                                         in_=wpr_src(p)))
                        ms[("wp", p)] = DS["dwp"].mark(p)
                    def sq(s):
                        if s % 32 == 0 and s >= 32:
                            eng.wait_ge(S["prmm"].h, b["prmm"] + 4 * (s // 32))
                        S["h"].inc(eng.activation(
                            hfm[:, (s % 32) * 512:(s % 32 + 1) * 512],
                            relutmp[s % 2][:], AF.Square))

                    if j == 0 and jj > 0:
                        sq(jj - 1)
                    eng.wait_ge(S["fcmm"].h, b["fcmm"] + jj + 1)
                    S["relu"].inc(eng.activation(
                        relutmp[jj % 2][:], psH[jj % 2][:], AF.Relu))
                    if jj >= 1 and j != 0:
                        sq(jj - 1)
            sq(127)

        def p3_vector(eng):
            for j in range(33):
                if j >= 1:
                    for q in (2 * (j - 1), 2 * (j - 1) + 1):
                        w = q // 2 if q % 2 == 0 else 32 + q // 2
                        for halfg in range(2):
                            grp = 2 * q + halfg
                            eng.wait_ge(S["tp"].h, b["tp"] + 4 * (grp + 1))
                            S["tpe"].inc(eng.tensor_copy(
                                wdst4(w, halfg * 4),
                                psT3[grp % 2][:].rearrange("p (j s) -> p j s",
                                                           j=4, s=128)))
                if j < 32:
                    DS["dw3"].wait(eng, ms[("w3", j)])
                    if j >= 2:
                        eng.wait_ge(S["tp"].h, b["tp"] + 8 * (2 * (j - 2)) + 8)
                    S["wcast"].inc(eng.tensor_copy(wcst_fc[j % 2],
                                                   resstg[j % 2][:]))
                    DS["dwp"].wait(eng, ms[("wp", j)])
                    if j >= 2:
                        eng.wait_ge(S["tp"].h,
                                    b["tp"] + 8 * (2 * (j - 2) + 1) + 8)
                    S["wcast"].inc(eng.tensor_copy(wcst_pr[j % 2],
                                                   outsb[j % 2][:]))
            for g4 in range(NG4):
                for sub in range(4):
                    g = 4 * g4 + sub
                    eng.wait_ge(S["prmm"].h, b["prmm"] + g + 1)
                    DS["dres"].wait(eng, ms[("res", g)])
                    if g >= 2:
                        DS["dout"].wait(eng, ms[("out", g - 2)])
                    y2 = [psY[sub % 2][:, h * 512:(h + 1) * 512] for h in range(2)]
                    mem2 = [memb[:, 0:512], spkb[:, 0:512]]
                    spk2 = [memb[:, 512:1024], spkb[:, 512:1024]]
                    res2 = [resstg[g % 2][:, 0:512], resstg[g % 2][:, 512:1024]]
                    acc2 = [outsb[g % 2][:, 0:512], outsb[g % 2][:, 512:1024]]

                    def step(ops, inc=None):
                        for h in range(2):
                            inst = ops(h)
                            if inc is not None:
                                inc(inst)

                    step(lambda h: eng.tensor_copy(mem2[h], y2[h]))
                    step(lambda h: eng.tensor_single_scalar(spk2[h], mem2[h],
                                                            1.0, ALU.is_gt))
                    step(lambda h: eng.scalar_tensor_tensor(
                        acc2[h], spk2[h], 0.2, res2[h], ALU.mult, ALU.add))
                    for _s in range(4):
                        step(lambda h: eng.scalar_tensor_tensor(
                            mem2[h], mem2[h], 0.9, spk2[h],
                            ALU.mult, ALU.subtract))
                        step(lambda h: eng.tensor_add(mem2[h], mem2[h], y2[h]))
                        step(lambda h: eng.tensor_single_scalar(
                            spk2[h], mem2[h], 1.0, ALU.is_gt))
                        if _s < 3:
                            step(lambda h: eng.scalar_tensor_tensor(
                                acc2[h], spk2[h], 0.2, acc2[h],
                                ALU.mult, ALU.add))
                        else:
                            step(lambda h: eng.scalar_tensor_tensor(
                                acc2[h], spk2[h], 0.2, acc2[h],
                                ALU.mult, ALU.add),
                                inc=lambda inst: S["snn"].inc(inst))

        if 3 not in phases:
            p3.close()
        else:
            phase("p3", {
                "sync": (p3_sync, [DS["dw3"], DS["dres"], DS["dout"],
                         DS["dlnw"]]),
                "tensor": (p3_tensor, None),
                "scalar": (p3_scalar, [DS["dwp"]]),
                "vector": (p3_vector, None),
            })
            p3.close()

    return nc


_NC_CACHE = []


def kernel(L0_attn_x, L0_attn_Wk, L0_attn_Wq, L0_attn_Wo, L0_attn_bo,
           L0_ffn_W_fc, L0_ffn_W_proj):
    x = np.ascontiguousarray(np.asarray(L0_attn_x, dtype=np.float32))
    wk = np.ascontiguousarray(np.asarray(L0_attn_Wk, dtype=np.float32))
    wq = np.ascontiguousarray(np.asarray(L0_attn_Wq, dtype=np.float32))
    wo = np.ascontiguousarray(np.asarray(L0_attn_Wo, dtype=np.float32))
    bo = np.ascontiguousarray(np.asarray(L0_attn_bo, dtype=np.float32))
    wfc = np.ascontiguousarray(np.asarray(L0_ffn_W_fc, dtype=np.float32))
    wpr = np.ascontiguousarray(np.asarray(L0_ffn_W_proj, dtype=np.float32))
    ident = np.eye(128, dtype=ml_dtypes.bfloat16)

    if not _NC_CACHE:
        _NC_CACHE.append(build_nc())
    nc = _NC_CACHE[0]

    in_maps = []
    for c in range(8):
        bidx, qi = c // 4, c % 4
        in_maps.append({
            "xfull": x[bidx],
            "xmy": x[bidx, qi * TQ:(qi + 1) * TQ],
            "wq": wq, "wk": wk, "wo": wo, "bo": bo,
            "wfc": wfc, "wproj": wpr, "ident": ident,
        })
    res = run_bass_kernel_spmd(nc, in_maps, core_ids=list(range(8)))
    out = np.empty((2, TB, D), dtype=np.float32)
    for c in range(8):
        bidx, qi = c // 4, c % 4
        out[bidx, qi * TQ:(qi + 1) * TQ] = np.asarray(res.results[c]["out"])
    return out


# version 16
# speedup vs baseline: 1.0608x; 1.0114x over previous
import numpy as np
import ml_dtypes
from contextlib import ExitStack

import concourse.bass as bass
import concourse.mybir as mybir
from concourse.bass_utils import run_bass_kernel_spmd

F32 = mybir.dt.float32
BF16 = mybir.dt.bfloat16
AF = mybir.ActivationFunctionType
ALU = mybir.AluOpType
AXX = mybir.AxisListType.X

TB = 8192
TQ = 2048
D = 1024
DC = 8
NP = 8
NTK = 64
NC4 = 4
NM = 16
EPS = 1e-5


class Sem:
    def __init__(self, ctx, nc, name):
        self.h = ctx.enter_context(nc.semaphore(name))
        self.n = 0

    def inc(self, inst, k=1):
        inst.then_inc(self.h, k)
        self.n += k
        return inst


class DStream:

    def __init__(self, ctx, nc, name, nslots):
        self.sems = [Sem(ctx, nc, f"{name}{j}") for j in range(nslots)]
        self.ns = nslots

    def issue(self, slot, inst):
        self.sems[slot % self.ns].inc(inst, 16)

    def mark(self, slot):
        s = self.sems[slot % self.ns]
        return (slot % self.ns, s.n)

    def wait(self, eng, mark):
        j, v = mark
        eng.wait_ge(self.sems[j].h, v)

    def drain(self, eng):
        for s in self.sems:
            eng.wait_ge(s.h, s.n)


def build_nc(phases=(0, 1, 2, 3), p1_np=NP, p1_attn=True, p1_parts="kvq",
             dbg_scratch=False):
    nc = bass.Bass("TRN2", target_bir_lowering=False,
                   detect_race_conditions=False)

    xfull_d = nc.dram_tensor("xfull", [TB, D], F32, kind="ExternalInput").ap()
    xmy_d = nc.dram_tensor("xmy", [TQ, D], F32, kind="ExternalInput").ap()
    wq_d = nc.dram_tensor("wq", [D, D], F32, kind="ExternalInput").ap()
    wk_d = nc.dram_tensor("wk", [D, D], F32, kind="ExternalInput").ap()
    wo_d = nc.dram_tensor("wo", [D, D], F32, kind="ExternalInput").ap()
    bo_d = nc.dram_tensor("bo", [D], F32, kind="ExternalInput").ap()
    wfc_d = nc.dram_tensor("wfc", [4 * D, D], F32, kind="ExternalInput").ap()
    wpr_d = nc.dram_tensor("wproj", [D, 4 * D], F32, kind="ExternalInput").ap()
    idm_d = nc.dram_tensor("ident", [128, 128], BF16, kind="ExternalInput").ap()
    out_d = nc.dram_tensor("out", [TQ, D], F32, kind="ExternalOutput").ap()

    skind = "ExternalOutput" if dbg_scratch else "Internal"
    lnx_hbm = nc.dram_tensor("lnx_scr", [128, DC * TB], BF16, kind=skind).ap()
    lnmy_hbm = nc.dram_tensor("lnmy_scr", [128, DC * TQ], BF16, kind=skind).ap()
    attn_hbm = nc.dram_tensor("attn_scr", [128, NP * TQ], BF16, kind=skind).ap()
    lsum_hbm = nc.dram_tensor("lsum_scr", [64, 512], F32, kind=skind).ap()
    rsum_hbm = nc.dram_tensor("rsum_scr", [64, 512], F32, kind=skind).ap()
    resid_hbm = nc.dram_tensor("resid_scr", [TQ, D], F32, kind=skind).ap()
    ln2_hbm = nc.dram_tensor("ln2_scr", [128, DC * TQ], BF16, kind=skind).ap()

    ctx = ExitStack()
    with ctx:
        idm = ctx.enter_context(nc.sbuf_tensor("sb_idm", [128, 128], BF16))

        S = {}
        for name in ("ph sq var std rstd norm tp tpe wcast "
                     "kproj kevac vt vte qproj qevac sc exp avd ave div "
                     "womm resid rs fcmm relu h prmm snn vch").split():
            S[name] = Sem(ctx, nc, "sem_" + name)
        DS = {}
        for name, nsl in (("dx", 16), ("dlns", 2), ("dlnw", 2), ("dwl", 1),
                          ("dao", 1), ("dwo", 2), ("dbo", 1), ("dlsum", 1),
                          ("dxmy", 2), ("dattn", 2), ("dbc", 1), ("drs", 1),
                          ("dspill", 4), ("dw3", 2), ("dres", 2), ("dout", 2),
                          ("dwp", 2)):
            DS[name] = DStream(ctx, nc, name, nsl)

        phase_k = [0]

        def vi(inst):
            S["vch"].inc(inst)
            return inst

        def vw(eng):
            eng.wait_ge(S["vch"].h, S["vch"].n)

        def phase(block_name, bodies):
            k = phase_k[0]
            with nc.Block(block_name) as blk:
                for engname in ("sync", "tensor", "scalar", "vector", "gpsimd"):
                    body, drain = bodies.get(engname, (None, None))

                    def run(eng, body=body, drain=drain, k=k):
                        if k > 0:
                            eng.wait_ge(S["ph"].h, 5 * k)
                        if body is not None:
                            body(eng)
                        for d in (drain or []):
                            if isinstance(d, Sem):
                                eng.wait_ge(d.h, d.n)
                            else:
                                d.drain(eng)
                        S["ph"].inc(eng.nop())

                    getattr(blk, engname)(run)
            phase_k[0] += 1

        p0 = ExitStack()
        NXS = 28
        xstage = [p0.enter_context(nc.sbuf_tensor(f"sb_xs{i}", [128, D], F32))
                  for i in range(NXS)]
        lntm = [p0.enter_context(nc.sbuf_tensor(f"sb_lntm{i}", [128, D], BF16))
                for i in range(4)]
        xsq = p0.enter_context(nc.sbuf_tensor("sb_xsq", [128, D], F32))
        sxb = p0.enter_context(nc.sbuf_tensor("sb_sx", [128, 16], F32))
        sxqb = p0.enter_context(nc.sbuf_tensor("sb_sxq", [128, 16], F32))
        meanb = p0.enter_context(nc.sbuf_tensor("sb_mean", [128, 16], F32))
        ex2b = p0.enter_context(nc.sbuf_tensor("sb_ex2", [128, 16], F32))
        m2b = p0.enter_context(nc.sbuf_tensor("sb_m2", [128, 16], F32))
        varb = p0.enter_context(nc.sbuf_tensor("sb_var", [128, 16], F32))
        stdb = p0.enter_context(nc.sbuf_tensor("sb_std", [128, 16], F32))
        rstdb = p0.enter_context(nc.sbuf_tensor("sb_rstd", [128, 16], F32))
        negmb = p0.enter_context(nc.sbuf_tensor("sb_negm", [128, 16], F32))
        lnstg = [p0.enter_context(nc.sbuf_tensor(f"sb_lnst{i}", [128, 1024], BF16))
                 for i in range(2)]
        psT0 = [p0.enter_context(nc.psum_tensor(f"ps_t0{i}", [128, 512], BF16))
                for i in range(4)]

        NT = 80
        NG = 20

        def src_rows(i):
            return xfull_d[i * 128:(i + 1) * 128, :] if i < 64 else \
                xmy_d[(i - 64) * 128:(i - 63) * 128, :]

        def ln_dst(i):
            if i < 64:
                return lnx_hbm[:].rearrange("p (k s) -> p k s", k=DC, s=TB)[
                    :, :, i * 128:(i + 1) * 128]
            im = i - 64
            return lnmy_hbm[:].rearrange("p (k s) -> p k s", k=DC, s=TQ)[
                :, :, im * 128:(im + 1) * 128]

        def lncol(g, j):
            return (g % 4) * 4 + j

        def xmark(i):
            return (i % 16, 16 * (i // 16 + 1))

        def lnsmark(i):
            return (i % 2, 16 * (i // 2 + 1))

        def p0_sync(eng):
            DS["dwl"].issue(0, eng.dma_start(out=idm[:], in_=idm_d))
            for i in range(NT):
                if i % 4 == 3:
                    continue
                if i >= NXS:
                    eng.wait_ge(S["norm"].h, i - NXS + 1)
                DS["dx"].issue(i, eng.dma_start(out=xstage[i % NXS][:],
                                                in_=src_rows(i)))

        def p0_tensor(eng):
            DS["dwl"].wait(eng, (0, 16))
            for i in range(NT):
                eng.wait_ge(S["norm"].h, i + 1)
                for k2 in range(DC):
                    t = 8 * i + k2
                    grp, sub = t // 4, t % 4
                    if grp >= 4 and sub == 0:
                        eng.wait_ge(S["tpe"].h, grp - 3)
                    S["tp"].inc(eng.transpose(
                        psT0[grp % 4][:, sub * 128:(sub + 1) * 128],
                        lntm[i % 4][:, k2 * 128:(k2 + 1) * 128],
                        idm[:]))

        def p0_scalar(eng):
            DS["dx"].issue(3, eng.dma_start(out=xstage[3][:], in_=src_rows(3)))
            for g in range(NG + 5):
                i4 = 4 * g + 7
                if i4 < NT:
                    DS["dx"].issue(i4, eng.dma_start(
                        out=xstage[i4 % NXS][:], in_=src_rows(i4)))
                if 0 <= g - 5 < NG:
                    for j in range(4):
                        i = 4 * (g - 5) + j
                        eng.wait_ge(S["tpe"].h, 2 * (i + 1))
                        DS["dlns"].issue(i, eng.dma_start(
                            out=ln_dst(i),
                            in_=lnstg[i % 2][:].rearrange("p (k s) -> p k s",
                                                          k=DC, s=128)))
                if 0 <= g - 4 < NG:
                    gg = g - 4
                    eng.wait_ge(S["rstd"].h, gg + 1)
                    for j in range(4):
                        i = 4 * gg + j
                        if i >= 4:
                            eng.wait_ge(S["tp"].h, 8 * (i - 3))
                        c = lncol(gg, j)
                        S["norm"].inc(eng.activation(
                            lntm[i % 4][:], xstage[i % NXS][:], AF.Identity,
                            scale=rstdb[:, c:c + 1], bias=negmb[:, c:c + 1]))
                if 0 <= g - 2 < NG:
                    gg = g - 2
                    eng.wait_ge(S["var"].h, gg + 1)
                    if gg >= 4:
                        eng.wait_ge(S["rstd"].h, gg - 3)
                    c0 = (gg % 4) * 4
                    S["std"].inc(eng.activation(
                        stdb[:, c0:c0 + 4], varb[:, c0:c0 + 4], AF.Sqrt))
                if g < NG:
                    if g >= 4:
                        eng.wait_ge(S["var"].h, g - 3)
                    for j in range(4):
                        i = 4 * g + j
                        DS["dx"].wait(eng, xmark(i))
                        S["sq"].inc(eng.activation(
                            xsq[:], xstage[i % NXS][:], AF.Square,
                            accum_out=sxqb[:, lncol(g, j):lncol(g, j) + 1]))

        def p0_vector(eng):
            for g in range(NG + 5):
                if g < NG:
                    for j in range(4):
                        i = 4 * g + j
                        DS["dx"].wait(eng, xmark(i))
                        eng.reduce_sum(sxb[:, lncol(g, j):lncol(g, j) + 1],
                                       xstage[i % NXS][:], axis=AXX)
                ga = g - 1 if 0 <= g - 1 < NG else None
                gb = g - 3 if 0 <= g - 3 < NG else None
                if ga is not None:
                    eng.wait_ge(S["sq"].h, 4 * (ga + 1))
                    if ga >= 4:
                        eng.wait_ge(S["std"].h, ga - 3)
                if gb is not None:
                    eng.wait_ge(S["std"].h, gb + 1)
                    if gb >= 4:
                        eng.wait_ge(S["norm"].h, 4 * (gb - 3))
                cb = (gb % 4) * 4 if gb is not None else 0
                if ga is not None:
                    ca = (ga % 4) * 4

                    def a_ex2():
                        eng.tensor_scalar_mul(ex2b[:, ca:ca + 4],
                                              sxqb[:, ca:ca + 4], 1.0 / D)
                    eng.tensor_scalar_mul(meanb[:, ca:ca + 4],
                                          sxb[:, ca:ca + 4], 1.0 / D)
                    a_ex2()
                    eng.tensor_mul(m2b[:, ca:ca + 4], meanb[:, ca:ca + 4],
                                   meanb[:, ca:ca + 4])
                    if gb is not None:
                        eng.reciprocal(rstdb[:, cb:cb + 4], stdb[:, cb:cb + 4])
                    else:
                        a_ex2()
                    eng.tensor_sub(varb[:, ca:ca + 4], ex2b[:, ca:ca + 4],
                                   m2b[:, ca:ca + 4])
                    if gb is not None:
                        S["rstd"].inc(eng.scalar_tensor_tensor(
                            negmb[:, cb:cb + 4], meanb[:, cb:cb + 4], -1.0,
                            rstdb[:, cb:cb + 4], ALU.mult, ALU.mult))
                    else:
                        a_ex2()
                    S["var"].inc(eng.tensor_scalar_add(
                        varb[:, ca:ca + 4], varb[:, ca:ca + 4], EPS))
                elif gb is not None:
                    eng.reciprocal(rstdb[:, cb:cb + 4], stdb[:, cb:cb + 4])
                    eng.reciprocal(rstdb[:, cb:cb + 4], stdb[:, cb:cb + 4])
                    S["rstd"].inc(eng.scalar_tensor_tensor(
                        negmb[:, cb:cb + 4], meanb[:, cb:cb + 4], -1.0,
                        rstdb[:, cb:cb + 4], ALU.mult, ALU.mult))
                if 0 <= g - 5 < NG:
                    for j in range(4):
                        i = 4 * (g - 5) + j
                        for half in range(2):
                            grp = 2 * i + half
                            eng.wait_ge(S["tp"].h, 4 * (grp + 1))
                            if i >= 2 and half == 0:
                                DS["dlns"].wait(eng, lnsmark(i - 2))
                            S["tpe"].inc(eng.tensor_copy(
                                lnstg[i % 2][:, half * 512:(half + 1) * 512],
                                psT0[grp % 4][:]))


        if 0 not in phases:
            p0.close()
        else:
            phase("p0", {
                "sync": (p0_sync, [DS["dx"], DS["dwl"]]),
                "tensor": (p0_tensor, None),
                "scalar": (p0_scalar, [DS["dlns"]]),
                "vector": (p0_vector, None),
            })
            p0.close()

        p1 = ExitStack()
        wstage = p1.enter_context(nc.sbuf_tensor("sb_wst", [128, D], F32))
        wqk_bf = p1.enter_context(nc.sbuf_tensor("sb_wqk", [128, 2048], BF16))
        lnxwin = [p1.enter_context(nc.sbuf_tensor(f"sb_lnw{i}", [128, 4096], BF16))
                  for i in range(2)]
        kfm2 = [p1.enter_context(nc.sbuf_tensor(f"sb_kfm{i}", [128, TB], BF16))
                for i in range(2)]
        vtm2 = [p1.enter_context(nc.sbuf_tensor(f"sb_vtm{i}", [128, NTK * 130], BF16))
                for i in range(2)]
        qfm2 = [p1.enter_context(nc.sbuf_tensor(f"sb_qfm{i}", [128, TQ], BF16))
                for i in range(2)]
        pt = [p1.enter_context(nc.sbuf_tensor(f"sb_pt{i}", [128, 1024], BF16))
              for i in range(4)]
        h1st = [p1.enter_context(nc.sbuf_tensor(f"sb_h1st{i}", [64, 512], BF16))
                for i in range(2)]
        h2st = [p1.enter_context(nc.sbuf_tensor(f"sb_h2st{i}", [64, 512], BF16))
                for i in range(2)]
        lrst = p1.enter_context(nc.sbuf_tensor("sb_lrst", [128, 1024], F32))
        psS = [p1.enter_context(nc.psum_tensor(f"ps_s{i}", [128, 1024], F32))
               for i in range(2)]
        psAV = [p1.enter_context(nc.psum_tensor(f"ps_av{i}", [128, 512], F32))
                for i in range(2)]
        psP = [p1.enter_context(nc.psum_tensor(f"ps_p{i}", [128, 512], F32))
               for i in range(2)]
        psPb = [psP[i][:, 0:256].bitcast(BF16) for i in range(2)]

        vtm4 = [vtm2[i][:].rearrange("p (t j s) -> p t j s", t=NTK, j=2, s=65)
                for i in range(2)]
        b = {k: S[k].n for k in S}
        bao = DS["dao"].sems[0].n
        ms = {}

        def p1_sync(eng):
            for p in range(NP):
                if p > 0:
                    eng.wait_ge(S["wcast"].h, b["wcast"] + 2 * p)
                for k2 in range(DC):
                    DS["dwl"].issue(0, eng.dma_start(
                        out=wstage[:, k2 * 128:(k2 + 1) * 128],
                        in_=wk_d[k2 * 128:(k2 + 1) * 128, p * 128:(p + 1) * 128]))
                ms[("wk", p)] = DS["dwl"].mark(0)
                for t16 in range(16):
                    if t16 == 0 and p > 0:
                        eng.wait_ge(S["qproj"].h, b["qproj"] + 4 * p)
                    if 16 * p + t16 - 1 > 0:
                        eng.wait_ge(S["kproj"].h, b["kproj"] + 16 * p + t16 - 1)
                    for k2 in range(DC):
                        DS["dlnw"].issue(t16, eng.dma_start(
                            out=lnxwin[t16 % 2][:, k2 * 512:(k2 + 1) * 512],
                            in_=lnx_hbm[:, k2 * TB + t16 * 512:
                                        k2 * TB + (t16 + 1) * 512]))
                    ms[("kw", p, t16)] = DS["dlnw"].mark(t16)
                eng.wait_ge(S["wcast"].h, b["wcast"] + 2 * p + 1)
                for k2 in range(DC):
                    DS["dwl"].issue(0, eng.dma_start(
                        out=wstage[:, k2 * 128:(k2 + 1) * 128],
                        in_=wq_d[k2 * 128:(k2 + 1) * 128, p * 128:(p + 1) * 128]))
                ms[("wq", p)] = DS["dwl"].mark(0)
                for c in range(NC4):
                    if c == 0:
                        eng.wait_ge(S["kproj"].h, b["kproj"] + 16 * (p + 1))
                    if c >= 2:
                        eng.wait_ge(S["qproj"].h, b["qproj"] + 4 * p + c - 1)
                    for k2 in range(DC):
                        DS["dlnw"].issue(c, eng.dma_start(
                            out=lnxwin[c % 2][:, k2 * 512:(k2 + 1) * 512],
                            in_=lnmy_hbm[:, k2 * TQ + c * 512:
                                         k2 * TQ + (c + 1) * 512]))
                    ms[("qw", p, c)] = DS["dlnw"].mark(c)

        def prep_jobs(eng, q):
            def kjob(t16):
                def f():
                    if t16 == 0:
                        eng.wait_ge(S["wcast"].h, b["wcast"] + 2 * q + 1)
                    DS["dlnw"].wait(eng, ms[("kw", q, t16)])
                    if 16 * q + t16 - 1 > 0:
                        eng.wait_ge(S["kevac"].h, b["kevac"] + 16 * q + t16 - 1)
                    for k2 in range(DC):
                        mmi = eng.matmul(
                            psP[t16 % 2][:],
                            wqk_bf[:, k2 * 128:(k2 + 1) * 128],
                            lnxwin[t16 % 2][:, k2 * 512:(k2 + 1) * 512],
                            start=(k2 == 0), stop=(k2 == DC - 1))
                    S["kproj"].inc(mmi)
                return f

            def vjob(grp):
                def f():
                    eng.wait_ge(S["kevac"].h, b["kevac"] + 16 * (q + 1))
                    if grp >= 2:
                        eng.wait_ge(S["vte"].h, b["vte"] + 16 * q + grp - 1)
                    for sub in range(4):
                        t = 4 * grp + sub
                        S["vt"].inc(eng.transpose(
                            psPb[grp % 2][:, sub * 128:(sub + 1) * 128],
                            kfm2[q % 2][:, t * 128:(t + 1) * 128], idm[:]))
                return f

            def qjob(c):
                def f():
                    if c == 0:
                        eng.wait_ge(S["wcast"].h, b["wcast"] + 2 * q + 2)
                        eng.wait_ge(S["vte"].h, b["vte"] + 16 * (q + 1))
                    DS["dlnw"].wait(eng, ms[("qw", q, c)])
                    if c >= 2:
                        eng.wait_ge(S["qevac"].h, b["qevac"] + 4 * q + c - 1)
                    for k2 in range(DC):
                        mmi = eng.matmul(
                            psP[c % 2][:],
                            wqk_bf[:, 1024 + k2 * 128: 1024 + (k2 + 1) * 128],
                            lnxwin[c % 2][:, k2 * 512:(k2 + 1) * 512],
                            start=(k2 == 0), stop=(k2 == DC - 1))
                    S["qproj"].inc(mmi)
                return f

            jobs = {}
            for t16 in range(16):
                jobs[7 * t16 + 3] = kjob(t16)
            for grp in range(8):
                jobs[130 + 4 * grp] = vjob(grp)
            for grp in range(8, 16):
                jobs[196 + 4 * (grp - 8)] = vjob(grp)
            for c in range(NC4):
                jobs[229 + 7 * c] = qjob(c)
            return jobs

        def p1_tensor(eng):
            jobs0 = prep_jobs(eng, 0)
            for slot in sorted(jobs0):
                jobs0[slot]()
            for q in range(NP):
                jobs = prep_jobs(eng, q + 1) if q + 1 < NP else {}
                kfm = kfm2[q % 2]
                qfm = qfm2[q % 2]
                eng.wait_ge(S["vte"].h, b["vte"] + 16 * (q + 1))
                eng.wait_ge(S["qevac"].h, b["qevac"] + 4 * (q + 1))
                for c in range(NC4):
                    base = 256 * q + 64 * c

                    def scores(t, c=c, base=base):
                        if base + t - 1 > 0:
                            eng.wait_ge(S["exp"].h, b["exp"] + base + t - 1)
                        eng.matmul(psS[t % 2][:, 0:512],
                                   kfm[0:64, t * 128:(t + 1) * 128],
                                   qfm[0:64, c * 512:(c + 1) * 512],
                                   start=True, stop=True)
                        S["sc"].inc(eng.matmul(
                            psS[t % 2][:, 512:1024],
                            kfm[64:128, t * 128:(t + 1) * 128],
                            qfm[64:128, c * 512:(c + 1) * 512],
                            start=True, stop=True))

                    def av(t, c=c, base=base):
                        eng.wait_ge(S["exp"].h, b["exp"] + base + t + 1)
                        if t == 0 and (q > 0 or c > 0):
                            eng.wait_ge(S["ave"].h, b["ave"] + 4 * q + c)
                        eng.matmul(psAV[0][0:65, :],
                                   vtm4[q % 2][:, t, 0, 0:65],
                                   pt[t % 4][:, 0:512],
                                   start=(t == 0), stop=(t == NTK - 1))
                        S["avd"].inc(eng.matmul(
                            psAV[1][0:65, :],
                            vtm4[q % 2][:, t, 1, 0:65],
                            pt[t % 4][:, 512:1024],
                            start=(t == 0), stop=(t == NTK - 1)))

                    for t in range(NTK):
                        slot = 64 * c + t
                        scores(t)
                        if t >= 1:
                            av(t - 1)
                        if slot in jobs:
                            jobs.pop(slot)()
                    av(NTK - 1)
                for slot in sorted(jobs):
                    jobs.pop(slot)()

        def ao_store(eng, idx):
            col = idx * 512
            pp, cc = idx // 4, idx % 4
            row = pp * 8 + cc * 2
            DS["dao"].issue(0, eng.dma_start(
                out=attn_hbm[0:64, col:col + 512], in_=h1st[idx % 2][:]))
            DS["dao"].issue(0, eng.dma_start(
                out=attn_hbm[64:128, col:col + 512], in_=h2st[idx % 2][:]))
            DS["dao"].issue(0, eng.dma_start(
                out=lsum_hbm[row:row + 1, :], in_=lrst[64:65, 0:512]))
            DS["dao"].issue(0, eng.dma_start(
                out=lsum_hbm[row + 1:row + 2, :], in_=lrst[64:65, 512:1024]))

        def p1_scalar_full(eng):
            for p in range(NP):
                for c in range(NC4):
                    idx = 4 * p + c
                    for t in range(NTK):
                        base = 256 * p + 64 * c
                        eng.wait_ge(S["sc"].h, b["sc"] + base + t + 1)
                        if base + t - 3 > 0:
                            eng.wait_ge(S["avd"].h, b["avd"] + base + t - 3)
                        S["exp"].inc(eng.activation(
                            pt[t % 4][:], psS[t % 2][:], AF.Exp, scale=1.0 / 32.0))
                    eng.wait_ge(S["avd"].h, b["avd"] + 256 * p + 64 * (c + 1))
                    if idx >= 1:
                        DS["dao"].wait(eng, (0, bao + 64 * idx))
                    eng.activation(h1st[idx % 2][:], psAV[0][0:64, :],
                                   AF.Identity)
                    eng.activation(h2st[idx % 2][:], psAV[1][0:64, :],
                                   AF.Identity)
                    eng.activation(lrst[64:65, 0:512], psAV[0][64:65, :],
                                   AF.Identity)
                    S["ave"].inc(eng.activation(
                        lrst[64:65, 512:1024], psAV[1][64:65, :], AF.Identity))
                    ao_store(eng, idx)

        def p1_vector_prep(eng, q, section):
            if section == 0:
                DS["dwl"].wait(eng, ms[("wk", q)])
                if q > 0:
                    eng.wait_ge(S["kproj"].h, b["kproj"] + 16 * q)
                S["wcast"].inc(eng.tensor_copy(wqk_bf[:, 0:1024], wstage[:]))
            elif section in (1, 2):
                for j in range(8 * (section - 1), 8 * section):
                    eng.wait_ge(S["kproj"].h, b["kproj"] + 16 * q + j + 1)
                    S["kevac"].inc(eng.tensor_copy(
                        kfm2[q % 2][:, j * 512:(j + 1) * 512], psP[j % 2][:]))
            elif section in (3, 4):
                if section == 4:
                    DS["dwl"].wait(eng, ms[("wq", q)])
                    eng.wait_ge(S["qproj"].h, b["qproj"] + 4 * q)
                    S["wcast"].inc(eng.tensor_copy(wqk_bf[:, 1024:2048],
                                                   wstage[:]))
                for grp in range(8 * (section - 3), 8 * (section - 2)):
                    eng.wait_ge(S["vt"].h, b["vt"] + 64 * q + 4 * (grp + 1))
                    if grp == 0:
                        if q >= 1:
                            eng.wait_ge(S["avd"].h, b["avd"] + 256 * (q - 1))
                        eng.memset(vtm4[q % 2][:, :, :, 64:65], 1.0)
                    S["vte"].inc(eng.tensor_copy(
                        vtm4[q % 2][:, 4 * grp:4 * grp + 4, :, 0:64],
                        psPb[grp % 2][:].rearrange("p (t j s) -> p t j s",
                                                   t=4, j=2, s=64)))
            elif section == 5:
                for c in range(2):
                    eng.wait_ge(S["qproj"].h, b["qproj"] + 4 * q + c + 1)
                    S["qevac"].inc(eng.tensor_copy(
                        qfm2[q % 2][:, c * 512:(c + 1) * 512], psP[c % 2][:]))
            elif section == 6:
                for c in range(2, NC4):
                    eng.wait_ge(S["qproj"].h, b["qproj"] + 4 * q + c + 1)
                    S["qevac"].inc(eng.tensor_copy(
                        qfm2[q % 2][:, c * 512:(c + 1) * 512], psP[c % 2][:]))

        def p1_vector(eng):
            for sec in range(7):
                p1_vector_prep(eng, 0, sec)
            for p in range(NP):
                for c in range(NC4):
                    if p + 1 < NP:
                        if c == 0:
                            p1_vector_prep(eng, p + 1, 0)
                            p1_vector_prep(eng, p + 1, 1)
                        elif c == 1:
                            p1_vector_prep(eng, p + 1, 2)
                        elif c == 2:
                            p1_vector_prep(eng, p + 1, 3)
                        else:
                            p1_vector_prep(eng, p + 1, 4)
                            p1_vector_prep(eng, p + 1, 5)
                if p + 1 < NP:
                    p1_vector_prep(eng, p + 1, 6)

        if 1 not in phases:
            p1.close()
        else:
            phase("p1", {
                "sync": (p1_sync, None),
                "tensor": (p1_tensor, None),
                "scalar": (p1_scalar_full, [DS["dao"]]),
                "vector": (p1_vector, None),
            })
            p1.close()

        p2 = ExitStack()
        wostage = [p2.enter_context(nc.sbuf_tensor(f"sb_wos{i}", [128, D], F32))
                   for i in range(2)]
        wo_bf = p2.enter_context(nc.sbuf_tensor("sb_wobf", [128, DC * D], BF16))
        bo_rep = p2.enter_context(nc.sbuf_tensor("sb_borep", [128, D], F32))
        lsum_sb = p2.enter_context(nc.sbuf_tensor("sb_lsum", [64, 512], F32))
        rsum_sb = p2.enter_context(nc.sbuf_tensor("sb_rsum", [64, 512], F32))
        R128 = [p2.enter_context(nc.sbuf_tensor(f"sb_r128_{p}", [128, 512], F32))
                for p in range(NP)]
        attnstg = [p2.enter_context(nc.sbuf_tensor(f"sb_astg{i}", [128, D], BF16))
                   for i in range(2)]
        attndiv = [p2.enter_context(nc.sbuf_tensor(f"sb_adiv{i}", [128, D], BF16))
                   for i in range(4)]
        xmystg = [p2.enter_context(nc.sbuf_tensor(f"sb_xmys{i}", [128, D], F32))
                  for i in range(2)]
        residb = [p2.enter_context(nc.sbuf_tensor(f"sb_res{i}", [128, D], F32))
                  for i in range(6)]
        ln2tm = [p2.enter_context(nc.sbuf_tensor(f"sb_ln2tm{i}", [128, D], BF16))
                 for i in range(4)]
        xsq2 = p2.enter_context(nc.sbuf_tensor("sb_xsq2", [128, D], F32))
        st2 = p2.enter_context(nc.sbuf_tensor("sb_st2", [128, 32], F32))
        psWo = [p2.enter_context(nc.psum_tensor(f"ps_wo{i}", [128, D], F32))
                for i in range(3)]
        ln2stg = [p2.enter_context(nc.sbuf_tensor(f"sb_l2st{i}", [128, 1024],
                                                   BF16)) for i in range(2)]
        psT2 = [p2.enter_context(nc.psum_tensor(f"ps_t2{i}", [128, 512], BF16))
                for i in range(2)]

        b = {k: S[k].n for k in S}
        bspill = DS["dspill"].sems[0].n
        ms = {}

        def p2_sync(eng):
            for k2 in range(DC):
                if k2 >= 2:
                    eng.wait_ge(S["wcast"].h, b["wcast"] + k2 - 1)
                DS["dwo"].issue(k2, eng.dma_start(
                    out=wostage[k2 % 2][:],
                    in_=wo_d[k2 * 128:(k2 + 1) * 128, :]))
                ms[("wo", k2)] = DS["dwo"].mark(k2)
            DS["dbo"].issue(0, eng.dma_start(
                out=bo_rep[:], in_=bo_d[None, :].broadcast_to([128, D])))
            ms["bo"] = DS["dbo"].mark(0)
            DS["dlsum"].issue(0, eng.dma_start(out=lsum_sb[:], in_=lsum_hbm))
            ms["lsum"] = DS["dlsum"].mark(0)
            for c in range(NC4):
                eng.wait_ge(S["rs"].h, b["rs"] + 1)
                DS["drs"].wait(eng, (0, 16))
                if c > 0:
                    eng.wait_ge(S["div"].h, b["div"] + 32 * c)
                for p in range(NP):
                    row = p * 8 + c * 2
                    DS["dbc"].issue(0, eng.dma_start(
                        out=R128[p][0:64, :],
                        in_=rsum_hbm[row:row + 1, :].broadcast_to([64, 512])))
                    DS["dbc"].issue(0, eng.dma_start(
                        out=R128[p][64:128, :],
                        in_=rsum_hbm[row + 1:row + 2, :].broadcast_to([64, 512])))
                ms[("bc", c)] = DS["dbc"].mark(0)
                for m in range(4 * c, 4 * c + 4):
                    if m >= 2:
                        eng.wait_ge(S["resid"].h, b["resid"] + m - 1)
                    DS["dxmy"].issue(m, eng.dma_start(
                        out=xmystg[m % 2][:], in_=xmy_d[m * 128:(m + 1) * 128, :]))
                    ms[("xmy", m)] = DS["dxmy"].mark(m)
                    if m >= 2:
                        eng.wait_ge(S["div"].h, b["div"] + 8 * (m - 1))
                    for p in range(NP):
                        DS["dattn"].issue(m, eng.dma_start(
                            out=attnstg[m % 2][:, p * 128:(p + 1) * 128],
                            in_=attn_hbm[:, p * TQ + m * 128:
                                         p * TQ + (m + 1) * 128]))
                    ms[("attn", m)] = DS["dattn"].mark(m)

        def p2_tensor(eng):
            eng.wait_ge(S["wcast"].h, b["wcast"] + DC)
            for m in range(NM + 6):
                if m < NM:
                    eng.wait_ge(S["div"].h, b["div"] + 8 * (m + 1))
                    if m >= 3:
                        eng.wait_ge(S["resid"].h, b["resid"] + m - 2)
                    for half in range(2):
                        for p in range(NP):
                            mmi = eng.matmul(
                                psWo[m % 3][:, half * 512:(half + 1) * 512],
                                attndiv[m % 4][:, p * 128:(p + 1) * 128],
                                wo_bf[:, p * D + half * 512:
                                      p * D + half * 512 + 512],
                                start=(p == 0), stop=(p == NP - 1))
                    S["womm"].inc(mmi)
                if 0 <= m - 6 < NM:
                    mm = m - 6
                    eng.wait_ge(S["norm"].h, b["norm"] + mm + 1)
                    for k2 in range(DC):
                        grp, sub = (8 * mm + k2) // 4, k2 % 4
                        if grp >= 2 and sub == 0:
                            eng.wait_ge(S["tpe"].h, b["tpe"] + grp - 1)
                        S["tp"].inc(eng.transpose(
                            psT2[grp % 2][:, sub * 128:(sub + 1) * 128],
                            ln2tm[mm % 4][:, k2 * 128:(k2 + 1) * 128], idm[:]))

        def ln2_dst(mm):
            return ln2_hbm[:].rearrange("p (k s) -> p k s", k=DC, s=TQ)[
                :, :, mm * 128:(mm + 1) * 128]

        def p2_scalar(eng):
            eng.wait_ge(S["rs"].h, b["rs"] + 1)
            DS["drs"].issue(0, eng.dma_start(out=rsum_hbm, in_=rsum_sb[:]))
            for m in range(NM + 9):
                if 0 <= m - 9 < NM:
                    mm = m - 9
                    eng.wait_ge(S["tpe"].h, b["tpe"] + 2 * (mm + 1))
                    DS["dlns"].issue(mm, eng.dma_start(
                        out=ln2_dst(mm),
                        in_=ln2stg[mm % 2][:].rearrange("p (k s) -> p k s",
                                                        k=DC, s=128)))
                    ms[("l2s", mm)] = DS["dlns"].mark(mm)
                if 0 <= m - 6 < NM:
                    mm = m - 6
                    eng.wait_ge(S["rstd"].h, b["rstd"] + mm + 1)
                    if mm >= 4:
                        eng.wait_ge(S["tp"].h, b["tp"] + 8 * (mm - 3))
                    S["norm"].inc(eng.activation(
                        ln2tm[mm % 4][:], residb[mm % 6][:], AF.Identity,
                        scale=st2[:, 24 + mm % 4: 25 + mm % 4],
                        bias=st2[:, 12 + mm % 4: 13 + mm % 4]))
                if 0 <= m - 4 < NM:
                    mm = m - 4
                    eng.wait_ge(S["var"].h, b["var"] + mm + 1)
                    if mm >= 4:
                        eng.wait_ge(S["rstd"].h, b["rstd"] + mm - 3)
                    S["std"].inc(eng.activation(
                        st2[:, 20 + mm % 4: 21 + mm % 4],
                        st2[:, 16 + mm % 4: 17 + mm % 4], AF.Sqrt))
                if 0 <= m - 2 < NM:
                    mm = m - 2
                    eng.wait_ge(S["resid"].h, b["resid"] + mm + 1)
                    if mm >= 4:
                        eng.wait_ge(S["var"].h, b["var"] + mm - 3)
                    DS["dspill"].issue(mm, eng.dma_start(
                        out=resid_hbm[mm * 128:(mm + 1) * 128, :],
                        in_=residb[mm % 6][:]))
                    S["sq"].inc(eng.activation(
                        xsq2[:], residb[mm % 6][:], AF.Square,
                        accum_out=st2[:, 4 + mm % 4: 5 + mm % 4]))

        def p2_vector(eng):
            for k2 in range(DC):
                DS["dwo"].wait(eng, ms[("wo", k2)])
                S["wcast"].inc(eng.tensor_copy(
                    wo_bf[:, k2 * D:(k2 + 1) * D], wostage[k2 % 2][:]))
            DS["dlsum"].wait(eng, ms["lsum"])
            S["rs"].inc(eng.reciprocal(rsum_sb[:], lsum_sb[:]), 1)
            for m in range(NM + 7):
                if m < NM:
                    c = m // 4
                    DS["dattn"].wait(eng, ms[("attn", m)])
                    DS["dbc"].wait(eng, ms[("bc", c)])
                    if m >= 4:
                        eng.wait_ge(S["womm"].h, b["womm"] + m - 3)
                    for p in range(NP):
                        S["div"].inc(eng.tensor_mul(
                            attndiv[m % 4][:, p * 128:(p + 1) * 128],
                            attnstg[m % 2][:, p * 128:(p + 1) * 128],
                            R128[p][:, (m % 4) * 128:(m % 4 + 1) * 128]))
                if 0 <= m - 1 < NM:
                    mR = m - 1
                    eng.wait_ge(S["womm"].h, b["womm"] + mR + 1)
                    DS["dxmy"].wait(eng, ms[("xmy", mR)])
                    DS["dbo"].wait(eng, ms["bo"])
                    if mR >= 6:
                        eng.wait_ge(S["norm"].h, b["norm"] + mR - 5)
                        DS["dspill"].wait(eng, ((mR - 6) % 4,
                                                bspill + 16 * ((mR - 6) // 4
                                                               + 1)))
                    rb, ps2m, xm = residb[mR % 6], psWo[mR % 3], xmystg[mR % 2]
                    eng.tensor_add(rb[:, 0:512], ps2m[:, 0:512], xm[:, 0:512])
                    eng.tensor_add(rb[:, 512:1024], ps2m[:, 512:1024],
                                   xm[:, 512:1024])
                    eng.tensor_add(rb[:, 0:512], rb[:, 0:512],
                                   bo_rep[:, 0:512])
                    S["resid"].inc(eng.tensor_add(
                        rb[:, 512:1024], rb[:, 512:1024], bo_rep[:, 512:1024]))
                mA = m - 3 if 0 <= m - 3 < NM else None
                mB = m - 5 if 0 <= m - 5 < NM else None
                if mB is not None:
                    eng.wait_ge(S["std"].h, b["std"] + mB + 1)
                    if mB >= 4:
                        eng.wait_ge(S["norm"].h, b["norm"] + mB - 3)
                    eng.reciprocal(st2[:, 24 + mB % 4:25 + mB % 4],
                                   st2[:, 20 + mB % 4:21 + mB % 4])
                if mA is not None:
                    eng.wait_ge(S["sq"].h, b["sq"] + mA + 1)
                    if mA >= 4:
                        eng.wait_ge(S["std"].h, b["std"] + mA - 3)
                    cA = mA % 4

                    def a_ex2():
                        eng.tensor_scalar_mul(st2[:, 16 + cA:17 + cA],
                                              st2[:, 4 + cA:5 + cA], 1.0 / D)
                    a_ex2()
                    eng.reduce_sum(st2[:, cA:cA + 1], residb[mA % 6][:],
                                   axis=AXX)
                    a_ex2()
                    eng.tensor_scalar_mul(st2[:, 8 + cA:9 + cA],
                                          st2[:, cA:cA + 1], 1.0 / D)
                    if mB is not None:
                        S["rstd"].inc(eng.scalar_tensor_tensor(
                            st2[:, 12 + mB % 4:13 + mB % 4],
                            st2[:, 8 + mB % 4:9 + mB % 4], -1.0,
                            st2[:, 24 + mB % 4:25 + mB % 4],
                            ALU.mult, ALU.mult))
                    else:
                        a_ex2()
                    eng.tensor_mul(st2[:, 28:29], st2[:, 8 + cA:9 + cA],
                                   st2[:, 8 + cA:9 + cA])
                    a_ex2()
                    eng.tensor_sub(st2[:, 16 + cA:17 + cA],
                                   st2[:, 16 + cA:17 + cA], st2[:, 28:29])
                    eng.tensor_mul(st2[:, 29:30], st2[:, 8 + cA:9 + cA],
                                   st2[:, 8 + cA:9 + cA])
                    S["var"].inc(eng.tensor_scalar_add(
                        st2[:, 16 + cA:17 + cA], st2[:, 16 + cA:17 + cA],
                        EPS))
                elif mB is not None:
                    eng.reciprocal(st2[:, 24 + mB % 4:25 + mB % 4],
                                   st2[:, 20 + mB % 4:21 + mB % 4])
                    S["rstd"].inc(eng.scalar_tensor_tensor(
                        st2[:, 12 + mB % 4:13 + mB % 4],
                        st2[:, 8 + mB % 4:9 + mB % 4], -1.0,
                        st2[:, 24 + mB % 4:25 + mB % 4],
                        ALU.mult, ALU.mult))
                if 0 <= m - 7 < NM:
                    mm = m - 7
                    for half in range(2):
                        grp = 2 * mm + half
                        eng.wait_ge(S["tp"].h, b["tp"] + 4 * (grp + 1))
                        if mm >= 2 and half == 0:
                            DS["dlns"].wait(eng, ms[("l2s", mm - 2)])
                        S["tpe"].inc(eng.tensor_copy(
                            ln2stg[mm % 2][:, half * 512:(half + 1) * 512],
                            psT2[grp % 2][:]))

        if 2 not in phases:
            p2.close()
        else:
            phase("p2", {
                "sync": (p2_sync, None),
                "tensor": (p2_tensor, None),
                "scalar": (p2_scalar, [DS["dlns"], DS["dspill"], DS["drs"]]),
                "vector": (p2_vector, None),
            })
            p2.close()

        p3 = ExitStack()
        wfcT = p3.enter_context(nc.sbuf_tensor("sb_wfcT", [128, DC * 4096], BF16))
        wprT = p3.enter_context(nc.sbuf_tensor("sb_wprT", [128, 32 * D], BF16))
        hfm = p3.enter_context(nc.sbuf_tensor("sb_hfm", [128, 32 * 512], BF16))
        relutmp = [p3.enter_context(nc.sbuf_tensor(f"sb_rel{i}", [128, 512], BF16))
                   for i in range(2)]
        memb = p3.enter_context(nc.sbuf_tensor("sb_mem", [128, 1024], F32))
        spkb = p3.enter_context(nc.sbuf_tensor("sb_spk", [128, 1024], F32))
        resstg = [p3.enter_context(nc.sbuf_tensor(f"sb_rstg{i}", [128, D], F32))
                  for i in range(2)]
        outsb = [p3.enter_context(nc.sbuf_tensor(f"sb_outs{i}", [128, D], F32))
                 for i in range(2)]
        ln2win = [p3.enter_context(nc.sbuf_tensor(f"sb_l2w{i}", [128, DC * 512],
                                                  BF16)) for i in range(2)]
        membBF = memb[:].bitcast(BF16)
        spkbBF = spkb[:].bitcast(BF16)
        wcst_fc = [membBF[:, 0:1024], membBF[:, 1024:2048]]
        wcst_pr = [spkbBF[:, 0:1024], spkbBF[:, 1024:2048]]
        psT3 = [p3.enter_context(nc.psum_tensor(f"ps_t3{i}", [128, 512], BF16))
                for i in range(2)]
        psH = [p3.enter_context(nc.psum_tensor(f"ps_h{i}", [128, 512], F32))
               for i in range(2)]
        psY = [p3.enter_context(nc.psum_tensor(f"ps_y{i}", [128, D], F32))
               for i in range(2)]

        b = {k: S[k].n for k in S}
        ms = {}
        NG4 = 4

        def wpr_src(p):
            return wpr_d[(p // 4) * 128:(p // 4 + 1) * 128,
                         (p % 4) * 1024:(p % 4 + 1) * 1024]

        def wdst4(w, k2):
            if w < 32:
                return wfcT[:].rearrange("p (k s) -> p k s", k=DC, s=4096)[
                    :, k2:k2 + 4, w * 128:(w + 1) * 128]
            i, jc = (w - 32) // 4, (w - 32) % 4
            return wprT[:].rearrange("p (j s) -> p j s", j=32, s=D)[
                :, jc * 8 + k2: jc * 8 + k2 + 4, i * 128:(i + 1) * 128]

        def lnw_load(eng, g4):
            for k2 in range(DC):
                DS["dlnw"].issue(g4, eng.dma_start(
                    out=ln2win[g4 % 2][:, k2 * 512:(k2 + 1) * 512],
                    in_=ln2_hbm[:, k2 * TQ + g4 * 512:
                                k2 * TQ + (g4 + 1) * 512]))
            ms[("lnw", g4)] = DS["dlnw"].mark(g4)

        def p3_sync(eng):
            for g4 in range(2):
                lnw_load(eng, g4)
            for j in range(32):
                if j >= 2:
                    eng.wait_ge(S["wcast"].h, b["wcast"] + 2 * (j - 2) + 1)
                DS["dw3"].issue(j, eng.dma_start(
                    out=resstg[j % 2][:],
                    in_=wfc_d[j * 128:(j + 1) * 128, :]))
                ms[("w3", j)] = DS["dw3"].mark(j)
            eng.wait_ge(S["wcast"].h, b["wcast"] + 64)
            for g in range(NM):
                if g >= 2:
                    eng.wait_ge(S["snn"].h, b["snn"] + 2 * (g - 1))
                DS["dres"].issue(g, eng.dma_start(
                    out=resstg[g % 2][:],
                    in_=resid_hbm[g * 128:(g + 1) * 128, :]))
                ms[("res", g)] = DS["dres"].mark(g)
                if g >= 1:
                    eng.wait_ge(S["snn"].h, b["snn"] + 2 * g)
                    DS["dout"].issue(g - 1, eng.dma_start(
                        out=out_d[(g - 1) * 128:g * 128, :],
                        in_=outsb[(g - 1) % 2][:]))
                    ms[("out", g - 1)] = DS["dout"].mark(g - 1)
                if g % 4 == 2 and g // 4 + 2 < NG4:
                    gw = g // 4 + 2
                    eng.wait_ge(S["fcmm"].h, b["fcmm"] + 32 * (gw - 1))
                    lnw_load(eng, gw)
            eng.wait_ge(S["snn"].h, b["snn"] + 2 * NM)
            DS["dout"].issue(NM - 1, eng.dma_start(
                out=out_d[(NM - 1) * 128:NM * 128, :],
                in_=outsb[(NM - 1) % 2][:]))

        def trans8(eng, q):
            eng.wait_ge(S["wcast"].h, b["wcast"] + q + 1)
            src = wcst_fc[(q // 2) % 2] if q % 2 == 0 else wcst_pr[(q // 2) % 2]
            for k2 in range(DC):
                grp, sub = 2 * q + k2 // 4, k2 % 4
                if grp >= 2 and sub == 0:
                    eng.wait_ge(S["tpe"].h, b["tpe"] + grp - 1)
                S["tp"].inc(eng.transpose(
                    psT3[grp % 2][:, sub * 128:(sub + 1) * 128],
                    src[:, k2 * 128:(k2 + 1) * 128], idm[:]))

        def p3_tensor(eng):
            for g4 in range(NG4):
                for j in range(32):
                    jj = 32 * g4 + j
                    if g4 == 0:
                        trans8(eng, 2 * j)
                        eng.wait_ge(S["tpe"].h, b["tpe"] + 4 * j + 2)
                    if j == 0:
                        DS["dlnw"].wait(eng, ms[("lnw", g4)])
                    if jj >= 2:
                        eng.wait_ge(S["relu"].h, b["relu"] + jj - 1)
                    for k2 in range(DC):
                        mmi = eng.matmul(
                            psH[jj % 2][:],
                            wfcT[:, k2 * 4096 + j * 128: k2 * 4096 + (j + 1) * 128],
                            ln2win[g4 % 2][:, k2 * 512:(k2 + 1) * 512],
                            start=(k2 == 0), stop=(k2 == DC - 1))
                    S["fcmm"].inc(mmi)
                    if g4 == 0:
                        trans8(eng, 2 * j + 1)
                for sub in range(4):
                    g = 4 * g4 + sub
                    if g == 0:
                        eng.wait_ge(S["tpe"].h, b["tpe"] + 128)
                    if g >= 2:
                        eng.wait_ge(S["snn"].h, b["snn"] + 2 * (g - 1))
                    for j in range(32):
                        if sub == 0:
                            eng.wait_ge(S["h"].h, b["h"] + 32 * g4 + j + 1)
                        for half in range(2):
                            mmi = eng.matmul(
                                psY[sub % 2][:, half * 512:(half + 1) * 512],
                                hfm[:, j * 512 + sub * 128:
                                    j * 512 + sub * 128 + 128],
                                wprT[:, j * D + half * 512:
                                     j * D + half * 512 + 512],
                                start=(j == 0), stop=(j == 31))
                    S["prmm"].inc(mmi)

        def p3_scalar(eng):
            for p in range(2):
                DS["dwp"].issue(p, eng.dma_start(out=outsb[p % 2][:],
                                                 in_=wpr_src(p)))
                ms[("wp", p)] = DS["dwp"].mark(p)
            for g4 in range(NG4):
                for j in range(32):
                    jj = 32 * g4 + j
                    if g4 == 0 and j < 30:
                        p = j + 2
                        eng.wait_ge(S["wcast"].h, b["wcast"] + 2 * j + 2)
                        DS["dwp"].issue(p, eng.dma_start(out=outsb[p % 2][:],
                                                         in_=wpr_src(p)))
                        ms[("wp", p)] = DS["dwp"].mark(p)
                    def sq(s):
                        if s % 32 == 0 and s >= 32:
                            eng.wait_ge(S["prmm"].h, b["prmm"] + 4 * (s // 32))
                        S["h"].inc(eng.activation(
                            hfm[:, (s % 32) * 512:(s % 32 + 1) * 512],
                            relutmp[s % 2][:], AF.Square))

                    if j == 0 and jj > 0:
                        sq(jj - 1)
                    eng.wait_ge(S["fcmm"].h, b["fcmm"] + jj + 1)
                    S["relu"].inc(eng.activation(
                        relutmp[jj % 2][:], psH[jj % 2][:], AF.Relu))
                    if jj >= 1 and j != 0:
                        sq(jj - 1)
            sq(127)

        def p3_vector(eng):
            for j in range(33):
                if j >= 1:
                    for q in (2 * (j - 1), 2 * (j - 1) + 1):
                        w = q // 2 if q % 2 == 0 else 32 + q // 2
                        for halfg in range(2):
                            grp = 2 * q + halfg
                            eng.wait_ge(S["tp"].h, b["tp"] + 4 * (grp + 1))
                            S["tpe"].inc(eng.tensor_copy(
                                wdst4(w, halfg * 4),
                                psT3[grp % 2][:].rearrange("p (j s) -> p j s",
                                                           j=4, s=128)))
                if j < 32:
                    DS["dw3"].wait(eng, ms[("w3", j)])
                    if j >= 2:
                        eng.wait_ge(S["tp"].h, b["tp"] + 8 * (2 * (j - 2)) + 8)
                    S["wcast"].inc(eng.tensor_copy(wcst_fc[j % 2],
                                                   resstg[j % 2][:]))
                    DS["dwp"].wait(eng, ms[("wp", j)])
                    if j >= 2:
                        eng.wait_ge(S["tp"].h,
                                    b["tp"] + 8 * (2 * (j - 2) + 1) + 8)
                    S["wcast"].inc(eng.tensor_copy(wcst_pr[j % 2],
                                                   outsb[j % 2][:]))
            for g4 in range(NG4):
                for sub in range(4):
                    g = 4 * g4 + sub
                    eng.wait_ge(S["prmm"].h, b["prmm"] + g + 1)
                    DS["dres"].wait(eng, ms[("res", g)])
                    if g >= 2:
                        DS["dout"].wait(eng, ms[("out", g - 2)])
                    y2 = [psY[sub % 2][:, h * 512:(h + 1) * 512] for h in range(2)]
                    mem2 = [memb[:, 0:512], spkb[:, 0:512]]
                    spk2 = [memb[:, 512:1024], spkb[:, 512:1024]]
                    res2 = [resstg[g % 2][:, 0:512], resstg[g % 2][:, 512:1024]]
                    acc2 = [outsb[g % 2][:, 0:512], outsb[g % 2][:, 512:1024]]

                    def step(ops, inc=None):
                        for h in range(2):
                            inst = ops(h)
                            if inc is not None:
                                inc(inst)

                    step(lambda h: eng.tensor_copy(mem2[h], y2[h]))
                    step(lambda h: eng.tensor_single_scalar(spk2[h], mem2[h],
                                                            1.0, ALU.is_gt))
                    step(lambda h: eng.scalar_tensor_tensor(
                        acc2[h], spk2[h], 0.2, res2[h], ALU.mult, ALU.add))
                    for _s in range(4):
                        step(lambda h: eng.scalar_tensor_tensor(
                            mem2[h], mem2[h], 0.9, spk2[h],
                            ALU.mult, ALU.subtract))
                        step(lambda h: eng.tensor_add(mem2[h], mem2[h], y2[h]))
                        step(lambda h: eng.tensor_single_scalar(
                            spk2[h], mem2[h], 1.0, ALU.is_gt))
                        if _s < 3:
                            step(lambda h: eng.scalar_tensor_tensor(
                                acc2[h], spk2[h], 0.2, acc2[h],
                                ALU.mult, ALU.add))
                        else:
                            step(lambda h: eng.scalar_tensor_tensor(
                                acc2[h], spk2[h], 0.2, acc2[h],
                                ALU.mult, ALU.add),
                                inc=lambda inst: S["snn"].inc(inst))

        if 3 not in phases:
            p3.close()
        else:
            phase("p3", {
                "sync": (p3_sync, [DS["dw3"], DS["dres"], DS["dout"],
                         DS["dlnw"]]),
                "tensor": (p3_tensor, None),
                "scalar": (p3_scalar, [DS["dwp"]]),
                "vector": (p3_vector, None),
            })
            p3.close()

    return nc


_NC_CACHE = []


def kernel(L0_attn_x, L0_attn_Wk, L0_attn_Wq, L0_attn_Wo, L0_attn_bo,
           L0_ffn_W_fc, L0_ffn_W_proj):
    x = np.ascontiguousarray(np.asarray(L0_attn_x, dtype=np.float32))
    wk = np.ascontiguousarray(np.asarray(L0_attn_Wk, dtype=np.float32))
    wq = np.ascontiguousarray(np.asarray(L0_attn_Wq, dtype=np.float32))
    wo = np.ascontiguousarray(np.asarray(L0_attn_Wo, dtype=np.float32))
    bo = np.ascontiguousarray(np.asarray(L0_attn_bo, dtype=np.float32))
    wfc = np.ascontiguousarray(np.asarray(L0_ffn_W_fc, dtype=np.float32))
    wpr = np.ascontiguousarray(np.asarray(L0_ffn_W_proj, dtype=np.float32))
    ident = np.eye(128, dtype=ml_dtypes.bfloat16)

    if not _NC_CACHE:
        _NC_CACHE.append(build_nc())
    nc = _NC_CACHE[0]

    in_maps = []
    for c in range(8):
        bidx, qi = c // 4, c % 4
        in_maps.append({
            "xfull": x[bidx],
            "xmy": x[bidx, qi * TQ:(qi + 1) * TQ],
            "wq": wq, "wk": wk, "wo": wo, "bo": bo,
            "wfc": wfc, "wproj": wpr, "ident": ident,
        })
    res = run_bass_kernel_spmd(nc, in_maps, core_ids=list(range(8)))
    out = np.empty((2, TB, D), dtype=np.float32)
    for c in range(8):
        bidx, qi = c // 4, c % 4
        out[bidx, qi * TQ:(qi + 1) * TQ] = np.asarray(res.results[c]["out"])
    return out
